# version 10
# baseline (speedup 1.0000x reference)
import os
import numpy as np
import ml_dtypes
from contextlib import ExitStack

import concourse.bass as bass
import concourse.mybir as mybir
from concourse.bass_utils import run_bass_kernel_spmd

F32 = mybir.dt.float32
BF16 = mybir.dt.bfloat16
AF = mybir.ActivationFunctionType
OP = mybir.AluOpType
AX = mybir.AxisListType

T = 2048
TB = 512
NTB = T // TB
D = 1024
NDC = D // 128
NFC = 32
E0, E1, R = 8, 16, 16
NIC = 4
NSH = 8
EPS = 1e-5

ENGINES = ("sync", "scalar", "vector", "tensor", "gpsimd")


class _Op:
    __slots__ = ("idx", "eng", "emit", "reads", "writes", "dma", "stream",
                 "name", "token", "needed")

    def __init__(self, idx, eng, emit, reads, writes, dma, stream, name):
        self.idx, self.eng, self.emit = idx, eng, emit
        self.reads, self.writes = tuple(reads), tuple(writes)
        self.dma, self.stream, self.name = dma, stream, name
        self.token, self.needed = None, False


class Prog:
    def __init__(self):
        self.ops = []
        self._lw = {}
        self._rd = {}
        self.deps = {}

    def add(self, eng, emit, reads=(), writes=(), dma=False, stream=None,
            name=""):
        assert eng in ENGINES, eng
        if dma and stream is None:
            stream = "q_" + eng
        op = _Op(len(self.ops), eng, emit, reads, writes, dma, stream, name)
        deps = set()
        for r in op.reads:
            w = self._lw.get(r)
            if w is not None:
                deps.add(w)
        for w in op.writes:
            lw = self._lw.get(w)
            if lw is not None:
                deps.add(lw)
            for rd in self._rd.get(w, ()):
                deps.add(rd)
        deps.discard(op)
        self.deps[op.idx] = deps
        for r in op.reads:
            self._rd.setdefault(r, []).append(op)
        for w in op.writes:
            self._lw[w] = op
            self._rd[w] = []
        self.ops.append(op)
        return op

    def emit_into(self, nc, block, ctx):
        if os.environ.get("KSCHED", "1") != "0":
            est = _schedule(self)
        for op in self.ops:
            for d in self.deps[op.idx]:
                d.needed = True
        counts, sem_names = {}, []
        for op in self.ops:
            sem = ("s_" + op.stream) if op.dma else ("e_" + op.eng)
            if op.dma or op.needed:
                counts[sem] = counts.get(sem, 0) + (16 if op.dma else 1)
                op.token = (sem, counts[sem])
                if sem not in sem_names:
                    sem_names.append(sem)
        sems = {nm: ctx.enter_context(nc.semaphore(nm)) for nm in sem_names}
        prog = self

        def make_body(eng_name):
            def body(eng):
                waited = {}
                for op in prog.ops:
                    if op.eng != eng_name:
                        continue
                    byname = {}
                    for d in prog.deps[op.idx]:
                        if d.token is None:
                            continue
                        sem, val = d.token
                        if val > byname.get(sem, 0):
                            byname[sem] = val
                    for sem in sorted(byname):
                        val = byname[sem]
                        if waited.get(sem, 0) >= val:
                            continue
                        eng.wait_ge(sems[sem], val)
                        waited[sem] = val
                    inst = op.emit(eng)
                    if op.token is not None:
                        assert inst is not None, f"op {op.name} returned None"
                        inst.then_inc(sems[op.token[0]], 16 if op.dma else 1)
                return None
            return body

        for eng_name in ENGINES:
            if any(o.eng == eng_name for o in self.ops):
                getattr(block, eng_name)(make_body(eng_name))
        return sems



_COST_TABLE = (
    ("d1_", 520), ("d3_", 520), ("z_", 520), ("t1_", 1400), ("t3_", 1400),
    ("si_", 1200), ("sA_", 2500), ("ap_", 750), ("mx_", 2600),
    ("c1m", 1800), ("c3m", 1800), ("c1e", 800), ("c3e", 800),
    ("sw", 7200), ("b2_", 1800), ("xa1m", 1800), ("xa3m", 1800),
    ("xa1e", 600), ("xa3e", 600), ("ze_", 600), ("zr_", 2000),
    ("xa1r", 1800), ("xa3r", 1800),
    ("L0ug", 1800), ("L0ue", 800), ("L0sg", 1200), ("L0bt", 1800),
    ("L0dt", 1800), ("L0y", 300), ("L0z", 750), ("L0o", 3600),
    ("L0scan", 2500), ("L0dec", 750), ("L0vv", 750), ("L0od", 750),
    ("L1xb", 1800), ("L1ve", 750), ("L1sc", 2500), ("L1o", 1800),
    ("L1oe", 750), ("sh1", 1800), ("sh3", 1800), ("shs", 750), ("sha", 750),
    ("eh1", 1800), ("eh3", 1800), ("ehs", 750), ("ehq", 750), ("eha", 600),
    ("ew2", 4200), ("ewe", 500),
    ("ld_wg", 3500), ("ld_wu", 3500), ("ld_wd", 3500), ("ld_w1", 2600),
    ("ld_w3", 2200), ("ld_w2", 3000), ("ld_", 3000),
    ("cast_", 1150),
    ("_ldx", 3200), ("_sp", 3200), ("_xnb", 3200), ("ce", 3200),
    ("f0xa", 2600), ("f0oe", 750), ("f0os", 2600), ("L0xa", 2600),
    ("L0os", 2600), ("L1xa", 2600), ("L1vs", 2600), ("L1vl", 5500),
    ("ews", 2600), ("_sumx", 1100), ("_ssq", 300), ("_sq", 550),
    ("_bcr", 900), ("_bcm", 900), ("_xnm", 750), ("_xns", 750),
    ("_gl_", 900), ("_gt_", 350), ("_gu_", 250), ("_gv_", 1500),
)


def _op_cost(op):
    n = op.name
    for pat, c in _COST_TABLE:
        if pat in n:
            return c
    return 1600 if op.dma else 400


class _Sched:
    pass


def _schedule(prog):
    ops = prog.ops
    n = len(ops)
    deps = prog.deps
    last_stream = {}
    for op in ops:
        if op.dma:
            prev = last_stream.get(op.stream)
            if prev is not None:
                deps[op.idx].add(prev)
            last_stream[op.stream] = op
    cost = [_op_cost(o) for o in ops]
    succ = [[] for _ in range(n)]
    ndep = [0] * n
    for i in range(n):
        dd = deps[i]
        ndep[i] = len(dd)
        for d in dd:
            succ[d.idx].append(i)
    prio = [0.0] * n
    for i in range(n - 1, -1, -1):
        m = 0.0
        for j in succ[i]:
            if prio[j] > m:
                m = prio[j]
        prio[i] = m + cost[i]
    import heapq
    done_t = [0.0] * n
    eng_free = {e: 0.0 for e in ENGINES}
    ready = []
    for i in range(n):
        if ndep[i] == 0:
            heapq.heappush(ready, (-prio[i], i))
    order = []
    while ready:
        _, i = heapq.heappop(ready)
        op = ops[i]
        est = eng_free[op.eng]
        for d in deps[i]:
            if done_t[d.idx] > est:
                est = done_t[d.idx]
        if op.dma:
            eng_free[op.eng] = est + 350
            done_t[i] = est + cost[i]
        else:
            eng_free[op.eng] = est + cost[i]
            done_t[i] = est + cost[i]
        order.append(i)
        for j in succ[i]:
            ndep[j] -= 1
            if ndep[j] == 0:
                heapq.heappush(ready, (-prio[j], j))
    assert len(order) == n, (len(order), n)
    perm = [ops[i] for i in order]
    fin = [o for o in perm if o.name == "final"]
    perm = [o for o in perm if o.name != "final"] + fin
    prog.ops = perm
    return max(eng_free.values())


def K(pool, lo_b, hi_b):
    return tuple(f"{pool}:{i}" for i in range(lo_b // 512, (hi_b + 511) // 512))


def _prep_weights(ins):
    f = np.float32
    c = np.ascontiguousarray
    w = {}
    w["winT"] = c(ins["L0_attn_W_in"].T)
    w["wstT"] = c(ins["L0_attn_W_state"].T.reshape(8, 128, 128)
                  .transpose(1, 0, 2).reshape(128, 1024))
    w["wdtT"] = c(ins["L0_attn_W_dt"].T.reshape(8, 128, 128)
                  .transpose(1, 0, 2).reshape(128, 1024))
    w["wcT"] = c(ins["L0_attn_W_C"])
    w["wcT"] = c(ins["L0_attn_W_C"].T)
    w["woT"] = c(ins["L0_attn_W_out"].T)
    wg = ins["L0_ffn_w_gate"]
    w["wg_s"] = c(wg.reshape(32, 128, 8, 128).transpose(0, 3, 2, 1)
                  .reshape(32, 128, 1024))
    wu = ins["L0_ffn_w_up"]
    w["wu_s"] = c(wu.reshape(32, 128, 8, 128).transpose(0, 3, 2, 1)
                  .reshape(32, 128, 1024))
    wd = ins["L0_ffn_w_down"]
    w["wd_s"] = c(wd.reshape(8, 128, 32, 128).transpose(2, 3, 0, 1)
                  .reshape(32, 128, 1024))
    w["a1T"] = c(ins["L0_ffn_A1"].transpose(2, 0, 1).reshape(1024, 128))
    w["a3T"] = c(ins["L0_ffn_A3"].transpose(2, 0, 1).reshape(1024, 128))
    for src, dstn in (("L0_ffn_B1", "b1s"), ("L0_ffn_B3", "b3s")):
        Br = ins[src].reshape(8, 8, 4, 128, 16)
        z5 = np.zeros((8, 4, 32, 8, 128), f)
        z5[:, :, :16] = Br.transpose(0, 2, 4, 1, 3)
        w[dstn] = c(z5.reshape(8, 128, 1024))
    w["a2s"] = c(ins["L0_ffn_A2"].reshape(8, 16, 32, 128)
                 .transpose(0, 3, 2, 1).reshape(8, 128, 512))
    B2r = ins["L0_ffn_B2"].reshape(2, 4, 8, 128, 16)
    z5 = np.zeros((2, 4, 32, 8, 128), f)
    z5[:, :, :16] = B2r.transpose(0, 1, 4, 2, 3)
    w["b2s"] = c(z5.reshape(2, 128, 1024))
    w["gwT0"] = c(ins["L0_ffn_gate_w"].T.reshape(8, 128, 8)
                  .transpose(1, 0, 2).reshape(128, 64))
    w["bw"] = c(ins["L1_attn_b_w"].T)
    w["cw"] = c(ins["L1_attn_c_w"].T)
    w["ws1"] = c(ins["L1_ffn_ws1"])
    w["ws3"] = c(ins["L1_ffn_ws3"])
    w["ws2"] = c(ins["L1_ffn_ws2"])
    w["w1"] = c(ins["L1_ffn_w1"])
    w["w2"] = c(ins["L1_ffn_w2"])
    w["w3"] = c(ins["L1_ffn_w3"])
    w["gwT1"] = c(ins["L1_ffn_gate_w"].T.reshape(8, 128, 16)
                  .transpose(1, 0, 2).reshape(128, 128))
    w["bdt"] = c(ins["L0_attn_b_dt"].reshape(128, 1))
    w["alog"] = c(ins["L0_attn_A_log"].reshape(128, 1))
    w["dcol"] = c(ins["L0_attn_D"].reshape(8, 128).T)
    w["llcol"] = c(ins["L1_attn_log_lambda"].reshape(8, 128).T)
    w["bbcol"] = c(ins["L1_attn_b_b"].reshape(8, 128).T)
    w["cbcol"] = c(ins["L1_attn_c_b"].reshape(8, 128).T)
    w["ident"] = np.eye(128, dtype=f)
    cv = np.zeros((128, 4), f)
    cv[:, 0] = 1e-5
    cv[:, 1] = -0.5
    cv[:, 2] = 1.0 / 1024.0
    cv[:, 3] = 2.0
    w["cvec"] = cv
    w["onescol"] = np.ones((128, 1), f)
    w["onesrow"] = np.ones((1, 128), f)
    return {k: np.ascontiguousarray(v, dtype=f) for k, v in w.items()}


_W_SHAPES = {
    "winT": [1024, 2048], "wstT": [128, 1024], "wdtT": [128, 1024],
    "wcT": [128, 1024], "woT": [1024, 1024],
    "wg_s": [32, 128, 1024], "wu_s": [32, 128, 1024], "wd_s": [32, 128, 1024],
    "a1T": [1024, 128], "a3T": [1024, 128],
    "b1s": [8, 128, 1024], "b3s": [8, 128, 1024],
    "a2s": [8, 128, 512], "b2s": [2, 128, 1024],
    "gwT0": [128, 64], "gwT1": [128, 128],
    "bw": [1024, 1024], "cw": [1024, 1024],
    "ws1": [1024, 1024], "ws3": [1024, 1024], "ws2": [1024, 1024],
    "w1": [16, 1024, 512], "w2": [16, 512, 1024], "w3": [16, 1024, 512],
    "bdt": [128, 1], "alog": [128, 1], "dcol": [128, 8], "llcol": [128, 8],
    "bbcol": [128, 8], "cbcol": [128, 8],
    "ident": [128, 128], "onescol": [128, 1], "onesrow": [1, 128],
    "cvec": [128, 4],
}


def build(dbg_internal=False):
    KSKIP = os.environ.get("KSKIP", "")
    nc = bass.Bass("TRN2", target_bir_lowering=False)
    dI = {"xT": nc.dram_tensor("xT", [1024, 2048], F32, kind="ExternalInput").ap()}
    for nm, sh in _W_SHAPES.items():
        dI[nm] = nc.dram_tensor(nm, sh, F32, kind="ExternalInput").ap()
    outT = nc.dram_tensor("outT", [1024, 2048], F32, kind="ExternalOutput").ap()
    scratch_kind = "ExternalOutput" if dbg_internal else "Internal"
    xa0 = nc.dram_tensor("xa0", [1024, 2048], F32, kind=scratch_kind).ap()
    vdram = nc.dram_tensor("vdram", [1024, 2048], F32, kind=scratch_kind).ap()
    c0d = nc.dram_tensor("c0d", [8, 2048], F32, kind=scratch_kind).ap()
    c1d = nc.dram_tensor("c1d", [16, 2048], F32, kind=scratch_kind).ap()
    xnbd = nc.dram_tensor("xnbd", [1024, 2048], BF16, kind="Internal").ap()

    with ExitStack() as ctx:
        def sb(nm, sh, dt=F32):
            return ctx.enter_context(nc.sbuf_tensor(nm, sh, dt))

        def pt(nm, sh, dt=F32):
            return ctx.enter_context(nc.psum_tensor(nm, sh, dt))

        PA = sb("PA", [128, 8192])
        PB = sb("PB", [128, 8192])
        PC = sb("PC", [128, 8192])
        SG = sb("SG", [128, 8192])
        XN = sb("XN", [128, 4096], BF16)
        stg = [sb(f"stg{i}", [128, 1024]) for i in range(2)]
        scr1 = sb("scr1", [128, 1024], BF16)
        scr2 = sb("scr2", [128, 1024])
        t3s = [sb(f"t3s{i}", [128, 1024], BF16) for i in range(2)]
        sss = [sb(f"sss{i}", [128, 1024], BF16) for i in range(2)]
        a1Tb = sb("a1Tb", [128, 1024], BF16)
        a3Tb = sb("a3Tb", [128, 1024], BF16)
        a2sb = [sb(f"a2sb{i}", [128, 512], BF16) for i in range(2)]
        b1sb = [sb(f"b1sb{i}", [128, 1024], BF16) for i in range(2)]
        b3sb = [sb(f"b3sb{i}", [128, 1024], BF16) for i in range(2)]
        b2pb = [sb(f"b2pb{i}", [128, 1024], BF16) for i in range(2)]
        wstb = sb("wstb", [128, 1024], BF16)
        wdtb = sb("wdtb", [128, 1024], BF16)
        wcb = sb("wcb", [128, 1024], BF16)
        gw0b = sb("gw0b", [128, 64], BF16)
        gw1b = sb("gw1b", [128, 128], BF16)
        xa1s = [sb(f"xa1s{i}", [128, 512], BF16) for i in range(2)]
        xa3s = [sb(f"xa3s{i}", [128, 512], BF16) for i in range(2)]
        actp = [sb(f"actp{i}", [128, 1024], BF16) for i in range(2)]
        zpk = sb("zpk", [128, 1024], BF16)
        ceb = [sb(f"ceb{i}", [128, 512]) for i in range(2)]
        xat = [sb(f"xat{i}", [128, 512]) for i in range(2)]
        osb = [sb(f"osb{i}", [128, 512]) for i in range(2)]
        gsc = sb("gsc", [128, 320])
        vcon = sb("vcon", [128, 64])
        identb = sb("identb_s", [128, 128])
        onescol = sb("onescol_s", [128, 1])
        onesrow = sb("onesrow_s", [1, 128])
        PP = [pt(f"PP{i}", [128, 1024]) for i in range(4)]

        def PH(i):
            return PP[i // 2][:, (i % 2) * 512:(i % 2) * 512 + 512]

        def pk(i):
            return (f"ps:{i}",)

        PAb = PA[:, :].bitcast(BF16)
        PBb = PB[:, :].bitcast(BF16)
        PCb = PC[:, :].bitcast(BF16)
        SGb = SG[:, :].bitcast(BF16)

        PAk = lambda lo, hi: K("PA", lo, hi)
        PBk = lambda lo, hi: K("PB", lo, hi)
        PCk = lambda lo, hi: K("PC", lo, hi)
        SGk = lambda lo, hi: K("SG", lo, hi)
        XNk = lambda lo, hi: K("XN", lo, hi)

        p = Prog()

        def dma(eng, dst_ap, src_ap, reads, writes, stream, name, accum=False):
            if accum:
                p.add(eng, (lambda d, s: lambda e: e.dma_start(
                    out=d, in_=s, accum_op=OP.add))(dst_ap, src_ap),
                    reads=reads, writes=writes, dma=True, stream=stream,
                    name=name)
            else:
                p.add(eng, (lambda d, s: lambda e: e.dma_start(
                    out=d, in_=s))(dst_ap, src_ap),
                    reads=reads, writes=writes, dma=True, stream=stream,
                    name=name)

        def V(emit, reads, writes, name):
            p.add("vector", emit, reads=reads, writes=writes, name=name)

        def S(emit, reads, writes, name):
            p.add("scalar", emit, reads=reads, writes=writes, name=name)

        def G(emit, reads, writes, name):
            p.add("gpsimd", emit, reads=reads, writes=writes, name=name)

        def PE(emit, reads, writes, name):
            p.add("tensor", emit, reads=reads, writes=writes, name=name)

        stg_ctr = [0]

        def wload(src_ap, dst_ap, dst_keys, name, ncols, scale_col=None,
                  scale_keys=(), ceng="S"):
            si = stg_ctr[0] % 2
            stg_ctr[0] += 1
            stk = K(f"stg{si}", 0, ncols * 4)
            dma("sync", stg[si][:, 0:ncols], src_ap, (), stk, f"wst{si}",
                "ld_" + name)
            if scale_col is None:
                if ceng == "V":
                    V((lambda d, s, n: lambda e: e.tensor_copy(d, s[:, 0:n]))(
                        dst_ap, stg[si], ncols), stk, dst_keys, "cast_" + name)
                else:
                    S((lambda d, s, n: lambda e: e.copy(d, s[:, 0:n]))(
                        dst_ap, stg[si], ncols), stk, dst_keys, "cast_" + name)
            else:
                V((lambda d, s, n, sc: lambda e: e.tensor_scalar_mul(
                    d, s[:, 0:n], sc))(dst_ap, stg[si], ncols, scale_col),
                  stk + scale_keys, dst_keys, "cast_" + name)

        for nm, dst, keyn in (("ident", identb[:, :], "identb"),
                              ("onescol", onescol[:, :], "onescol"),
                              ("onesrow", onesrow[0:1, :], "onesrow")):
            dma("sync", dst, dI[nm], (), (keyn,), "cst_" + nm, "ld_" + nm)
        VC = {"bdt": 0, "alog": 1, "negrate": 2, "dcol": 3, "llcol": 11,
              "lam": 19, "oml": 27, "omlbb": 35, "bbcol": 43, "cbcol": 51,
              "cvec": 59}
        for nm, w_ in (("bdt", 1), ("alog", 1), ("dcol", 8), ("llcol", 8),
                       ("bbcol", 8), ("cbcol", 8), ("cvec", 4)):
            c0 = VC[nm]
            dma("sync", vcon[:, c0:c0 + w_], dI[nm], (), ("vcon",), "cst_" + nm,
                "ld_" + nm)

        def vc(nm, w_=1):
            c0 = VC[nm]
            return vcon[:, c0:c0 + w_]

        S(lambda e: e.activation(vc("negrate"), vc("alog"), AF.Exp),
          ("vcon",), ("vcon",), "nr1")
        S(lambda e: e.activation(vc("negrate"), vc("negrate"), AF.Ln, bias=1.0),
          ("vcon",), ("vcon",), "nr2")
        V(lambda e: e.tensor_scalar(vc("negrate"), vc("negrate"), 1e-4, -1.0,
                                    OP.add, OP.mult),
          ("vcon",), ("vcon",), "nr3")
        S(lambda e: e.activation(vc("lam", 8), vc("llcol", 8), AF.Exp),
          ("vcon",), ("vcon",), "lam1")
        S(lambda e: e.activation(vc("lam", 8), vc("lam", 8), AF.Sigmoid),
          ("vcon",), ("vcon",), "lam2")
        V(lambda e: e.tensor_scalar(vc("oml", 8), vc("lam", 8), -1.0, 1.0,
                                    OP.mult, OP.add),
          ("vcon",), ("vcon",), "lam3")
        V(lambda e: e.tensor_mul(vc("omlbb", 8), vc("oml", 8), vc("bbcol", 8)),
          ("vcon",), ("vcon",), "lam4")
        G(lambda e: e.memset(zpk[:, :], 0.0), (),
          tuple(f"zpk{i}" for i in range(8)) + ("zpk_pad",), "zpk0")
        wload(dI["gwT0"], gw0b[:, :], ("gw0b",), "gw0", 64)
        wload(dI["gwT1"], gw1b[:, :], ("gw1b",), "gw1", 128)

        def layernorm(tb, src_dram, lname):
            xtk = []
            for dc in range(NDC):
                kk = PBk(dc * 2048, (dc + 1) * 2048)
                xtk.append(kk)
                dma("sync", PB[:, dc * 512:(dc + 1) * 512],
                    src_dram[dc * 128:(dc + 1) * 128, tb * TB:(tb + 1) * TB],
                    (), kk, f"qx{dc}", f"{lname}_ldx{dc}_{tb}")

            def mm_sumx(e):
                inst = None
                for dc in range(NDC):
                    inst = e.matmul(PH(0)[0:1, :], onescol[:, 0:1],
                                    PB[:, dc * 512:(dc + 1) * 512],
                                    start=(dc == 0), stop=(dc == NDC - 1))
                return inst
            PE(mm_sumx, tuple(x for k in xtk for x in k) + ("onescol",),
               pk(0), f"{lname}_sumx_{tb}")
            sqk = [PBk(16384 + i * 2048, 16384 + (i + 1) * 2048) for i in range(2)]
            for dc in range(NDC):
                si = dc % 2
                S((lambda dc, si: lambda e: e.activation(
                    PB[:, 4096 + si * 512:4096 + (si + 1) * 512],
                    PB[:, dc * 512:(dc + 1) * 512], AF.Square))(dc, si),
                  xtk[dc], sqk[si], f"{lname}_sq{dc}_{tb}")
                PE((lambda dc, si: lambda e: e.matmul(
                    PH(1)[0:1, :], onescol[:, 0:1],
                    PB[:, 4096 + si * 512:4096 + (si + 1) * 512],
                    start=(dc == 0), stop=(dc == NDC - 1)))(dc, si),
                   sqk[si] + ("onescol",), pk(1), f"{lname}_ssq{dc}_{tb}")
            mrow = PB[0:1, 5120:5632]
            m2row = PB[0:1, 5632:6144]
            vrow = PB[0:1, 6144:6656]
            mk = PBk(20480, 22528)
            m2k = PBk(22528, 24576)
            vk = PBk(24576, 26624)
            S(lambda e: e.activation(mrow, PH(0)[0:1, :], AF.Copy,
                                     scale=vc("cvec", 4)[0:1, 2:3]),
              pk(0) + ("vcon",), mk, f"{lname}_m_{tb}")
            V(lambda e: e.tensor_mul(m2row, mrow, mrow), mk, m2k,
              f"{lname}_m2_{tb}")
            V(lambda e: e.scalar_tensor_tensor(vrow, PH(1)[0:1, :], 1.0 / D,
                                               m2row, OP.mult, OP.subtract),
              pk(1) + m2k, vk, f"{lname}_var_{tb}")
            S(lambda e: e.activation(vrow, vrow, AF.Ln,
                                     bias=vc("cvec", 4)[0:1, 0:1]),
              vk + ("vcon",), vk, f"{lname}_lnv_{tb}")
            S(lambda e: e.activation(vrow, vrow, AF.Exp,
                                     scale=vc("cvec", 4)[0:1, 1:2]),
              vk + ("vcon",), vk, f"{lname}_rs_{tb}")
            V(lambda e: e.tensor_mul(m2row, mrow, vrow), mk + vk, m2k,
              f"{lname}_mrs_{tb}")
            PE(lambda e: e.matmul(PH(2)[:, :], onesrow[0:1, :], vrow,
                                  start=True, stop=True),
               vk + ("onesrow",), pk(2), f"{lname}_bcr_{tb}")
            PE(lambda e: e.matmul(PH(3)[:, :], onesrow[0:1, :], m2row,
                                  start=True, stop=True),
               m2k + ("onesrow",), pk(3), f"{lname}_bcm_{tb}")
            for dc in range(NDC):
                si = dc % 2
                V((lambda dc, si: lambda e: e.tensor_mul(
                    PB[:, 4096 + si * 512:4096 + (si + 1) * 512],
                    PB[:, dc * 512:(dc + 1) * 512], PH(2)[:, :]))(dc, si),
                  xtk[dc] + pk(2), sqk[si], f"{lname}_xnm{dc}_{tb}")
                V((lambda dc, si: lambda e: e.tensor_sub(
                    XN[:, dc * 512:(dc + 1) * 512],
                    PB[:, 4096 + si * 512:4096 + (si + 1) * 512],
                    PH(3)[:, :]))(dc, si),
                  sqk[si] + pk(3), XNk(dc * 1024, (dc + 1) * 1024),
                  f"{lname}_xns{dc}_{tb}")
                dma("gpsimd",
                    xnbd[dc * 128:(dc + 1) * 128, tb * TB:(tb + 1) * TB],
                    XN[:, dc * 512:(dc + 1) * 512],
                    XNk(dc * 1024, (dc + 1) * 1024), (f"xnbd{dc}_{tb}",),
                    f"qsp{dc}", f"{lname}_sp{dc}_{tb}")

        def load_xnb(tb, who):
            for dc in range(NDC):
                dma("sync", XN[:, dc * 512:(dc + 1) * 512],
                    xnbd[dc * 128:(dc + 1) * 128, tb * TB:(tb + 1) * TB],
                    (f"xnbd{dc}_{tb}",), XNk(dc * 1024, (dc + 1) * 1024),
                    f"qx{dc}", f"{who}_xnb{dc}_{tb}")

        def gating(tb, nexp, gw_sb, gw_key, cd_dram, lname):
            W = 4 * nexp
            ga = gsc[:, 0:W]
            gb = gsc[:, 64:64 + W]
            nm = gsc[:, 256:260]
            m1 = gsc[:, 260:264]
            m2 = gsc[:, 264:268]
            wv = gsc[:, 268:272]
            gs = gsc[:, 272:288]
            gk = ("gsc",)

            def v3(ap):
                return ap.rearrange("p (t n) -> p t n", t=4, n=nexp)

            def b3(ap):
                return ap.unsqueeze(2).broadcast_to([128, 4, nexp])

            for tt4 in range(4):
                def mm_log(e, tt4=tt4):
                    inst = None
                    for dc in range(NDC):
                        inst = e.matmul(
                            PH(4)[:, tt4 * nexp:(tt4 + 1) * nexp],
                            XN[:, dc * 512 + tt4 * 128:dc * 512 + (tt4 + 1) * 128],
                            gw_sb[:, dc * nexp:(dc + 1) * nexp],
                            start=(dc == 0), stop=(dc == NDC - 1))
                    return inst
                PE(mm_log, XNk(0, 8192) + (gw_key,), pk(4),
                   f"{lname}_gl_{tb}_{tt4}")
            V(lambda e: e.reduce_max(nm, v3(PH(4)[:, 0:W]), axis=AX.X),
              pk(4), gk, f"{lname}_ga_{tb}")
            V(lambda e: e.tensor_tensor(v3(ga), v3(PH(4)[:, 0:W]), b3(nm),
                                        OP.subtract),
              pk(4) + gk, gk, f"{lname}_gb_{tb}")
            S(lambda e: e.activation(ga, ga, AF.Exp), gk, gk, f"{lname}_gc_{tb}")
            V(lambda e: e.reduce_sum(nm, v3(ga), axis=AX.X), gk, gk,
              f"{lname}_gd_{tb}")
            V(lambda e: e.reciprocal(nm, nm), gk, gk, f"{lname}_ge_{tb}")
            V(lambda e: e.tensor_tensor(v3(ga), v3(ga), b3(nm), OP.mult),
              gk, gk, f"{lname}_gf_{tb}")
            if nexp == 8:
                V(lambda e: e.reduce_max(m1, v3(ga), axis=AX.X), gk, gk,
                  f"{lname}_g1_{tb}")
                V(lambda e: e.tensor_tensor(v3(gb), v3(ga), b3(m1), OP.is_ge),
                  gk, gk, f"{lname}_g2_{tb}")
                V(lambda e: e.scalar_tensor_tensor(gb, gb, -2.0, ga,
                                                   OP.mult, OP.add),
                  gk, gk, f"{lname}_g3_{tb}")
                V(lambda e: e.reduce_max(m2, v3(gb), axis=AX.X), gk, gk,
                  f"{lname}_g4_{tb}")
                V(lambda e: e.tensor_add(wv, m1, m2), gk, gk, f"{lname}_g5_{tb}")
                V(lambda e: e.reciprocal(wv, wv), gk, gk, f"{lname}_g6_{tb}")
                V(lambda e: e.tensor_tensor(v3(gb), v3(ga), b3(m2), OP.is_ge),
                  gk, gk, f"{lname}_g7_{tb}")
                V(lambda e: e.tensor_tensor(v3(ga), v3(ga), b3(wv), OP.mult),
                  gk, gk, f"{lname}_g8_{tb}")
                V(lambda e: e.tensor_mul(ga, ga, gb), gk, gk, f"{lname}_g9_{tb}")
            else:
                gs2 = gsc[:, 288:304]
                gc = gsc[:, 128:192]
                gd = gsc[:, 192:256]
                gtg = lambda ap: ap.rearrange("p (t g) -> p t g", t=4, g=4)
                b4 = lambda ap: ap.unsqueeze(2).broadcast_to([128, 4, 4])
                vg = lambda ap: ap.rearrange("p (a k) -> p a k", a=16, k=4)
                bgk = lambda ap: ap.unsqueeze(2).broadcast_to([128, 16, 4])
                v16 = lambda ap: ap.rearrange("p (t e) -> p t e", t=4, e=16)
                b16 = lambda ap: ap.unsqueeze(2).broadcast_to([128, 4, 16])
                V(lambda e: e.reduce_max(gs, vg(ga), axis=AX.X), gk, gk,
                  f"{lname}_h1_{tb}")
                V(lambda e: e.reduce_max(m1, gtg(gs), axis=AX.X), gk, gk,
                  f"{lname}_h2_{tb}")
                V(lambda e: e.tensor_tensor(gtg(gs2), gtg(gs), b4(m1),
                                            OP.is_ge), gk, gk,
                  f"{lname}_h3_{tb}")
                V(lambda e: e.scalar_tensor_tensor(gs2, gs2, -2.0, gs,
                                                   OP.mult, OP.add), gk, gk,
                  f"{lname}_h4_{tb}")
                V(lambda e: e.reduce_max(m2, gtg(gs2), axis=AX.X), gk, gk,
                  f"{lname}_h5_{tb}")
                V(lambda e: e.tensor_tensor(gtg(gs2), gtg(gs), b4(m2),
                                            OP.is_ge), gk, gk,
                  f"{lname}_h6_{tb}")
                V(lambda e: e.tensor_tensor(vg(gb), vg(ga), bgk(gs2),
                                            OP.mult), gk, ("gtm",) + gk,
                  f"{lname}_h7_{tb}")
                V(lambda e: e.tensor_copy(gc, gb), gk + ("gtm",), ("gtm",),
                  f"{lname}_h8_{tb}")
                for it in range(3):
                    V((lambda it: lambda e: e.reduce_max(
                        m1, v16(gc), axis=AX.X))(it), ("gtm",) + gk, gk,
                      f"{lname}_i{it}a_{tb}")
                    V((lambda it: lambda e: e.tensor_tensor(
                        v16(gd), v16(gc), b16(m1), OP.is_ge))(it),
                      ("gtm",) + gk, ("gtm",), f"{lname}_i{it}b_{tb}")
                    V((lambda it: lambda e: e.scalar_tensor_tensor(
                        gc, gd, -4.0, gc, OP.mult, OP.add))(it),
                      ("gtm",), ("gtm",), f"{lname}_i{it}c_{tb}")
                V(lambda e: e.reduce_max(m1, v16(gc), axis=AX.X),
                  ("gtm",) + gk, gk, f"{lname}_i3_{tb}")
                V(lambda e: e.tensor_tensor(v16(gd), v16(gb), b16(m1),
                                            OP.is_ge), gk + ("gtm",), ("gtm",),
                  f"{lname}_i4_{tb}")
                V(lambda e: e.tensor_mul(ga, ga, gd), gk + ("gtm",), gk,
                  f"{lname}_i5_{tb}")
            PE(lambda e: e.transpose(PH(5)[0:W, 0:128], ga, identb[:, :]),
               gk + ("identb",), pk(5), f"{lname}_gt_{tb}")
            V(lambda e: e.tensor_copy(gsc[0:W, 128:256], PH(5)[0:W, 0:128]),
              pk(5), ("gtm",), f"{lname}_gu_{tb}")
            for tt4 in range(4):
                dma("gpsimd",
                    cd_dram[0:nexp, (tb * 4 + tt4) * 128:(tb * 4 + tt4 + 1) * 128],
                    gsc[tt4 * nexp:(tt4 + 1) * nexp, 128:256],
                    ("gtm",), (f"cd{lname}_{tb * 4 + tt4}",),
                    "qcd", f"{lname}_gv_{tb}_{tt4}")

        for dc in range(NDC):
            for h in range(2):
                wload(dI["winT"][dc * 128:(dc + 1) * 128, h * 1024:(h + 1) * 1024],
                      PCb[:, dc * 2048 + h * 1024:dc * 2048 + (h + 1) * 1024],
                      PCk(dc * 4096 + h * 2048, dc * 4096 + (h + 1) * 2048),
                      f"winT{dc}_{h}", 1024)
        wload(dI["wstT"], wstb[:, :], ("wstb",), "wstT", 1024)
        wload(dI["wdtT"], wdtb[:, :], ("wdtb",), "wdtT", 1024)

        def l0_phase1(tb):
            layernorm(tb, dI["xT"], "L0ln")
            gating(tb, 8, gw0b, "gw0b", c0d, "L0g")
            for oc in range(16):
                bank = 5 + (oc % 2)

                def mm_ug(e, oc=oc, bank=bank):
                    inst = None
                    for dc in range(NDC):
                        inst = e.matmul(
                            PH(bank)[:, :],
                            PCb[:, dc * 2048 + oc * 128:dc * 2048 + (oc + 1) * 128],
                            XN[:, dc * 512:(dc + 1) * 512],
                            start=(dc == 0), stop=(dc == NDC - 1))
                    return inst
                PE(mm_ug, XNk(0, 8192) + PCk(0, 32768), pk(bank),
                   f"L0ug{oc}_{tb}")
                if oc < 8:
                    utk = PAk(22528 + oc * 1024, 22528 + (oc + 1) * 1024)
                    S((lambda oc, bank, utk: lambda e: e.copy(
                        PAb[:, 11264 + oc * 512:11264 + (oc + 1) * 512],
                        PH(bank)[:, :]))(oc, bank, utk),
                      pk(bank), utk, f"L0ue{oc}_{tb}")
                else:
                    og = oc - 8
                    S((lambda og, bank: lambda e: e.activation(
                        SGb[:, og * 2048 + tb * 512:og * 2048 + (tb + 1) * 512],
                        PH(bank)[:, :], AF.Silu))(og, bank),
                      pk(bank), SGk(og * 4096 + tb * 1024,
                                    og * 4096 + (tb + 1) * 1024),
                      f"L0sg{og}_{tb}")
            utk_all = PAk(22528, 30720)

            def mm_bt(e):
                inst = None
                for dc in range(NDC):
                    inst = e.matmul(PH(2)[:, :],
                                    wstb[:, dc * 128:(dc + 1) * 128],
                                    PAb[:, 11264 + dc * 512:11264 + (dc + 1) * 512],
                                    start=(dc == 0), stop=(dc == NDC - 1))
                return inst
            PE(mm_bt, utk_all + ("wstb",), pk(2), f"L0bt_{tb}")
            btk = PAk(16384, 18432)
            V(lambda e: e.tensor_copy(PA[:, 4096:4608], PH(2)[:, :]),
              pk(2), btk, f"L0bte_{tb}")

            def mm_dt(e):
                inst = None
                for dc in range(NDC):
                    inst = e.matmul(PH(3)[:, :],
                                    wdtb[:, dc * 128:(dc + 1) * 128],
                                    PAb[:, 11264 + dc * 512:11264 + (dc + 1) * 512],
                                    start=(dc == 0), stop=(dc == NDC - 1))
                return inst
            PE(mm_dt, utk_all + ("wdtb",), pk(3), f"L0dt_{tb}")
            sak = PAk(18432, 20480)
            S(lambda e: e.activation(PA[:, 4608:5120], PH(3)[:, :], AF.Exp,
                                     bias=vc("bdt")[:, 0:1]),
              pk(3) + ("vcon",), sak, f"L0dta_{tb}")
            S(lambda e: e.activation(PA[:, 4608:5120], PA[:, 4608:5120],
                                     AF.Ln, bias=1.0), sak, sak, f"L0dtb_{tb}")
            V(lambda e: e.tensor_scalar(PA[:, 4608:5120], PA[:, 4608:5120],
                                        10.0, vc("negrate")[:, 0:1],
                                        OP.min, OP.mult),
              sak + ("vcon",), sak, f"L0dtc_{tb}")
            deck = PAk(tb * 2048, (tb + 1) * 2048)
            S(lambda e: e.activation(PA[:, tb * 512:(tb + 1) * 512],
                                     PA[:, 4608:5120], AF.Exp),
              sak, deck, f"L0dec_{tb}")
            sbk = PAk(20480, 22528)
            V(lambda e: e.tensor_scalar(PA[:, 5120:5632],
                                        PA[:, tb * 512:(tb + 1) * 512],
                                        -1.0, 1.0, OP.mult, OP.add),
              deck, sbk, f"L0od_{tb}")
            vvk = PAk(8192 + tb * 2048, 8192 + (tb + 1) * 2048)
            V(lambda e: e.tensor_mul(PA[:, 2048 + tb * 512:2048 + (tb + 1) * 512],
                                     PA[:, 5120:5632], PA[:, 4096:4608]),
              sbk + btk, vvk, f"L0vv_{tb}")

        for tb in (() if "l0" in KSKIP else range(NTB)):
            l0_phase1(tb)

        if "l0" not in KSKIP:
            V(lambda e: e.tensor_tensor_scan(PBb[:, 0:2048], PA[:, 0:2048],
                                             PA[:, 2048:4096], 0.0,
                                             OP.mult, OP.add),
              PAk(0, 16384), PBk(0, 4096), "L0scan")

        wload(dI["wcT"], wcb[:, :], ("wcb",), "wcT", 1024)
        for dc in range(NDC):
            wload(dI["woT"][dc * 128:(dc + 1) * 128, :],
                  PCb[:, dc * 1024:(dc + 1) * 1024],
                  PCk(dc * 2048, (dc + 1) * 2048), f"woT{dc}", 1024)
            wload(dI["woT"][dc * 128:(dc + 1) * 128, :],
                  PCb[:, 8192 + dc * 1024:8192 + (dc + 1) * 1024],
                  PCk(16384 + dc * 2048, 16384 + (dc + 1) * 2048),
                  f"woTD{dc}", 1024,
                  scale_col=vc("dcol", 8)[:, dc:dc + 1], scale_keys=("vcon",))

        def l0_phase2(tb):
            load_xnb(tb, "p2")
            for oc in range(NDC):
                bank = oc % 2
                PE((lambda oc, bank: lambda e: e.matmul(
                    PH(bank)[:, :], wcb[:, oc * 128:(oc + 1) * 128],
                    PBb[:, tb * 512:(tb + 1) * 512],
                    start=True, stop=True))(oc, bank),
                   ("wcb",) + PBk(0, 4096), pk(bank), f"L0y{oc}_{tb}")
                ztk = PAk(22528 + oc * 1024, 22528 + (oc + 1) * 1024)
                V((lambda oc, bank: lambda e: e.scalar_tensor_tensor(
                    PAb[:, 11264 + oc * 512:11264 + (oc + 1) * 512],
                    PH(bank)[:, :], 1.0,
                    SGb[:, oc * 2048 + tb * 512:oc * 2048 + (tb + 1) * 512],
                    OP.mult, OP.mult))(oc, bank),
                  pk(bank) + SGk(oc * 4096 + tb * 1024,
                                 oc * 4096 + (tb + 1) * 1024),
                  ztk, f"L0z{oc}_{tb}")
            ztk_all = PAk(22528, 30720)
            for oc in range(NDC):
                bank = 2 + (oc % 4)

                def mm_out(e, oc=oc, bank=bank):
                    inst = None
                    for dc in range(NDC):
                        inst = e.matmul(
                            PH(bank)[:, :],
                            PCb[:, 8192 + dc * 1024 + oc * 128:
                                8192 + dc * 1024 + (oc + 1) * 128],
                            XN[:, dc * 512:(dc + 1) * 512],
                            start=(dc == 0), stop=False)
                    for dc in range(NDC):
                        inst = e.matmul(
                            PH(bank)[:, :],
                            PCb[:, dc * 1024 + oc * 128:
                                dc * 1024 + (oc + 1) * 128],
                            PAb[:, 11264 + dc * 512:11264 + (dc + 1) * 512],
                            start=False, stop=(dc == NDC - 1))
                    return inst
                PE(mm_out, XNk(0, 8192) + ztk_all + PCk(0, 32768), pk(bank),
                   f"L0o{oc}_{tb}")
                xi = oc % 2
                dma("sync", xat[xi][:, :],
                    dI["xT"][oc * 128:(oc + 1) * 128, tb * TB:(tb + 1) * TB],
                    (), (f"xat{xi}",), f"qxa{xi}", f"L0xa{oc}_{tb}")
                V((lambda oc, bank, xi: lambda e: e.scalar_tensor_tensor(
                    osb[xi][:, :], PH(bank)[:, :], 1.0, xat[xi][:, :],
                    OP.mult, OP.add))(oc, bank, xi),
                  pk(bank) + (f"xat{xi}",), (f"osb{xi}",), f"L0oe{oc}_{tb}")
                dma("gpsimd", xa0[oc * 128:(oc + 1) * 128, tb * TB:(tb + 1) * TB],
                    osb[xi][:, :], (f"osb{xi}",), (f"xa0_{oc}_{tb}",),
                    f"qst{xi}", f"L0os{oc}_{tb}")

        for tb in (() if "l0" in KSKIP else range(NTB)):
            l0_phase2(tb)

        for tb in range(NTB):
            l0_phase2(tb)

        for dc in range(NDC):
            wload(dI["a1T"][dc * 128:(dc + 1) * 128, :],
                  a1Tb[:, dc * 128:(dc + 1) * 128], ("a1Tb",), f"a1T{dc}", 128)
            wload(dI["a3T"][dc * 128:(dc + 1) * 128, :],
                  a3Tb[:, dc * 128:(dc + 1) * 128], ("a3Tb",), f"a3T{dc}", 128)
        for g in range(2):
            wload(dI["b2s"][g], b2pb[g][:, :], (f"b2pb{g}",), f"b2s{g}", 1024)

        def l0_ffn(tb):
            load_xnb(tb, "ffn")
            for fc in (() if "ffnc" in KSKIP else range(NFC)):
                si = fc % 2
                wgk = SGk(16384 + si * 2048, 16384 + (si + 1) * 2048)
                wuk = SGk(20480 + si * 2048, 20480 + (si + 1) * 2048)
                wload(dI["wg_s"][fc],
                      SGb[:, 8192 + si * 1024:8192 + (si + 1) * 1024],
                      wgk, f"wg{fc}_{tb}", 1024, ceng="V")
                wload(dI["wu_s"][fc],
                      SGb[:, 10240 + si * 1024:10240 + (si + 1) * 1024],
                      wuk, f"wu{fc}_{tb}", 1024, ceng="V")
                bank = fc % 2

                def mm_c1(e, fc=fc, si=si, bank=bank):
                    inst = None
                    for dc in range(NDC):
                        inst = e.matmul(
                            PH(bank)[:, :],
                            SGb[:, 8192 + si * 1024 + dc * 128:
                                8192 + si * 1024 + (dc + 1) * 128],
                            XN[:, dc * 512:(dc + 1) * 512],
                            start=(dc == 0), stop=(dc == NDC - 1))
                    return inst
                PE(mm_c1, XNk(0, 8192) + wgk, pk(bank), f"c1m{fc}_{tb}")
                S((lambda fc, bank: lambda e: e.copy(
                    PAb[:, fc * 512:(fc + 1) * 512], PH(bank)[:, :]))(fc, bank),
                  pk(bank), PAk(fc * 1024, (fc + 1) * 1024), f"c1e{fc}_{tb}")

                def mm_c3(e, fc=fc, si=si):
                    inst = None
                    for dc in range(NDC):
                        inst = e.matmul(
                            PH(6)[:, :],
                            SGb[:, 10240 + si * 1024 + dc * 128:
                                10240 + si * 1024 + (dc + 1) * 128],
                            XN[:, dc * 512:(dc + 1) * 512],
                            start=(dc == 0), stop=(dc == NDC - 1))
                    return inst
                PE(mm_c3, XNk(0, 8192) + wuk, pk(6), f"c3m{fc}_{tb}")
                S((lambda fc: lambda e: e.copy(
                    PBb[:, fc * 512:(fc + 1) * 512], PH(6)[:, :]))(fc),
                  pk(6), PBk(fc * 1024, (fc + 1) * 1024), f"c3e{fc}_{tb}")

            for ex in (() if "ffne" in KSKIP else range(E0)):
                si = ex % 2
                wload(dI["b1s"][ex], b1sb[si][:, :], (f"b1sb{si}",),
                      f"b1_{ex}_{tb}", 1024)
                wload(dI["b3s"][ex], b3sb[si][:, :], (f"b3sb{si}",),
                      f"b3_{ex}_{tb}", 1024)
                wload(dI["a2s"][ex], a2sb[si][:, :], (f"a2sb{si}",),
                      f"a2_{ex}_{tb}", 512)
                dma("sync", ceb[si][:, :],
                    c0d[ex:ex + 1, tb * TB:(tb + 1) * TB].broadcast_to([128, TB]),
                    tuple(f"cdL0g_{tt}" for tt in range(tb * 4, tb * 4 + 4)),
                    (f"ceb{si}",), f"qce{si}", f"ce{ex}_{tb}")

                def mm_xa1(e, ex=ex):
                    inst = None
                    for dc in range(NDC):
                        inst = e.matmul(
                            PH(7)[0:16, :],
                            a1Tb[:, dc * 128 + ex * 16:dc * 128 + ex * 16 + 16],
                            XN[:, dc * 512:(dc + 1) * 512],
                            start=(dc == 0), stop=(dc == NDC - 1))
                    return inst
                PE(mm_xa1, XNk(0, 8192) + ("a1Tb",), pk(7), f"xa1m{ex}_{tb}")
                S((lambda si: lambda e: e.activation(
                    xa1s[si][0:16, :], PH(7)[0:16, :],
                    AF.Copy, scale=vc("cvec", 4)[0:16, 3:4]))(si),
                  pk(7) + ("vcon",), (f"xa1s{si}",), f"xa1e{ex}_{tb}")
                for q in (1, 2, 3):
                    dma("sync", xa1s[si][32 * q:32 * q + 16, :],
                        xa1s[si][0:16, :], (f"xa1s{si}",), (f"xa1s{si}r{q}",),
                        f"qxr1{si}_{q}", f"xa1r{ex}_{q}_{tb}")

                def mm_xa3(e, ex=ex):
                    inst = None
                    for dc in range(NDC):
                        inst = e.matmul(
                            PH(7)[0:16, :],
                            a3Tb[:, dc * 128 + ex * 16:dc * 128 + ex * 16 + 16],
                            XN[:, dc * 512:(dc + 1) * 512],
                            start=(dc == 0), stop=(dc == NDC - 1))
                    return inst
                PE(mm_xa3, XNk(0, 8192) + ("a3Tb",), pk(7), f"xa3m{ex}_{tb}")
                S((lambda si: lambda e: e.activation(
                    xa3s[si][0:16, :], PH(7)[0:16, :],
                    AF.Copy, scale=vc("cvec", 4)[0:16, 3:4]))(si),
                  pk(7) + ("vcon",), (f"xa3s{si}",), f"xa3e{ex}_{tb}")
                for q in (1, 2, 3):
                    dma("sync", xa3s[si][32 * q:32 * q + 16, :],
                        xa3s[si][0:16, :], (f"xa3s{si}",), (f"xa3s{si}r{q}",),
                        f"qxr3{si}_{q}", f"xa3r{ex}_{q}_{tb}")

                zq = []

                def flush_z():
                    mmz, rds, nm = zq.pop(0)
                    PE(mmz, rds, pk(6), nm)

                aq = []

                def flush_a():
                    fp, par, adst, adk = aq.pop(0)
                    V((lambda adst, par: lambda e: e.tensor_mul(
                        adst, sss[par][:, :], t3s[par][:, :]))(adst, par),
                      (f"sss{par}", f"t3s{par}"), adk, f"ap_{ex}_{fp}_{tb}")
                    fc0, fc1 = 2 * fp, 2 * fp + 1

                    def mm_z(e, fc0=fc0, fc1=fc1, si=si, adst=adst):
                        inst = None
                        for h, fc in ((0, fc0), (1, fc1)):
                            inst = e.matmul(
                                PH(6)[0:16, :],
                                a2sb[si][:, fc * 16:fc * 16 + 16],
                                adst[:, h * 512:(h + 1) * 512],
                                start=(fc == 0), stop=(fc == NFC - 1))
                        return inst
                    zq.append((mm_z, (f"a2sb{si}",) + adk, f"z_{ex}_{fp}_{tb}"))
                    if len(zq) > 1:
                        flush_z()
                    if ex > 0:
                        dma("gpsimd", PCb[:, fp * 1024:(fp + 1) * 1024],
                            adst,
                            adk + PCk(fp * 2048, (fp + 1) * 2048),
                            PCk(fp * 2048, (fp + 1) * 2048),
                            f"qmx{fp % 2}", f"mx_{ex}_{fp}_{tb}", accum=True)

                for fp in range(16):
                    fc0, fc1 = 2 * fp, 2 * fp + 1
                    par = fp % 2
                    d1p = [PP[1], PP[0]][par]
                    d1k = (pk(2) + pk(3), pk(0) + pk(1))[par]

                    def mm_d1(e, fc0=fc0, fc1=fc1, si=si, d1p=d1p):
                        for h, fc in ((0, fc0), (1, fc1)):
                            q = fc % 4
                            blk = fc // 4
                            inst = e.matmul(
                                d1p[:, h * 512:(h + 1) * 512],
                                b1sb[si][32 * q:32 * q + 16,
                                         blk * 128:(blk + 1) * 128],
                                xa1s[si][32 * q:32 * q + 16, :],
                                start=True, stop=True,
                                tile_position=(32 * q, 0))
                        return inst
                    PE(mm_d1, (f"b1sb{si}", f"xa1s{si}") +
                       tuple(f"xa1s{si}r{q}" for q in (1, 2, 3)),
                       d1k, f"d1_{ex}_{fp}_{tb}")

                    def mm_d3(e, fc0=fc0, fc1=fc1, si=si):
                        for h, fc in ((0, fc0), (1, fc1)):
                            q = fc % 4
                            blk = fc // 4
                            inst = e.matmul(
                                PP[2][:, h * 512:(h + 1) * 512],
                                b3sb[si][32 * q:32 * q + 16,
                                         blk * 128:(blk + 1) * 128],
                                xa3s[si][32 * q:32 * q + 16, :],
                                start=True, stop=True,
                                tile_position=(32 * q, 0))
                        return inst
                    PE(mm_d3, (f"b3sb{si}", f"xa3s{si}") +
                       tuple(f"xa3s{si}r{q}" for q in (1, 2, 3)),
                       pk(4) + pk(5), f"d3_{ex}_{fp}_{tb}")
                    c1k = PAk(fc0 * 1024, (fc1 + 1) * 1024)
                    c3k = PBk(fc0 * 1024, (fc1 + 1) * 1024)
                    V((lambda fc0, par, d1p: lambda e: e.scalar_tensor_tensor(
                        scr1[:, :] if par == 0 else scr2[:, :], d1p[:, :], 1.0,
                        PAb[:, fc0 * 512:(fc0 + 2) * 512], OP.mult, OP.add))(fc0, par, d1p),
                      d1k + c1k, (("scr1",), ("scr2",))[par], f"t1_{ex}_{fp}_{tb}")
                    V((lambda fc0, par: lambda e: e.scalar_tensor_tensor(
                        t3s[par][:, :], PP[2][:, :], 1.0,
                        PBb[:, fc0 * 512:(fc0 + 2) * 512], OP.mult, OP.add))(fc0, par),
                      pk(4) + pk(5) + c3k, (f"t3s{par}",), f"t3_{ex}_{fp}_{tb}")
                    S((lambda par: lambda e: e.activation(
                        sss[par][:, :], scr1[:, :] if par == 0 else scr2[:, :],
                        AF.Silu))(par),
                      (("scr1",), ("scr2",))[par], (f"sss{par}",),
                      f"si_{ex}_{fp}_{tb}")
                    G((lambda par, si: lambda e: e.tensor_tensor(
                        sss[par][:, :].rearrange("p (a b) -> p a b", a=2, b=512),
                        sss[par][:, :].rearrange("p (a b) -> p a b", a=2, b=512),
                        ceb[si][:, :].unsqueeze(1).broadcast_to([128, 2, 512]),
                        OP.mult))(par, si),
                      (f"sss{par}", f"ceb{si}"), (f"sss{par}",),
                      f"sA_{ex}_{fp}_{tb}")
                    if ex == 0:
                        adst = PCb[:, fp * 1024:(fp + 1) * 1024]
                        adk = PCk(fp * 2048, (fp + 1) * 2048)
                    else:
                        adst = actp[fp % 2][:, :]
                        adk = (f"actp{fp % 2}",)
                    aq.append((fp, par, adst, adk))
                    if len(aq) > 1:
                        flush_a()
                while aq:
                    flush_a()
                while zq:
                    flush_z()
                S((lambda ex: lambda e: e.activation(
                    SGb[0:16, 12288 + ex * 512:12288 + (ex + 1) * 512],
                    PH(6)[0:16, :], AF.Copy,
                    scale=vc("cvec", 4)[0:16, 3:4]))(ex),
                  pk(6) + ("vcon",), SGk(24576 + ex * 1024, 24576 + (ex + 1) * 1024),
                  f"ze_{ex}_{tb}")
                qz = ex % 4
                dma("sync", (lambda ex, qz: zpk[
                    32 * qz:32 * qz + 16,
                    (ex // 4) * 512:(ex // 4 + 1) * 512])(ex, qz),
                    SGb[0:16, 12288 + ex * 512:12288 + (ex + 1) * 512],
                    SGk(24576 + ex * 1024, 24576 + (ex + 1) * 1024),
                    (f"zpk{ex}",), f"qzr{ex}", f"zr_{ex}_{tb}")

            allb = tuple(x for i in range(8) for x in pk(i))
            if "ffns" in KSKIP:
                return
            for g in range(8):
                wdk = []
                for j in range(4):
                    fc = 4 * g + j
                    sl = fc % 8
                    kk = SGk(sl * 2048, (sl + 1) * 2048)
                    wdk.append(kk)
                    wload(dI["wd_s"][fc],
                          SGb[:, sl * 1024:(sl + 1) * 1024], kk,
                          f"wd{fc}_{tb}", 1024)

                def mm_sw(e, g=g):
                    inst = None
                    laststop = ("ffnb" in KSKIP)
                    for j in range(4):
                        fc = 4 * g + j
                        sl = fc % 8
                        for oc in range(NDC):
                            inst = e.matmul(
                                PH(oc)[:, :],
                                SGb[:, sl * 1024 + oc * 128:
                                    sl * 1024 + (oc + 1) * 128],
                                PCb[:, fc * 512:(fc + 1) * 512],
                                start=(fc == 0), stop=(laststop and fc == 31))
                    return inst
                PE(mm_sw, tuple(x for k in wdk for x in k) + PCk(0, 32768),
                   allb, f"sw{g}_{tb}")

            def mm_b2(e):
                inst = None
                for oc in range(NDC):
                    for g in range(2):
                        inst = e.matmul(
                            PH(oc)[:, :],
                            b2pb[g][:, oc * 128:(oc + 1) * 128],
                            zpk[:, g * 512:(g + 1) * 512],
                            start=False, stop=(g == 1))
                return inst
            if "ffnb" not in KSKIP:
                PE(mm_b2, ("b2pb0", "b2pb1", "zpk_pad") +
                   tuple(f"zpk{ex}" for ex in range(E0)),
                   allb, f"b2_{tb}")
            for oc in range(NDC):
                xi = oc % 2
                dma("sync", xat[xi][:, :],
                    xa0[oc * 128:(oc + 1) * 128, tb * TB:(tb + 1) * TB],
                    (f"xa0_{oc}_{tb}",), (f"xat{xi}",), f"qxa{xi}",
                    f"f0xa{oc}_{tb}")
                V((lambda oc, xi: lambda e: e.scalar_tensor_tensor(
                    osb[xi][:, :], PH(oc)[:, :], 1.0, xat[xi][:, :],
                    OP.mult, OP.add))(oc, xi),
                  pk(oc) + (f"xat{xi}",), (f"osb{xi}",), f"f0oe{oc}_{tb}")
                dma("gpsimd",
                    xa0[oc * 128:(oc + 1) * 128, tb * TB:(tb + 1) * TB],
                    osb[xi][:, :], (f"osb{xi}",), (f"xa0_{oc}_{tb}",),
                    f"qst{xi}", f"f0os{oc}_{tb}")

        for tb in (() if ("l0" in KSKIP or "l0f" in KSKIP) else range(NTB)):
            l0_ffn(tb)

        for dc in range(NDC):
            wload(dI["bw"][dc * 128:(dc + 1) * 128, :],
                  PCb[:, dc * 1024:(dc + 1) * 1024],
                  PCk(dc * 2048, (dc + 1) * 2048), f"bw{dc}", 1024)

        def l1_phase1(tb):
            layernorm(tb, xa0, "L1ln")
            gating(tb, 16, gw1b, "gw1b", c1d, "L1g")
            for ec in range(NDC):
                bank = 5 + (ec % 2)

                def mm_xb(e, ec=ec, bank=bank):
                    inst = None
                    for dc in range(NDC):
                        inst = e.matmul(
                            PH(bank)[:, :],
                            PCb[:, dc * 1024 + ec * 128:dc * 1024 + (ec + 1) * 128],
                            XN[:, dc * 512:(dc + 1) * 512],
                            start=(dc == 0), stop=(dc == NDC - 1))
                    return inst
                PE(mm_xb, XNk(0, 8192) + PCk(0, 16384), pk(bank),
                   f"L1xb{ec}_{tb}")
                xi = ec % 2
                V((lambda ec, bank, xi: lambda e: e.tensor_scalar(
                    osb[xi][:, :], PH(bank)[:, :],
                    vc("oml", 8)[:, ec:ec + 1],
                    vc("omlbb", 8)[:, ec:ec + 1], OP.mult, OP.add))(ec, bank, xi),
                  pk(bank) + ("vcon",), (f"osb{xi}",), f"L1ve{ec}_{tb}")
                dma("gpsimd",
                    vdram[ec * 128:(ec + 1) * 128, tb * TB:(tb + 1) * TB],
                    osb[xi][:, :], (f"osb{xi}",), (f"vd{ec}_{tb}",),
                    f"qst{xi}", f"L1vs{ec}_{tb}")

        for tb in (() if "l1" in KSKIP else range(NTB)):
            l1_phase1(tb)

        for ec in (() if "l1" in KSKIP else range(NDC)):
            si = ec % 2
            vlo = si * 2048
            vk_ = PAk(vlo * 4, (vlo + 2048) * 4)
            llo = 4096 + si * 2048
            lk_ = PAk(llo * 4, (llo + 2048) * 4)
            dma("sync", PA[:, vlo:vlo + 2048],
                vdram[ec * 128:(ec + 1) * 128, :],
                tuple(f"vd{ec}_{tb}" for tb in range(NTB)), vk_,
                f"qv{si}", f"L1vl{ec}")
            V((lambda ec, vlo: lambda e: e.tensor_tensor_scan(
                SGb[:, ec * 2048:(ec + 1) * 2048],
                vc("lam", 8)[:, ec:ec + 1].broadcast_to([128, 2048]),
                PA[:, vlo:vlo + 2048], 0.0, OP.mult, OP.add))(ec, vlo),
              vk_ + ("vcon",), SGk(ec * 4096, (ec + 1) * 4096), f"L1sc{ec}")

        for ec in range(NDC):
            wload(dI["cw"][ec * 128:(ec + 1) * 128, :],
                  PCb[:, 8192 + ec * 1024:8192 + (ec + 1) * 1024],
                  PCk(16384 + ec * 2048, 16384 + (ec + 1) * 2048),
                  f"cw{ec}", 1024)

        def l1_out(tb):
            for oc in range(NDC):
                bank = oc % 4

                def mm_o(e, oc=oc, bank=bank):
                    inst = None
                    for ec in range(NDC):
                        inst = e.matmul(
                            PH(bank)[:, :],
                            PCb[:, 8192 + ec * 1024 + oc * 128:
                                8192 + ec * 1024 + (oc + 1) * 128],
                            SGb[:, ec * 2048 + tb * 512:ec * 2048 + (tb + 1) * 512],
                            start=(ec == 0), stop=(ec == NDC - 1))
                    return inst
                PE(mm_o, SGk(0, 32768) + PCk(16384, 32768), pk(bank),
                   f"L1o{oc}_{tb}")
                xi = oc % 2
                dma("sync", xat[xi][:, :],
                    xa0[oc * 128:(oc + 1) * 128, tb * TB:(tb + 1) * TB],
                    (f"xa0_{oc}_{tb}",), (f"xat{xi}",), f"qxa{xi}",
                    f"L1xa{oc}_{tb}")
                V((lambda oc, bank, xi: lambda e: e.scalar_tensor_tensor(
                    osb[xi][:, :], PH(bank)[:, :],
                    vc("cbcol", 8)[:, oc:oc + 1], xat[xi][:, :],
                    OP.add, OP.add))(oc, bank, xi),
                  pk(bank) + (f"xat{xi}", "vcon"), (f"osb{xi}",),
                  f"L1oe{oc}_{tb}")
                dma("gpsimd",
                    outT[oc * 128:(oc + 1) * 128, tb * TB:(tb + 1) * TB],
                    osb[xi][:, :], (f"osb{xi}",), (f"out_{oc}_{tb}",),
                    f"qst{xi}", f"L1os{oc}_{tb}")

        for tb in (() if "l1" in KSKIP else range(NTB)):
            l1_out(tb)

        for dc in range(NDC):
            wload(dI["ws1"][dc * 128:(dc + 1) * 128, :],
                  PCb[:, dc * 1024:(dc + 1) * 1024],
                  PCk(dc * 2048, (dc + 1) * 2048), f"ws1_{dc}", 1024)
            wload(dI["ws3"][dc * 128:(dc + 1) * 128, :],
                  PCb[:, 8192 + dc * 1024:8192 + (dc + 1) * 1024],
                  PCk(16384 + dc * 2048, 16384 + (dc + 1) * 2048),
                  f"ws3_{dc}", 1024)

        def l1_absh(tb):
            load_xnb(tb, "ab")
            for ic in range(NSH):
                b1 = ic % 2
                b3 = 2 + (ic % 2)

                def mm_h1(e, ic=ic, b1=b1):
                    inst = None
                    for dc in range(NDC):
                        inst = e.matmul(
                            PH(b1)[:, :],
                            PCb[:, dc * 1024 + ic * 128:dc * 1024 + (ic + 1) * 128],
                            XN[:, dc * 512:(dc + 1) * 512],
                            start=(dc == 0), stop=(dc == NDC - 1))
                    return inst
                PE(mm_h1, XNk(0, 8192) + PCk(0, 16384), pk(b1),
                   f"sh1_{ic}_{tb}")

                def mm_h3(e, ic=ic, b3=b3):
                    inst = None
                    for dc in range(NDC):
                        inst = e.matmul(
                            PH(b3)[:, :],
                            PCb[:, 8192 + dc * 1024 + ic * 128:
                                8192 + dc * 1024 + (ic + 1) * 128],
                            XN[:, dc * 512:(dc + 1) * 512],
                            start=(dc == 0), stop=(dc == NDC - 1))
                    return inst
                PE(mm_h3, XNk(0, 8192) + PCk(16384, 32768), pk(b3),
                   f"sh3_{ic}_{tb}")
                S((lambda b1: lambda e: e.activation(
                    scr1[:, 0:512], PH(b1)[:, :], AF.Silu))(b1),
                  pk(b1), ("scr1",), f"shs_{ic}_{tb}")
                V((lambda ic, b3: lambda e: e.scalar_tensor_tensor(
                    SGb[:, ic * 2048 + tb * 512:ic * 2048 + (tb + 1) * 512],
                    PH(b3)[:, :], 1.0, scr1[:, 0:512], OP.mult, OP.mult))(ic, b3),
                  pk(b3) + ("scr1",),
                  SGk(ic * 4096 + tb * 1024, ic * 4096 + (tb + 1) * 1024),
                  f"sha_{ic}_{tb}")

        if "l1f" not in KSKIP and "l1" not in KSKIP:
            for tb in range(NTB):
                l1_absh(tb)

        for ic in range(NSH):
            wload(dI["ws2"][ic * 128:(ic + 1) * 128, :],
                  PCb[:, ic * 1024:(ic + 1) * 1024],
                  PCk(ic * 2048, (ic + 1) * 2048), f"ws2_{ic}", 1024)

        def emit_sweep(wv, tb):
            for oc in range(NDC):
                bank = 4 + (oc % 4)

                def mm_w2(e, wv=wv, oc=oc, bank=bank, tb=tb):
                    inst = None
                    n_mm = 8 + (NSH if wv == 0 else 0)
                    i = 0
                    for e4 in range(2):
                        for ic in range(NIC):
                            aw = 8192 + (tb % 2) * 4096 + (e4 * 4 + ic) * 512
                            i += 1
                            inst = e.matmul(
                                PH(bank)[:, :],
                                PBb[:, e4 * 4096 + ic * 1024 + oc * 128:
                                    e4 * 4096 + ic * 1024 + (oc + 1) * 128],
                                PCb[:, aw:aw + 512],
                                start=(i == 1), stop=(i == n_mm))
                    if wv == 0:
                        for ic in range(NSH):
                            i += 1
                            inst = e.matmul(
                                PH(bank)[:, :],
                                PCb[:, ic * 1024 + oc * 128:
                                    ic * 1024 + (oc + 1) * 128],
                                SGb[:, ic * 2048 + tb * 512:
                                    ic * 2048 + (tb + 1) * 512],
                                start=False, stop=(i == n_mm))
                    return inst
                base = 16384 + (tb % 2) * 8192
                rd = PBk(0, 16384) + PCk(base, base + 8192)
                if wv == 0:
                    rd = rd + PCk(0, 16384) + SGk(0, 32768)
                PE(mm_w2, rd, pk(bank), f"ew2_{wv}_{oc}_{tb}")
                xi = oc % 2
                V((lambda oc, bank, xi: lambda e: e.tensor_copy(
                    osb[xi][:, :], PH(bank)[:, :]))(oc, bank, xi),
                  pk(bank), (f"osb{xi}",), f"ewe_{wv}_{oc}_{tb}")
                dma("gpsimd",
                    outT[oc * 128:(oc + 1) * 128, tb * TB:(tb + 1) * TB],
                    osb[xi][:, :], (f"osb{xi}",), (f"out_{oc}_{tb}",),
                    f"qac{xi}", f"ews_{wv}_{oc}_{tb}", accum=True)

        for wv in (() if ("l1f" in KSKIP or "l1" in KSKIP) else range(8)):
            for e4 in range(2):
                ex = 2 * wv + e4
                si = e4
                for dc in range(NDC):
                    wload(dI["w1"][ex][dc * 128:(dc + 1) * 128, :],
                          PAb[:, si * 4096 + dc * 512:si * 4096 + (dc + 1) * 512],
                          PAk(si * 8192 + dc * 1024,
                              si * 8192 + (dc + 1) * 1024),
                          f"w1_{ex}_{dc}", 512)
                    wload(dI["w3"][ex][dc * 128:(dc + 1) * 128, :],
                          PAb[:, 8192 + si * 4096 + dc * 512:
                              8192 + si * 4096 + (dc + 1) * 512],
                          PAk(16384 + si * 8192 + dc * 1024,
                              16384 + si * 8192 + (dc + 1) * 1024),
                          f"w3_{ex}_{dc}", 512)
                for ic in range(NIC):
                    wload(dI["w2"][ex][ic * 128:(ic + 1) * 128, :],
                          PBb[:, e4 * 4096 + ic * 1024:
                              e4 * 4096 + (ic + 1) * 1024],
                          PBk(e4 * 8192 + ic * 2048,
                              e4 * 8192 + (ic + 1) * 2048),
                          f"w2_{ex}_{ic}", 1024)
            swq = []
            for tb in range(NTB):
                load_xnb(tb, f"w{wv}")
                for e4 in range(2):
                    ex = 2 * wv + e4
                    si = e4
                    ci = e4 % 2
                    dma("sync", ceb[ci][:, :],
                        c1d[ex:ex + 1, tb * TB:(tb + 1) * TB]
                        .broadcast_to([128, TB]),
                        tuple(f"cdL1g_{tt}" for tt in range(tb * 4, tb * 4 + 4)),
                        (f"ceb{ci}",), f"qce{ci}", f"ce1_{ex}_{tb}")
                    for ic in range(NIC):
                        b1 = ic % 2
                        b3 = 2 + (ic % 2)

                        def mm_e1(e, si=si, ic=ic, b1=b1):
                            inst = None
                            for dc in range(NDC):
                                inst = e.matmul(
                                    PH(b1)[:, :],
                                    PAb[:, si * 4096 + dc * 512 + ic * 128:
                                        si * 4096 + dc * 512 + (ic + 1) * 128],
                                    XN[:, dc * 512:(dc + 1) * 512],
                                    start=(dc == 0), stop=(dc == NDC - 1))
                            return inst
                        PE(mm_e1, XNk(0, 8192) + PAk(si * 8192,
                                                     si * 8192 + 8192),
                           pk(b1), f"eh1_{ex}_{ic}_{tb}")

                        def mm_e3(e, si=si, ic=ic, b3=b3):
                            inst = None
                            for dc in range(NDC):
                                inst = e.matmul(
                                    PH(b3)[:, :],
                                    PAb[:, 8192 + si * 4096 + dc * 512 + ic * 128:
                                        8192 + si * 4096 + dc * 512 + (ic + 1) * 128],
                                    XN[:, dc * 512:(dc + 1) * 512],
                                    start=(dc == 0), stop=(dc == NDC - 1))
                            return inst
                        PE(mm_e3, XNk(0, 8192) + PAk(16384 + si * 8192,
                                                     16384 + si * 8192 + 8192),
                           pk(b3), f"eh3_{ex}_{ic}_{tb}")
                        S((lambda b1: lambda e: e.activation(
                            scr1[:, 0:512], PH(b1)[:, :], AF.Silu))(b1),
                          pk(b1), ("scr1",), f"ehs_{ex}_{ic}_{tb}")
                        V((lambda b3, ci: lambda e: e.scalar_tensor_tensor(
                            scr2[:, 0:512], PH(b3)[:, :], 1.0, ceb[ci][:, :],
                            OP.mult, OP.mult))(b3, ci),
                          pk(b3) + (f"ceb{ci}",), ("scr2",),
                          f"ehq_{ex}_{ic}_{tb}")
                        aw = 8192 + (tb % 2) * 4096 + (e4 * 4 + ic) * 512
                        V((lambda aw: lambda e: e.tensor_mul(
                            PCb[:, aw:aw + 512], scr1[:, 0:512],
                            scr2[:, 0:512]))(aw),
                          ("scr1", "scr2"), PCk(aw * 2, aw * 2 + 1024),
                          f"eha_{ex}_{ic}_{tb}")
                swq.append(tb)
                if len(swq) > 1:
                    emit_sweep(wv, swq.pop(0))
            while swq:
                emit_sweep(wv, swq.pop(0))

        fin = ("out_" if "l1" not in KSKIP else "xa0_")
        p.add("sync", lambda e: None,
              reads=tuple(f"{fin}{oc}_{tb}" for oc in range(NDC)
                          for tb in range(NTB)),
              name="final")

        with nc.Block() as block:
            p.emit_into(nc, block, ctx)
    return nc


def kernel(**inputs):
    w = _prep_weights(inputs)
    x = np.asarray(inputs["L0_attn_x"], dtype=np.float32)
    nc = build()
    in_maps = []
    for b in range(8):
        m = {"xT": np.ascontiguousarray(x[b].T)}
        m.update(w)
        in_maps.append(m)
    res = run_bass_kernel_spmd(nc, in_maps, core_ids=list(range(8)))
    out = np.stack([np.ascontiguousarray(res.results[b]["outT"].T)
                    for b in range(8)])
    return out.astype(np.float32)


# version 11
# speedup vs baseline: 1.0123x; 1.0123x over previous
import os
import numpy as np
import ml_dtypes
from contextlib import ExitStack

import concourse.bass as bass
import concourse.mybir as mybir
from concourse.bass_utils import run_bass_kernel_spmd

F32 = mybir.dt.float32
BF16 = mybir.dt.bfloat16
AF = mybir.ActivationFunctionType
OP = mybir.AluOpType
AX = mybir.AxisListType

T = 2048
TB = 512
NTB = T // TB
D = 1024
NDC = D // 128
NFC = 32
E0, E1, R = 8, 16, 16
NIC = 4
NSH = 8
EPS = 1e-5

ENGINES = ("sync", "scalar", "vector", "tensor", "gpsimd")


class _Op:
    __slots__ = ("idx", "eng", "emit", "reads", "writes", "dma", "stream",
                 "name", "token", "needed")

    def __init__(self, idx, eng, emit, reads, writes, dma, stream, name):
        self.idx, self.eng, self.emit = idx, eng, emit
        self.reads, self.writes = tuple(reads), tuple(writes)
        self.dma, self.stream, self.name = dma, stream, name
        self.token, self.needed = None, False


class Prog:
    def __init__(self):
        self.ops = []
        self._lw = {}
        self._rd = {}
        self.deps = {}

    def add(self, eng, emit, reads=(), writes=(), dma=False, stream=None,
            name=""):
        assert eng in ENGINES, eng
        if dma and stream is None:
            stream = "q_" + eng
        op = _Op(len(self.ops), eng, emit, reads, writes, dma, stream, name)
        deps = set()
        for r in op.reads:
            w = self._lw.get(r)
            if w is not None:
                deps.add(w)
        for w in op.writes:
            lw = self._lw.get(w)
            if lw is not None:
                deps.add(lw)
            for rd in self._rd.get(w, ()):
                deps.add(rd)
        deps.discard(op)
        self.deps[op.idx] = deps
        for r in op.reads:
            self._rd.setdefault(r, []).append(op)
        for w in op.writes:
            self._lw[w] = op
            self._rd[w] = []
        self.ops.append(op)
        return op

    def emit_into(self, nc, block, ctx):
        if os.environ.get("KSCHED", "1") != "0":
            est = _schedule(self)
        for op in self.ops:
            for d in self.deps[op.idx]:
                d.needed = True
        counts, sem_names = {}, []
        for op in self.ops:
            sem = ("s_" + op.stream) if op.dma else ("e_" + op.eng)
            if op.dma or op.needed:
                counts[sem] = counts.get(sem, 0) + (16 if op.dma else 1)
                op.token = (sem, counts[sem])
                if sem not in sem_names:
                    sem_names.append(sem)
        sems = {nm: ctx.enter_context(nc.semaphore(nm)) for nm in sem_names}
        prog = self

        def make_body(eng_name):
            def body(eng):
                waited = {}
                for op in prog.ops:
                    if op.eng != eng_name:
                        continue
                    byname = {}
                    for d in prog.deps[op.idx]:
                        if d.token is None:
                            continue
                        sem, val = d.token
                        if val > byname.get(sem, 0):
                            byname[sem] = val
                    for sem in sorted(byname):
                        val = byname[sem]
                        if waited.get(sem, 0) >= val:
                            continue
                        eng.wait_ge(sems[sem], val)
                        waited[sem] = val
                    inst = op.emit(eng)
                    if op.token is not None:
                        assert inst is not None, f"op {op.name} returned None"
                        inst.then_inc(sems[op.token[0]], 16 if op.dma else 1)
                return None
            return body

        for eng_name in ENGINES:
            if any(o.eng == eng_name for o in self.ops):
                getattr(block, eng_name)(make_body(eng_name))
        return sems



_COST_TABLE = (
    ("d1_", 520), ("d3_", 520), ("z_", 520), ("t1_", 1400), ("t3_", 1400),
    ("si_", 1200), ("sA_", 2500), ("ap_", 750), ("mx_", 2600),
    ("c1m", 1800), ("c3m", 1800), ("c1e", 800), ("c3e", 800),
    ("sw", 7200), ("b2_", 1800), ("xa1m", 1800), ("xa3m", 1800),
    ("xa1e", 600), ("xa3e", 600), ("ze_", 600), ("zr_", 2000),
    ("xa1r", 1800), ("xa3r", 1800),
    ("L0ug", 1800), ("L0ue", 800), ("L0sg", 1200), ("L0bt", 1800),
    ("L0dt", 1800), ("L0y", 300), ("L0z", 750), ("L0o", 3600),
    ("L0scan", 2500), ("L0dec", 750), ("L0vv", 750), ("L0od", 750),
    ("L1xb", 1800), ("L1ve", 750), ("L1sc", 2500), ("L1o", 1800),
    ("L1oe", 750), ("sh1", 1800), ("sh3", 1800), ("shs", 750), ("sha", 750),
    ("eh1", 1800), ("eh3", 1800), ("ehs", 750), ("ehq", 750), ("eha", 600),
    ("ew2", 4200), ("ewe", 500),
    ("ld_wg", 3000), ("ld_wu", 3000), ("ld_wd", 3000), ("ld_w1", 2200),
    ("ld_w3", 2200), ("ld_w2", 3000), ("ld_", 3000),
    ("cast_", 1150),
    ("_ldx", 2600), ("_sp", 2600), ("_xnb", 2600), ("ce", 2600),
    ("f0xa", 2600), ("f0oe", 750), ("f0os", 2600), ("L0xa", 2600),
    ("L0os", 2600), ("L1xa", 2600), ("L1vs", 2600), ("L1vl", 4500),
    ("ews", 2600), ("_sumx", 1100), ("_ssq", 300), ("_sq", 550),
    ("_bcr", 900), ("_bcm", 900), ("_xnm", 750), ("_xns", 750),
    ("_gl_", 900), ("_gt_", 350), ("_gu_", 250), ("_gv_", 1500),
)


def _op_cost(op):
    n = op.name
    for pat, c in _COST_TABLE:
        if pat in n:
            return c
    return 1600 if op.dma else 400


class _Sched:
    pass


def _schedule(prog):
    ops = prog.ops
    n = len(ops)
    deps = prog.deps
    last_stream = {}
    for op in ops:
        if op.dma:
            prev = last_stream.get(op.stream)
            if prev is not None:
                deps[op.idx].add(prev)
            last_stream[op.stream] = op
    cost = [_op_cost(o) for o in ops]
    succ = [[] for _ in range(n)]
    ndep = [0] * n
    for i in range(n):
        dd = deps[i]
        ndep[i] = len(dd)
        for d in dd:
            succ[d.idx].append(i)
    prio = [0.0] * n
    for i in range(n - 1, -1, -1):
        m = 0.0
        for j in succ[i]:
            if prio[j] > m:
                m = prio[j]
        prio[i] = m + cost[i]
    import heapq
    done_t = [0.0] * n
    eng_free = {e: 0.0 for e in ENGINES}
    ready = []
    for i in range(n):
        if ndep[i] == 0:
            heapq.heappush(ready, (-prio[i], i))
    order = []
    while ready:
        _, i = heapq.heappop(ready)
        op = ops[i]
        est = eng_free[op.eng]
        for d in deps[i]:
            if done_t[d.idx] > est:
                est = done_t[d.idx]
        if op.dma:
            eng_free[op.eng] = est + 250
            done_t[i] = est + cost[i]
        else:
            eng_free[op.eng] = est + cost[i]
            done_t[i] = est + cost[i]
        order.append(i)
        for j in succ[i]:
            ndep[j] -= 1
            if ndep[j] == 0:
                heapq.heappush(ready, (-prio[j], j))
    assert len(order) == n, (len(order), n)
    perm = [ops[i] for i in order]
    fin = [o for o in perm if o.name == "final"]
    perm = [o for o in perm if o.name != "final"] + fin
    prog.ops = perm
    return max(eng_free.values())


def K(pool, lo_b, hi_b):
    return tuple(f"{pool}:{i}" for i in range(lo_b // 512, (hi_b + 511) // 512))


def _prep_weights(ins):
    f = np.float32
    c = np.ascontiguousarray
    w = {}
    w["winT"] = c(ins["L0_attn_W_in"].T)
    w["wstT"] = c(ins["L0_attn_W_state"].T.reshape(8, 128, 128)
                  .transpose(1, 0, 2).reshape(128, 1024))
    w["wdtT"] = c(ins["L0_attn_W_dt"].T.reshape(8, 128, 128)
                  .transpose(1, 0, 2).reshape(128, 1024))
    w["wcT"] = c(ins["L0_attn_W_C"])
    w["wcT"] = c(ins["L0_attn_W_C"].T)
    w["woT"] = c(ins["L0_attn_W_out"].T)
    wg = ins["L0_ffn_w_gate"]
    w["wg_s"] = c(wg.reshape(32, 128, 8, 128).transpose(0, 3, 2, 1)
                  .reshape(32, 128, 1024))
    wu = ins["L0_ffn_w_up"]
    w["wu_s"] = c(wu.reshape(32, 128, 8, 128).transpose(0, 3, 2, 1)
                  .reshape(32, 128, 1024))
    wd = ins["L0_ffn_w_down"]
    w["wd_s"] = c(wd.reshape(8, 128, 32, 128).transpose(2, 3, 0, 1)
                  .reshape(32, 128, 1024))
    w["a1T"] = c(ins["L0_ffn_A1"].transpose(2, 0, 1).reshape(1024, 128))
    w["a3T"] = c(ins["L0_ffn_A3"].transpose(2, 0, 1).reshape(1024, 128))
    for src, dstn in (("L0_ffn_B1", "b1s"), ("L0_ffn_B3", "b3s")):
        Br = ins[src].reshape(8, 8, 4, 128, 16)
        z5 = np.zeros((8, 4, 32, 8, 128), f)
        z5[:, :, :16] = Br.transpose(0, 2, 4, 1, 3)
        w[dstn] = c(z5.reshape(8, 128, 1024))
    w["a2s"] = c(ins["L0_ffn_A2"].reshape(8, 16, 32, 128)
                 .transpose(0, 3, 2, 1).reshape(8, 128, 512))
    B2r = ins["L0_ffn_B2"].reshape(2, 4, 8, 128, 16)
    z5 = np.zeros((2, 4, 32, 8, 128), f)
    z5[:, :, :16] = B2r.transpose(0, 1, 4, 2, 3)
    w["b2s"] = c(z5.reshape(2, 128, 1024))
    w["gwT0"] = c(ins["L0_ffn_gate_w"].T.reshape(8, 128, 8)
                  .transpose(1, 0, 2).reshape(128, 64))
    w["bw"] = c(ins["L1_attn_b_w"].T)
    w["cw"] = c(ins["L1_attn_c_w"].T)
    w["ws1"] = c(ins["L1_ffn_ws1"])
    w["ws3"] = c(ins["L1_ffn_ws3"])
    w["ws2"] = c(ins["L1_ffn_ws2"])
    w["w1"] = c(ins["L1_ffn_w1"])
    w["w2"] = c(ins["L1_ffn_w2"])
    w["w3"] = c(ins["L1_ffn_w3"])
    w["gwT1"] = c(ins["L1_ffn_gate_w"].T.reshape(8, 128, 16)
                  .transpose(1, 0, 2).reshape(128, 128))
    w["bdt"] = c(ins["L0_attn_b_dt"].reshape(128, 1))
    w["alog"] = c(ins["L0_attn_A_log"].reshape(128, 1))
    w["dcol"] = c(ins["L0_attn_D"].reshape(8, 128).T)
    w["llcol"] = c(ins["L1_attn_log_lambda"].reshape(8, 128).T)
    w["bbcol"] = c(ins["L1_attn_b_b"].reshape(8, 128).T)
    w["cbcol"] = c(ins["L1_attn_c_b"].reshape(8, 128).T)
    w["ident"] = np.eye(128, dtype=f)
    cv = np.zeros((128, 4), f)
    cv[:, 0] = 1e-5
    cv[:, 1] = -0.5
    cv[:, 2] = 1.0 / 1024.0
    cv[:, 3] = 2.0
    w["cvec"] = cv
    w["onescol"] = np.ones((128, 1), f)
    w["onesrow"] = np.ones((1, 128), f)
    return {k: np.ascontiguousarray(v, dtype=f) for k, v in w.items()}


_W_SHAPES = {
    "winT": [1024, 2048], "wstT": [128, 1024], "wdtT": [128, 1024],
    "wcT": [128, 1024], "woT": [1024, 1024],
    "wg_s": [32, 128, 1024], "wu_s": [32, 128, 1024], "wd_s": [32, 128, 1024],
    "a1T": [1024, 128], "a3T": [1024, 128],
    "b1s": [8, 128, 1024], "b3s": [8, 128, 1024],
    "a2s": [8, 128, 512], "b2s": [2, 128, 1024],
    "gwT0": [128, 64], "gwT1": [128, 128],
    "bw": [1024, 1024], "cw": [1024, 1024],
    "ws1": [1024, 1024], "ws3": [1024, 1024], "ws2": [1024, 1024],
    "w1": [16, 1024, 512], "w2": [16, 512, 1024], "w3": [16, 1024, 512],
    "bdt": [128, 1], "alog": [128, 1], "dcol": [128, 8], "llcol": [128, 8],
    "bbcol": [128, 8], "cbcol": [128, 8],
    "ident": [128, 128], "onescol": [128, 1], "onesrow": [1, 128],
    "cvec": [128, 4],
}


def build(dbg_internal=False):
    KSKIP = os.environ.get("KSKIP", "")
    nc = bass.Bass("TRN2", target_bir_lowering=False)
    dI = {"xT": nc.dram_tensor("xT", [1024, 2048], F32, kind="ExternalInput").ap()}
    for nm, sh in _W_SHAPES.items():
        dI[nm] = nc.dram_tensor(nm, sh, F32, kind="ExternalInput").ap()
    outT = nc.dram_tensor("outT", [1024, 2048], F32, kind="ExternalOutput").ap()
    scratch_kind = "ExternalOutput" if dbg_internal else "Internal"
    xa0 = nc.dram_tensor("xa0", [1024, 2048], F32, kind=scratch_kind).ap()
    vdram = nc.dram_tensor("vdram", [1024, 2048], F32, kind=scratch_kind).ap()
    c0d = nc.dram_tensor("c0d", [8, 2048], F32, kind=scratch_kind).ap()
    c1d = nc.dram_tensor("c1d", [16, 2048], F32, kind=scratch_kind).ap()
    xnbd = nc.dram_tensor("xnbd", [1024, 2048], BF16, kind="Internal").ap()

    with ExitStack() as ctx:
        def sb(nm, sh, dt=F32):
            return ctx.enter_context(nc.sbuf_tensor(nm, sh, dt))

        def pt(nm, sh, dt=F32):
            return ctx.enter_context(nc.psum_tensor(nm, sh, dt))

        PA = sb("PA", [128, 8192])
        PB = sb("PB", [128, 8192])
        PC = sb("PC", [128, 8192])
        SG = sb("SG", [128, 8192])
        XN = sb("XN", [128, 4096], BF16)
        stg = [sb(f"stg{i}", [128, 1024]) for i in range(2)]
        scr1 = sb("scr1", [128, 1024], BF16)
        scr2 = sb("scr2", [128, 1024])
        t3s = [sb(f"t3s{i}", [128, 1024], BF16) for i in range(2)]
        sss = [sb(f"sss{i}", [128, 1024], BF16) for i in range(2)]
        a1Tb = sb("a1Tb", [128, 1024], BF16)
        a3Tb = sb("a3Tb", [128, 1024], BF16)
        a2sb = [sb(f"a2sb{i}", [128, 512], BF16) for i in range(2)]
        b1sb = [sb(f"b1sb{i}", [128, 1024], BF16) for i in range(2)]
        b3sb = [sb(f"b3sb{i}", [128, 1024], BF16) for i in range(2)]
        b2pb = [sb(f"b2pb{i}", [128, 1024], BF16) for i in range(2)]
        wstb = sb("wstb", [128, 1024], BF16)
        wdtb = sb("wdtb", [128, 1024], BF16)
        wcb = sb("wcb", [128, 1024], BF16)
        gw0b = sb("gw0b", [128, 64], BF16)
        gw1b = sb("gw1b", [128, 128], BF16)
        xa1s = [sb(f"xa1s{i}", [128, 512], BF16) for i in range(2)]
        xa3s = [sb(f"xa3s{i}", [128, 512], BF16) for i in range(2)]
        actp = [sb(f"actp{i}", [128, 1024], BF16) for i in range(2)]
        zpk = sb("zpk", [128, 1024], BF16)
        ceb = [sb(f"ceb{i}", [128, 512]) for i in range(2)]
        xat = [sb(f"xat{i}", [128, 512]) for i in range(2)]
        osb = [sb(f"osb{i}", [128, 512]) for i in range(2)]
        gsc = sb("gsc", [128, 320])
        vcon = sb("vcon", [128, 64])
        identb = sb("identb_s", [128, 128])
        onescol = sb("onescol_s", [128, 1])
        onesrow = sb("onesrow_s", [1, 128])
        PP = [pt(f"PP{i}", [128, 1024]) for i in range(4)]

        def PH(i):
            return PP[i // 2][:, (i % 2) * 512:(i % 2) * 512 + 512]

        def pk(i):
            return (f"ps:{i}",)

        PAb = PA[:, :].bitcast(BF16)
        PBb = PB[:, :].bitcast(BF16)
        PCb = PC[:, :].bitcast(BF16)
        SGb = SG[:, :].bitcast(BF16)

        PAk = lambda lo, hi: K("PA", lo, hi)
        PBk = lambda lo, hi: K("PB", lo, hi)
        PCk = lambda lo, hi: K("PC", lo, hi)
        SGk = lambda lo, hi: K("SG", lo, hi)
        XNk = lambda lo, hi: K("XN", lo, hi)

        p = Prog()

        def dma(eng, dst_ap, src_ap, reads, writes, stream, name, accum=False):
            if accum:
                p.add(eng, (lambda d, s: lambda e: e.dma_start(
                    out=d, in_=s, accum_op=OP.add))(dst_ap, src_ap),
                    reads=reads, writes=writes, dma=True, stream=stream,
                    name=name)
            else:
                p.add(eng, (lambda d, s: lambda e: e.dma_start(
                    out=d, in_=s))(dst_ap, src_ap),
                    reads=reads, writes=writes, dma=True, stream=stream,
                    name=name)

        def V(emit, reads, writes, name):
            p.add("vector", emit, reads=reads, writes=writes, name=name)

        def S(emit, reads, writes, name):
            p.add("scalar", emit, reads=reads, writes=writes, name=name)

        def G(emit, reads, writes, name):
            p.add("gpsimd", emit, reads=reads, writes=writes, name=name)

        def PE(emit, reads, writes, name):
            p.add("tensor", emit, reads=reads, writes=writes, name=name)

        stg_ctr = [0]

        def wload(src_ap, dst_ap, dst_keys, name, ncols, scale_col=None,
                  scale_keys=(), ceng="S"):
            si = stg_ctr[0] % 2
            stg_ctr[0] += 1
            stk = K(f"stg{si}", 0, ncols * 4)
            dma("sync", stg[si][:, 0:ncols], src_ap, (), stk, f"wst{si}",
                "ld_" + name)
            if scale_col is None:
                if ceng == "V":
                    V((lambda d, s, n: lambda e: e.tensor_copy(d, s[:, 0:n]))(
                        dst_ap, stg[si], ncols), stk, dst_keys, "cast_" + name)
                else:
                    S((lambda d, s, n: lambda e: e.copy(d, s[:, 0:n]))(
                        dst_ap, stg[si], ncols), stk, dst_keys, "cast_" + name)
            else:
                V((lambda d, s, n, sc: lambda e: e.tensor_scalar_mul(
                    d, s[:, 0:n], sc))(dst_ap, stg[si], ncols, scale_col),
                  stk + scale_keys, dst_keys, "cast_" + name)

        for nm, dst, keyn in (("ident", identb[:, :], "identb"),
                              ("onescol", onescol[:, :], "onescol"),
                              ("onesrow", onesrow[0:1, :], "onesrow")):
            dma("sync", dst, dI[nm], (), (keyn,), "cst_" + nm, "ld_" + nm)
        VC = {"bdt": 0, "alog": 1, "negrate": 2, "dcol": 3, "llcol": 11,
              "lam": 19, "oml": 27, "omlbb": 35, "bbcol": 43, "cbcol": 51,
              "cvec": 59}
        for nm, w_ in (("bdt", 1), ("alog", 1), ("dcol", 8), ("llcol", 8),
                       ("bbcol", 8), ("cbcol", 8), ("cvec", 4)):
            c0 = VC[nm]
            dma("sync", vcon[:, c0:c0 + w_], dI[nm], (), ("vcon",), "cst_" + nm,
                "ld_" + nm)

        def vc(nm, w_=1):
            c0 = VC[nm]
            return vcon[:, c0:c0 + w_]

        S(lambda e: e.activation(vc("negrate"), vc("alog"), AF.Exp),
          ("vcon",), ("vcon",), "nr1")
        S(lambda e: e.activation(vc("negrate"), vc("negrate"), AF.Ln, bias=1.0),
          ("vcon",), ("vcon",), "nr2")
        V(lambda e: e.tensor_scalar(vc("negrate"), vc("negrate"), 1e-4, -1.0,
                                    OP.add, OP.mult),
          ("vcon",), ("vcon",), "nr3")
        S(lambda e: e.activation(vc("lam", 8), vc("llcol", 8), AF.Exp),
          ("vcon",), ("vcon",), "lam1")
        S(lambda e: e.activation(vc("lam", 8), vc("lam", 8), AF.Sigmoid),
          ("vcon",), ("vcon",), "lam2")
        V(lambda e: e.tensor_scalar(vc("oml", 8), vc("lam", 8), -1.0, 1.0,
                                    OP.mult, OP.add),
          ("vcon",), ("vcon",), "lam3")
        V(lambda e: e.tensor_mul(vc("omlbb", 8), vc("oml", 8), vc("bbcol", 8)),
          ("vcon",), ("vcon",), "lam4")
        G(lambda e: e.memset(zpk[:, :], 0.0), (),
          tuple(f"zpk{i}" for i in range(8)) + ("zpk_pad",), "zpk0")
        wload(dI["gwT0"], gw0b[:, :], ("gw0b",), "gw0", 64)
        wload(dI["gwT1"], gw1b[:, :], ("gw1b",), "gw1", 128)

        def layernorm(tb, src_dram, lname):
            xtk = []
            for dc in range(NDC):
                kk = PBk(dc * 2048, (dc + 1) * 2048)
                xtk.append(kk)
                dma("sync", PB[:, dc * 512:(dc + 1) * 512],
                    src_dram[dc * 128:(dc + 1) * 128, tb * TB:(tb + 1) * TB],
                    (), kk, f"qx{dc}", f"{lname}_ldx{dc}_{tb}")

            def mm_sumx(e):
                inst = None
                for dc in range(NDC):
                    inst = e.matmul(PH(0)[0:1, :], onescol[:, 0:1],
                                    PB[:, dc * 512:(dc + 1) * 512],
                                    start=(dc == 0), stop=(dc == NDC - 1))
                return inst
            PE(mm_sumx, tuple(x for k in xtk for x in k) + ("onescol",),
               pk(0), f"{lname}_sumx_{tb}")
            sqk = [PBk(16384 + i * 2048, 16384 + (i + 1) * 2048) for i in range(2)]
            for dc in range(NDC):
                si = dc % 2
                S((lambda dc, si: lambda e: e.activation(
                    PB[:, 4096 + si * 512:4096 + (si + 1) * 512],
                    PB[:, dc * 512:(dc + 1) * 512], AF.Square))(dc, si),
                  xtk[dc], sqk[si], f"{lname}_sq{dc}_{tb}")
                PE((lambda dc, si: lambda e: e.matmul(
                    PH(1)[0:1, :], onescol[:, 0:1],
                    PB[:, 4096 + si * 512:4096 + (si + 1) * 512],
                    start=(dc == 0), stop=(dc == NDC - 1)))(dc, si),
                   sqk[si] + ("onescol",), pk(1), f"{lname}_ssq{dc}_{tb}")
            mrow = PB[0:1, 5120:5632]
            m2row = PB[0:1, 5632:6144]
            vrow = PB[0:1, 6144:6656]
            mk = PBk(20480, 22528)
            m2k = PBk(22528, 24576)
            vk = PBk(24576, 26624)
            S(lambda e: e.activation(mrow, PH(0)[0:1, :], AF.Copy,
                                     scale=vc("cvec", 4)[0:1, 2:3]),
              pk(0) + ("vcon",), mk, f"{lname}_m_{tb}")
            V(lambda e: e.tensor_mul(m2row, mrow, mrow), mk, m2k,
              f"{lname}_m2_{tb}")
            V(lambda e: e.scalar_tensor_tensor(vrow, PH(1)[0:1, :], 1.0 / D,
                                               m2row, OP.mult, OP.subtract),
              pk(1) + m2k, vk, f"{lname}_var_{tb}")
            S(lambda e: e.activation(vrow, vrow, AF.Ln,
                                     bias=vc("cvec", 4)[0:1, 0:1]),
              vk + ("vcon",), vk, f"{lname}_lnv_{tb}")
            S(lambda e: e.activation(vrow, vrow, AF.Exp,
                                     scale=vc("cvec", 4)[0:1, 1:2]),
              vk + ("vcon",), vk, f"{lname}_rs_{tb}")
            V(lambda e: e.tensor_mul(m2row, mrow, vrow), mk + vk, m2k,
              f"{lname}_mrs_{tb}")
            PE(lambda e: e.matmul(PH(2)[:, :], onesrow[0:1, :], vrow,
                                  start=True, stop=True),
               vk + ("onesrow",), pk(2), f"{lname}_bcr_{tb}")
            PE(lambda e: e.matmul(PH(3)[:, :], onesrow[0:1, :], m2row,
                                  start=True, stop=True),
               m2k + ("onesrow",), pk(3), f"{lname}_bcm_{tb}")
            for dc in range(NDC):
                si = dc % 2
                V((lambda dc, si: lambda e: e.tensor_mul(
                    PB[:, 4096 + si * 512:4096 + (si + 1) * 512],
                    PB[:, dc * 512:(dc + 1) * 512], PH(2)[:, :]))(dc, si),
                  xtk[dc] + pk(2), sqk[si], f"{lname}_xnm{dc}_{tb}")
                V((lambda dc, si: lambda e: e.tensor_sub(
                    XN[:, dc * 512:(dc + 1) * 512],
                    PB[:, 4096 + si * 512:4096 + (si + 1) * 512],
                    PH(3)[:, :]))(dc, si),
                  sqk[si] + pk(3), XNk(dc * 1024, (dc + 1) * 1024),
                  f"{lname}_xns{dc}_{tb}")
                dma("gpsimd",
                    xnbd[dc * 128:(dc + 1) * 128, tb * TB:(tb + 1) * TB],
                    XN[:, dc * 512:(dc + 1) * 512],
                    XNk(dc * 1024, (dc + 1) * 1024), (f"xnbd{dc}_{tb}",),
                    f"qsp{dc}", f"{lname}_sp{dc}_{tb}")

        def load_xnb(tb, who):
            for dc in range(NDC):
                dma("sync", XN[:, dc * 512:(dc + 1) * 512],
                    xnbd[dc * 128:(dc + 1) * 128, tb * TB:(tb + 1) * TB],
                    (f"xnbd{dc}_{tb}",), XNk(dc * 1024, (dc + 1) * 1024),
                    f"qx{dc}", f"{who}_xnb{dc}_{tb}")

        def gating(tb, nexp, gw_sb, gw_key, cd_dram, lname):
            W = 4 * nexp
            ga = gsc[:, 0:W]
            gb = gsc[:, 64:64 + W]
            nm = gsc[:, 256:260]
            m1 = gsc[:, 260:264]
            m2 = gsc[:, 264:268]
            wv = gsc[:, 268:272]
            gs = gsc[:, 272:288]
            gk = ("gsc",)

            def v3(ap):
                return ap.rearrange("p (t n) -> p t n", t=4, n=nexp)

            def b3(ap):
                return ap.unsqueeze(2).broadcast_to([128, 4, nexp])

            for tt4 in range(4):
                def mm_log(e, tt4=tt4):
                    inst = None
                    for dc in range(NDC):
                        inst = e.matmul(
                            PH(4)[:, tt4 * nexp:(tt4 + 1) * nexp],
                            XN[:, dc * 512 + tt4 * 128:dc * 512 + (tt4 + 1) * 128],
                            gw_sb[:, dc * nexp:(dc + 1) * nexp],
                            start=(dc == 0), stop=(dc == NDC - 1))
                    return inst
                PE(mm_log, XNk(0, 8192) + (gw_key,), pk(4),
                   f"{lname}_gl_{tb}_{tt4}")
            V(lambda e: e.reduce_max(nm, v3(PH(4)[:, 0:W]), axis=AX.X),
              pk(4), gk, f"{lname}_ga_{tb}")
            V(lambda e: e.tensor_tensor(v3(ga), v3(PH(4)[:, 0:W]), b3(nm),
                                        OP.subtract),
              pk(4) + gk, gk, f"{lname}_gb_{tb}")
            S(lambda e: e.activation(ga, ga, AF.Exp), gk, gk, f"{lname}_gc_{tb}")
            V(lambda e: e.reduce_sum(nm, v3(ga), axis=AX.X), gk, gk,
              f"{lname}_gd_{tb}")
            V(lambda e: e.reciprocal(nm, nm), gk, gk, f"{lname}_ge_{tb}")
            V(lambda e: e.tensor_tensor(v3(ga), v3(ga), b3(nm), OP.mult),
              gk, gk, f"{lname}_gf_{tb}")
            if nexp == 8:
                V(lambda e: e.reduce_max(m1, v3(ga), axis=AX.X), gk, gk,
                  f"{lname}_g1_{tb}")
                V(lambda e: e.tensor_tensor(v3(gb), v3(ga), b3(m1), OP.is_ge),
                  gk, gk, f"{lname}_g2_{tb}")
                V(lambda e: e.scalar_tensor_tensor(gb, gb, -2.0, ga,
                                                   OP.mult, OP.add),
                  gk, gk, f"{lname}_g3_{tb}")
                V(lambda e: e.reduce_max(m2, v3(gb), axis=AX.X), gk, gk,
                  f"{lname}_g4_{tb}")
                V(lambda e: e.tensor_add(wv, m1, m2), gk, gk, f"{lname}_g5_{tb}")
                V(lambda e: e.reciprocal(wv, wv), gk, gk, f"{lname}_g6_{tb}")
                V(lambda e: e.tensor_tensor(v3(gb), v3(ga), b3(m2), OP.is_ge),
                  gk, gk, f"{lname}_g7_{tb}")
                V(lambda e: e.tensor_tensor(v3(ga), v3(ga), b3(wv), OP.mult),
                  gk, gk, f"{lname}_g8_{tb}")
                V(lambda e: e.tensor_mul(ga, ga, gb), gk, gk, f"{lname}_g9_{tb}")
            else:
                gs2 = gsc[:, 288:304]
                gc = gsc[:, 128:192]
                gd = gsc[:, 192:256]
                gtg = lambda ap: ap.rearrange("p (t g) -> p t g", t=4, g=4)
                b4 = lambda ap: ap.unsqueeze(2).broadcast_to([128, 4, 4])
                vg = lambda ap: ap.rearrange("p (a k) -> p a k", a=16, k=4)
                bgk = lambda ap: ap.unsqueeze(2).broadcast_to([128, 16, 4])
                v16 = lambda ap: ap.rearrange("p (t e) -> p t e", t=4, e=16)
                b16 = lambda ap: ap.unsqueeze(2).broadcast_to([128, 4, 16])
                V(lambda e: e.reduce_max(gs, vg(ga), axis=AX.X), gk, gk,
                  f"{lname}_h1_{tb}")
                V(lambda e: e.reduce_max(m1, gtg(gs), axis=AX.X), gk, gk,
                  f"{lname}_h2_{tb}")
                V(lambda e: e.tensor_tensor(gtg(gs2), gtg(gs), b4(m1),
                                            OP.is_ge), gk, gk,
                  f"{lname}_h3_{tb}")
                V(lambda e: e.scalar_tensor_tensor(gs2, gs2, -2.0, gs,
                                                   OP.mult, OP.add), gk, gk,
                  f"{lname}_h4_{tb}")
                V(lambda e: e.reduce_max(m2, gtg(gs2), axis=AX.X), gk, gk,
                  f"{lname}_h5_{tb}")
                V(lambda e: e.tensor_tensor(gtg(gs2), gtg(gs), b4(m2),
                                            OP.is_ge), gk, gk,
                  f"{lname}_h6_{tb}")
                V(lambda e: e.tensor_tensor(vg(gb), vg(ga), bgk(gs2),
                                            OP.mult), gk, ("gtm",) + gk,
                  f"{lname}_h7_{tb}")
                V(lambda e: e.tensor_copy(gc, gb), gk + ("gtm",), ("gtm",),
                  f"{lname}_h8_{tb}")
                for it in range(3):
                    V((lambda it: lambda e: e.reduce_max(
                        m1, v16(gc), axis=AX.X))(it), ("gtm",) + gk, gk,
                      f"{lname}_i{it}a_{tb}")
                    V((lambda it: lambda e: e.tensor_tensor(
                        v16(gd), v16(gc), b16(m1), OP.is_ge))(it),
                      ("gtm",) + gk, ("gtm",), f"{lname}_i{it}b_{tb}")
                    V((lambda it: lambda e: e.scalar_tensor_tensor(
                        gc, gd, -4.0, gc, OP.mult, OP.add))(it),
                      ("gtm",), ("gtm",), f"{lname}_i{it}c_{tb}")
                V(lambda e: e.reduce_max(m1, v16(gc), axis=AX.X),
                  ("gtm",) + gk, gk, f"{lname}_i3_{tb}")
                V(lambda e: e.tensor_tensor(v16(gd), v16(gb), b16(m1),
                                            OP.is_ge), gk + ("gtm",), ("gtm",),
                  f"{lname}_i4_{tb}")
                V(lambda e: e.tensor_mul(ga, ga, gd), gk + ("gtm",), gk,
                  f"{lname}_i5_{tb}")
            PE(lambda e: e.transpose(PH(5)[0:W, 0:128], ga, identb[:, :]),
               gk + ("identb",), pk(5), f"{lname}_gt_{tb}")
            V(lambda e: e.tensor_copy(gsc[0:W, 128:256], PH(5)[0:W, 0:128]),
              pk(5), ("gtm",), f"{lname}_gu_{tb}")
            for tt4 in range(4):
                dma("gpsimd",
                    cd_dram[0:nexp, (tb * 4 + tt4) * 128:(tb * 4 + tt4 + 1) * 128],
                    gsc[tt4 * nexp:(tt4 + 1) * nexp, 128:256],
                    ("gtm",), (f"cd{lname}_{tb * 4 + tt4}",),
                    "qcd", f"{lname}_gv_{tb}_{tt4}")

        for dc in range(NDC):
            for h in range(2):
                wload(dI["winT"][dc * 128:(dc + 1) * 128, h * 1024:(h + 1) * 1024],
                      PCb[:, dc * 2048 + h * 1024:dc * 2048 + (h + 1) * 1024],
                      PCk(dc * 4096 + h * 2048, dc * 4096 + (h + 1) * 2048),
                      f"winT{dc}_{h}", 1024)
        wload(dI["wstT"], wstb[:, :], ("wstb",), "wstT", 1024)
        wload(dI["wdtT"], wdtb[:, :], ("wdtb",), "wdtT", 1024)

        def l0_phase1(tb):
            layernorm(tb, dI["xT"], "L0ln")
            gating(tb, 8, gw0b, "gw0b", c0d, "L0g")
            for oc in range(16):
                bank = 5 + (oc % 2)

                def mm_ug(e, oc=oc, bank=bank):
                    inst = None
                    for dc in range(NDC):
                        inst = e.matmul(
                            PH(bank)[:, :],
                            PCb[:, dc * 2048 + oc * 128:dc * 2048 + (oc + 1) * 128],
                            XN[:, dc * 512:(dc + 1) * 512],
                            start=(dc == 0), stop=(dc == NDC - 1))
                    return inst
                PE(mm_ug, XNk(0, 8192) + PCk(0, 32768), pk(bank),
                   f"L0ug{oc}_{tb}")
                if oc < 8:
                    utk = PAk(22528 + oc * 1024, 22528 + (oc + 1) * 1024)
                    S((lambda oc, bank, utk: lambda e: e.copy(
                        PAb[:, 11264 + oc * 512:11264 + (oc + 1) * 512],
                        PH(bank)[:, :]))(oc, bank, utk),
                      pk(bank), utk, f"L0ue{oc}_{tb}")
                else:
                    og = oc - 8
                    S((lambda og, bank: lambda e: e.activation(
                        SGb[:, og * 2048 + tb * 512:og * 2048 + (tb + 1) * 512],
                        PH(bank)[:, :], AF.Silu))(og, bank),
                      pk(bank), SGk(og * 4096 + tb * 1024,
                                    og * 4096 + (tb + 1) * 1024),
                      f"L0sg{og}_{tb}")
            utk_all = PAk(22528, 30720)

            def mm_bt(e):
                inst = None
                for dc in range(NDC):
                    inst = e.matmul(PH(2)[:, :],
                                    wstb[:, dc * 128:(dc + 1) * 128],
                                    PAb[:, 11264 + dc * 512:11264 + (dc + 1) * 512],
                                    start=(dc == 0), stop=(dc == NDC - 1))
                return inst
            PE(mm_bt, utk_all + ("wstb",), pk(2), f"L0bt_{tb}")
            btk = PAk(16384, 18432)
            V(lambda e: e.tensor_copy(PA[:, 4096:4608], PH(2)[:, :]),
              pk(2), btk, f"L0bte_{tb}")

            def mm_dt(e):
                inst = None
                for dc in range(NDC):
                    inst = e.matmul(PH(3)[:, :],
                                    wdtb[:, dc * 128:(dc + 1) * 128],
                                    PAb[:, 11264 + dc * 512:11264 + (dc + 1) * 512],
                                    start=(dc == 0), stop=(dc == NDC - 1))
                return inst
            PE(mm_dt, utk_all + ("wdtb",), pk(3), f"L0dt_{tb}")
            sak = PAk(18432, 20480)
            S(lambda e: e.activation(PA[:, 4608:5120], PH(3)[:, :], AF.Exp,
                                     bias=vc("bdt")[:, 0:1]),
              pk(3) + ("vcon",), sak, f"L0dta_{tb}")
            S(lambda e: e.activation(PA[:, 4608:5120], PA[:, 4608:5120],
                                     AF.Ln, bias=1.0), sak, sak, f"L0dtb_{tb}")
            V(lambda e: e.tensor_scalar(PA[:, 4608:5120], PA[:, 4608:5120],
                                        10.0, vc("negrate")[:, 0:1],
                                        OP.min, OP.mult),
              sak + ("vcon",), sak, f"L0dtc_{tb}")
            deck = PAk(tb * 2048, (tb + 1) * 2048)
            S(lambda e: e.activation(PA[:, tb * 512:(tb + 1) * 512],
                                     PA[:, 4608:5120], AF.Exp),
              sak, deck, f"L0dec_{tb}")
            sbk = PAk(20480, 22528)
            V(lambda e: e.tensor_scalar(PA[:, 5120:5632],
                                        PA[:, tb * 512:(tb + 1) * 512],
                                        -1.0, 1.0, OP.mult, OP.add),
              deck, sbk, f"L0od_{tb}")
            vvk = PAk(8192 + tb * 2048, 8192 + (tb + 1) * 2048)
            V(lambda e: e.tensor_mul(PA[:, 2048 + tb * 512:2048 + (tb + 1) * 512],
                                     PA[:, 5120:5632], PA[:, 4096:4608]),
              sbk + btk, vvk, f"L0vv_{tb}")

        for tb in (() if "l0" in KSKIP else range(NTB)):
            l0_phase1(tb)

        if "l0" not in KSKIP:
            V(lambda e: e.tensor_tensor_scan(PBb[:, 0:2048], PA[:, 0:2048],
                                             PA[:, 2048:4096], 0.0,
                                             OP.mult, OP.add),
              PAk(0, 16384), PBk(0, 4096), "L0scan")

        wload(dI["wcT"], wcb[:, :], ("wcb",), "wcT", 1024)
        for dc in range(NDC):
            wload(dI["woT"][dc * 128:(dc + 1) * 128, :],
                  PCb[:, dc * 1024:(dc + 1) * 1024],
                  PCk(dc * 2048, (dc + 1) * 2048), f"woT{dc}", 1024)
            wload(dI["woT"][dc * 128:(dc + 1) * 128, :],
                  PCb[:, 8192 + dc * 1024:8192 + (dc + 1) * 1024],
                  PCk(16384 + dc * 2048, 16384 + (dc + 1) * 2048),
                  f"woTD{dc}", 1024,
                  scale_col=vc("dcol", 8)[:, dc:dc + 1], scale_keys=("vcon",))

        def l0_phase2(tb):
            load_xnb(tb, "p2")
            for oc in range(NDC):
                bank = oc % 2
                PE((lambda oc, bank: lambda e: e.matmul(
                    PH(bank)[:, :], wcb[:, oc * 128:(oc + 1) * 128],
                    PBb[:, tb * 512:(tb + 1) * 512],
                    start=True, stop=True))(oc, bank),
                   ("wcb",) + PBk(0, 4096), pk(bank), f"L0y{oc}_{tb}")
                ztk = PAk(22528 + oc * 1024, 22528 + (oc + 1) * 1024)
                V((lambda oc, bank: lambda e: e.scalar_tensor_tensor(
                    PAb[:, 11264 + oc * 512:11264 + (oc + 1) * 512],
                    PH(bank)[:, :], 1.0,
                    SGb[:, oc * 2048 + tb * 512:oc * 2048 + (tb + 1) * 512],
                    OP.mult, OP.mult))(oc, bank),
                  pk(bank) + SGk(oc * 4096 + tb * 1024,
                                 oc * 4096 + (tb + 1) * 1024),
                  ztk, f"L0z{oc}_{tb}")
            ztk_all = PAk(22528, 30720)
            for oc in range(NDC):
                bank = 2 + (oc % 4)

                def mm_out(e, oc=oc, bank=bank):
                    inst = None
                    for dc in range(NDC):
                        inst = e.matmul(
                            PH(bank)[:, :],
                            PCb[:, 8192 + dc * 1024 + oc * 128:
                                8192 + dc * 1024 + (oc + 1) * 128],
                            XN[:, dc * 512:(dc + 1) * 512],
                            start=(dc == 0), stop=False)
                    for dc in range(NDC):
                        inst = e.matmul(
                            PH(bank)[:, :],
                            PCb[:, dc * 1024 + oc * 128:
                                dc * 1024 + (oc + 1) * 128],
                            PAb[:, 11264 + dc * 512:11264 + (dc + 1) * 512],
                            start=False, stop=(dc == NDC - 1))
                    return inst
                PE(mm_out, XNk(0, 8192) + ztk_all + PCk(0, 32768), pk(bank),
                   f"L0o{oc}_{tb}")
                xi = oc % 2
                dma("sync", xat[xi][:, :],
                    dI["xT"][oc * 128:(oc + 1) * 128, tb * TB:(tb + 1) * TB],
                    (), (f"xat{xi}",), f"qxa{xi}", f"L0xa{oc}_{tb}")
                V((lambda oc, bank, xi: lambda e: e.scalar_tensor_tensor(
                    osb[xi][:, :], PH(bank)[:, :], 1.0, xat[xi][:, :],
                    OP.mult, OP.add))(oc, bank, xi),
                  pk(bank) + (f"xat{xi}",), (f"osb{xi}",), f"L0oe{oc}_{tb}")
                dma("gpsimd", xa0[oc * 128:(oc + 1) * 128, tb * TB:(tb + 1) * TB],
                    osb[xi][:, :], (f"osb{xi}",), (f"xa0_{oc}_{tb}",),
                    f"qst{xi}", f"L0os{oc}_{tb}")

        for tb in (() if "l0" in KSKIP else range(NTB)):
            l0_phase2(tb)

        for tb in range(NTB):
            l0_phase2(tb)

        for dc in range(NDC):
            wload(dI["a1T"][dc * 128:(dc + 1) * 128, :],
                  a1Tb[:, dc * 128:(dc + 1) * 128], ("a1Tb",), f"a1T{dc}", 128)
            wload(dI["a3T"][dc * 128:(dc + 1) * 128, :],
                  a3Tb[:, dc * 128:(dc + 1) * 128], ("a3Tb",), f"a3T{dc}", 128)
        for g in range(2):
            wload(dI["b2s"][g], b2pb[g][:, :], (f"b2pb{g}",), f"b2s{g}", 1024)

        def l0_ffn(tb):
            load_xnb(tb, "ffn")
            for fc in (() if "ffnc" in KSKIP else range(NFC)):
                si = fc % 2
                wgk = SGk(16384 + si * 2048, 16384 + (si + 1) * 2048)
                wuk = SGk(20480 + si * 2048, 20480 + (si + 1) * 2048)
                wload(dI["wg_s"][fc],
                      SGb[:, 8192 + si * 1024:8192 + (si + 1) * 1024],
                      wgk, f"wg{fc}_{tb}", 1024)
                wload(dI["wu_s"][fc],
                      SGb[:, 10240 + si * 1024:10240 + (si + 1) * 1024],
                      wuk, f"wu{fc}_{tb}", 1024)
                bank = fc % 2

                def mm_c1(e, fc=fc, si=si, bank=bank):
                    inst = None
                    for dc in range(NDC):
                        inst = e.matmul(
                            PH(bank)[:, :],
                            SGb[:, 8192 + si * 1024 + dc * 128:
                                8192 + si * 1024 + (dc + 1) * 128],
                            XN[:, dc * 512:(dc + 1) * 512],
                            start=(dc == 0), stop=(dc == NDC - 1))
                    return inst
                PE(mm_c1, XNk(0, 8192) + wgk, pk(bank), f"c1m{fc}_{tb}")
                S((lambda fc, bank: lambda e: e.copy(
                    PAb[:, fc * 512:(fc + 1) * 512], PH(bank)[:, :]))(fc, bank),
                  pk(bank), PAk(fc * 1024, (fc + 1) * 1024), f"c1e{fc}_{tb}")

                def mm_c3(e, fc=fc, si=si):
                    inst = None
                    for dc in range(NDC):
                        inst = e.matmul(
                            PH(6)[:, :],
                            SGb[:, 10240 + si * 1024 + dc * 128:
                                10240 + si * 1024 + (dc + 1) * 128],
                            XN[:, dc * 512:(dc + 1) * 512],
                            start=(dc == 0), stop=(dc == NDC - 1))
                    return inst
                PE(mm_c3, XNk(0, 8192) + wuk, pk(6), f"c3m{fc}_{tb}")
                S((lambda fc: lambda e: e.copy(
                    PBb[:, fc * 512:(fc + 1) * 512], PH(6)[:, :]))(fc),
                  pk(6), PBk(fc * 1024, (fc + 1) * 1024), f"c3e{fc}_{tb}")

            for ex in (() if "ffne" in KSKIP else range(E0)):
                si = ex % 2
                wload(dI["b1s"][ex], b1sb[si][:, :], (f"b1sb{si}",),
                      f"b1_{ex}_{tb}", 1024)
                wload(dI["b3s"][ex], b3sb[si][:, :], (f"b3sb{si}",),
                      f"b3_{ex}_{tb}", 1024)
                wload(dI["a2s"][ex], a2sb[si][:, :], (f"a2sb{si}",),
                      f"a2_{ex}_{tb}", 512)
                dma("sync", ceb[si][:, :],
                    c0d[ex:ex + 1, tb * TB:(tb + 1) * TB].broadcast_to([128, TB]),
                    tuple(f"cdL0g_{tt}" for tt in range(tb * 4, tb * 4 + 4)),
                    (f"ceb{si}",), f"qce{si}", f"ce{ex}_{tb}")

                def mm_xa1(e, ex=ex):
                    inst = None
                    for dc in range(NDC):
                        inst = e.matmul(
                            PH(7)[0:16, :],
                            a1Tb[:, dc * 128 + ex * 16:dc * 128 + ex * 16 + 16],
                            XN[:, dc * 512:(dc + 1) * 512],
                            start=(dc == 0), stop=(dc == NDC - 1))
                    return inst
                PE(mm_xa1, XNk(0, 8192) + ("a1Tb",), pk(7), f"xa1m{ex}_{tb}")
                S((lambda si: lambda e: e.activation(
                    xa1s[si][0:16, :], PH(7)[0:16, :],
                    AF.Copy, scale=vc("cvec", 4)[0:16, 3:4]))(si),
                  pk(7) + ("vcon",), (f"xa1s{si}",), f"xa1e{ex}_{tb}")
                for q in (1, 2, 3):
                    dma("sync", xa1s[si][32 * q:32 * q + 16, :],
                        xa1s[si][0:16, :], (f"xa1s{si}",), (f"xa1s{si}r{q}",),
                        f"qxr1{si}_{q}", f"xa1r{ex}_{q}_{tb}")

                def mm_xa3(e, ex=ex):
                    inst = None
                    for dc in range(NDC):
                        inst = e.matmul(
                            PH(7)[0:16, :],
                            a3Tb[:, dc * 128 + ex * 16:dc * 128 + ex * 16 + 16],
                            XN[:, dc * 512:(dc + 1) * 512],
                            start=(dc == 0), stop=(dc == NDC - 1))
                    return inst
                PE(mm_xa3, XNk(0, 8192) + ("a3Tb",), pk(7), f"xa3m{ex}_{tb}")
                S((lambda si: lambda e: e.activation(
                    xa3s[si][0:16, :], PH(7)[0:16, :],
                    AF.Copy, scale=vc("cvec", 4)[0:16, 3:4]))(si),
                  pk(7) + ("vcon",), (f"xa3s{si}",), f"xa3e{ex}_{tb}")
                for q in (1, 2, 3):
                    dma("sync", xa3s[si][32 * q:32 * q + 16, :],
                        xa3s[si][0:16, :], (f"xa3s{si}",), (f"xa3s{si}r{q}",),
                        f"qxr3{si}_{q}", f"xa3r{ex}_{q}_{tb}")

                zq = []

                def flush_z():
                    mmz, rds, nm = zq.pop(0)
                    PE(mmz, rds, pk(6), nm)

                aq = []

                def flush_a():
                    fp, par, adst, adk = aq.pop(0)
                    V((lambda adst, par: lambda e: e.tensor_mul(
                        adst, sss[par][:, :], t3s[par][:, :]))(adst, par),
                      (f"sss{par}", f"t3s{par}"), adk, f"ap_{ex}_{fp}_{tb}")
                    fc0, fc1 = 2 * fp, 2 * fp + 1

                    def mm_z(e, fc0=fc0, fc1=fc1, si=si, adst=adst):
                        inst = None
                        for h, fc in ((0, fc0), (1, fc1)):
                            inst = e.matmul(
                                PH(6)[0:16, :],
                                a2sb[si][:, fc * 16:fc * 16 + 16],
                                adst[:, h * 512:(h + 1) * 512],
                                start=(fc == 0), stop=(fc == NFC - 1))
                        return inst
                    zq.append((mm_z, (f"a2sb{si}",) + adk, f"z_{ex}_{fp}_{tb}"))
                    if len(zq) > 1:
                        flush_z()
                    if ex > 0:
                        dma("gpsimd", PCb[:, fp * 1024:(fp + 1) * 1024],
                            adst,
                            adk + PCk(fp * 2048, (fp + 1) * 2048),
                            PCk(fp * 2048, (fp + 1) * 2048),
                            f"qmx{fp % 2}", f"mx_{ex}_{fp}_{tb}", accum=True)

                for fp in range(16):
                    fc0, fc1 = 2 * fp, 2 * fp + 1
                    par = fp % 2
                    d1p = [PP[1], PP[0]][par]
                    d1k = (pk(2) + pk(3), pk(0) + pk(1))[par]

                    def mm_d1(e, fc0=fc0, fc1=fc1, si=si, d1p=d1p):
                        for h, fc in ((0, fc0), (1, fc1)):
                            q = fc % 4
                            blk = fc // 4
                            inst = e.matmul(
                                d1p[:, h * 512:(h + 1) * 512],
                                b1sb[si][32 * q:32 * q + 16,
                                         blk * 128:(blk + 1) * 128],
                                xa1s[si][32 * q:32 * q + 16, :],
                                start=True, stop=True,
                                tile_position=(32 * q, 0))
                        return inst
                    PE(mm_d1, (f"b1sb{si}", f"xa1s{si}") +
                       tuple(f"xa1s{si}r{q}" for q in (1, 2, 3)),
                       d1k, f"d1_{ex}_{fp}_{tb}")

                    def mm_d3(e, fc0=fc0, fc1=fc1, si=si):
                        for h, fc in ((0, fc0), (1, fc1)):
                            q = fc % 4
                            blk = fc // 4
                            inst = e.matmul(
                                PP[2][:, h * 512:(h + 1) * 512],
                                b3sb[si][32 * q:32 * q + 16,
                                         blk * 128:(blk + 1) * 128],
                                xa3s[si][32 * q:32 * q + 16, :],
                                start=True, stop=True,
                                tile_position=(32 * q, 0))
                        return inst
                    PE(mm_d3, (f"b3sb{si}", f"xa3s{si}") +
                       tuple(f"xa3s{si}r{q}" for q in (1, 2, 3)),
                       pk(4) + pk(5), f"d3_{ex}_{fp}_{tb}")
                    c1k = PAk(fc0 * 1024, (fc1 + 1) * 1024)
                    c3k = PBk(fc0 * 1024, (fc1 + 1) * 1024)
                    V((lambda fc0, par, d1p: lambda e: e.scalar_tensor_tensor(
                        scr1[:, :] if par == 0 else scr2[:, :], d1p[:, :], 1.0,
                        PAb[:, fc0 * 512:(fc0 + 2) * 512], OP.mult, OP.add))(fc0, par, d1p),
                      d1k + c1k, (("scr1",), ("scr2",))[par], f"t1_{ex}_{fp}_{tb}")
                    V((lambda fc0, par: lambda e: e.scalar_tensor_tensor(
                        t3s[par][:, :], PP[2][:, :], 1.0,
                        PBb[:, fc0 * 512:(fc0 + 2) * 512], OP.mult, OP.add))(fc0, par),
                      pk(4) + pk(5) + c3k, (f"t3s{par}",), f"t3_{ex}_{fp}_{tb}")
                    S((lambda par: lambda e: e.activation(
                        sss[par][:, :], scr1[:, :] if par == 0 else scr2[:, :],
                        AF.Silu))(par),
                      (("scr1",), ("scr2",))[par], (f"sss{par}",),
                      f"si_{ex}_{fp}_{tb}")
                    G((lambda par, si: lambda e: e.tensor_tensor(
                        sss[par][:, :].rearrange("p (a b) -> p a b", a=2, b=512),
                        sss[par][:, :].rearrange("p (a b) -> p a b", a=2, b=512),
                        ceb[si][:, :].unsqueeze(1).broadcast_to([128, 2, 512]),
                        OP.mult))(par, si),
                      (f"sss{par}", f"ceb{si}"), (f"sss{par}",),
                      f"sA_{ex}_{fp}_{tb}")
                    if ex == 0:
                        adst = PCb[:, fp * 1024:(fp + 1) * 1024]
                        adk = PCk(fp * 2048, (fp + 1) * 2048)
                    else:
                        adst = actp[fp % 2][:, :]
                        adk = (f"actp{fp % 2}",)
                    aq.append((fp, par, adst, adk))
                    if len(aq) > 1:
                        flush_a()
                while aq:
                    flush_a()
                while zq:
                    flush_z()
                S((lambda ex: lambda e: e.activation(
                    SGb[0:16, 12288 + ex * 512:12288 + (ex + 1) * 512],
                    PH(6)[0:16, :], AF.Copy,
                    scale=vc("cvec", 4)[0:16, 3:4]))(ex),
                  pk(6) + ("vcon",), SGk(24576 + ex * 1024, 24576 + (ex + 1) * 1024),
                  f"ze_{ex}_{tb}")
                qz = ex % 4
                dma("sync", (lambda ex, qz: zpk[
                    32 * qz:32 * qz + 16,
                    (ex // 4) * 512:(ex // 4 + 1) * 512])(ex, qz),
                    SGb[0:16, 12288 + ex * 512:12288 + (ex + 1) * 512],
                    SGk(24576 + ex * 1024, 24576 + (ex + 1) * 1024),
                    (f"zpk{ex}",), f"qzr{ex}", f"zr_{ex}_{tb}")

            allb = tuple(x for i in range(8) for x in pk(i))
            if "ffns" in KSKIP:
                return
            for g in range(8):
                wdk = []
                for j in range(4):
                    fc = 4 * g + j
                    sl = fc % 8
                    kk = SGk(sl * 2048, (sl + 1) * 2048)
                    wdk.append(kk)
                    wload(dI["wd_s"][fc],
                          SGb[:, sl * 1024:(sl + 1) * 1024], kk,
                          f"wd{fc}_{tb}", 1024)

                def mm_sw(e, g=g):
                    inst = None
                    laststop = ("ffnb" in KSKIP)
                    for j in range(4):
                        fc = 4 * g + j
                        sl = fc % 8
                        for oc in range(NDC):
                            inst = e.matmul(
                                PH(oc)[:, :],
                                SGb[:, sl * 1024 + oc * 128:
                                    sl * 1024 + (oc + 1) * 128],
                                PCb[:, fc * 512:(fc + 1) * 512],
                                start=(fc == 0), stop=(laststop and fc == 31))
                    return inst
                PE(mm_sw, tuple(x for k in wdk for x in k) + PCk(0, 32768),
                   allb, f"sw{g}_{tb}")

            def mm_b2(e):
                inst = None
                for oc in range(NDC):
                    for g in range(2):
                        inst = e.matmul(
                            PH(oc)[:, :],
                            b2pb[g][:, oc * 128:(oc + 1) * 128],
                            zpk[:, g * 512:(g + 1) * 512],
                            start=False, stop=(g == 1))
                return inst
            if "ffnb" not in KSKIP:
                PE(mm_b2, ("b2pb0", "b2pb1", "zpk_pad") +
                   tuple(f"zpk{ex}" for ex in range(E0)),
                   allb, f"b2_{tb}")
            for oc in range(NDC):
                xi = oc % 2
                dma("sync", xat[xi][:, :],
                    xa0[oc * 128:(oc + 1) * 128, tb * TB:(tb + 1) * TB],
                    (f"xa0_{oc}_{tb}",), (f"xat{xi}",), f"qxa{xi}",
                    f"f0xa{oc}_{tb}")
                V((lambda oc, xi: lambda e: e.scalar_tensor_tensor(
                    osb[xi][:, :], PH(oc)[:, :], 1.0, xat[xi][:, :],
                    OP.mult, OP.add))(oc, xi),
                  pk(oc) + (f"xat{xi}",), (f"osb{xi}",), f"f0oe{oc}_{tb}")
                dma("gpsimd",
                    xa0[oc * 128:(oc + 1) * 128, tb * TB:(tb + 1) * TB],
                    osb[xi][:, :], (f"osb{xi}",), (f"xa0_{oc}_{tb}",),
                    f"qst{xi}", f"f0os{oc}_{tb}")

        for tb in (() if ("l0" in KSKIP or "l0f" in KSKIP) else range(NTB)):
            l0_ffn(tb)

        for dc in range(NDC):
            wload(dI["bw"][dc * 128:(dc + 1) * 128, :],
                  PCb[:, dc * 1024:(dc + 1) * 1024],
                  PCk(dc * 2048, (dc + 1) * 2048), f"bw{dc}", 1024)

        def l1_phase1(tb):
            layernorm(tb, xa0, "L1ln")
            gating(tb, 16, gw1b, "gw1b", c1d, "L1g")
            for ec in range(NDC):
                bank = 5 + (ec % 2)

                def mm_xb(e, ec=ec, bank=bank):
                    inst = None
                    for dc in range(NDC):
                        inst = e.matmul(
                            PH(bank)[:, :],
                            PCb[:, dc * 1024 + ec * 128:dc * 1024 + (ec + 1) * 128],
                            XN[:, dc * 512:(dc + 1) * 512],
                            start=(dc == 0), stop=(dc == NDC - 1))
                    return inst
                PE(mm_xb, XNk(0, 8192) + PCk(0, 16384), pk(bank),
                   f"L1xb{ec}_{tb}")
                xi = ec % 2
                V((lambda ec, bank, xi: lambda e: e.tensor_scalar(
                    osb[xi][:, :], PH(bank)[:, :],
                    vc("oml", 8)[:, ec:ec + 1],
                    vc("omlbb", 8)[:, ec:ec + 1], OP.mult, OP.add))(ec, bank, xi),
                  pk(bank) + ("vcon",), (f"osb{xi}",), f"L1ve{ec}_{tb}")
                dma("gpsimd",
                    vdram[ec * 128:(ec + 1) * 128, tb * TB:(tb + 1) * TB],
                    osb[xi][:, :], (f"osb{xi}",), (f"vd{ec}_{tb}",),
                    f"qst{xi}", f"L1vs{ec}_{tb}")

        for tb in (() if "l1" in KSKIP else range(NTB)):
            l1_phase1(tb)

        for ec in (() if "l1" in KSKIP else range(NDC)):
            si = ec % 2
            vlo = si * 2048
            vk_ = PAk(vlo * 4, (vlo + 2048) * 4)
            llo = 4096 + si * 2048
            lk_ = PAk(llo * 4, (llo + 2048) * 4)
            dma("sync", PA[:, vlo:vlo + 2048],
                vdram[ec * 128:(ec + 1) * 128, :],
                tuple(f"vd{ec}_{tb}" for tb in range(NTB)), vk_,
                f"qv{si}", f"L1vl{ec}")
            V((lambda ec, vlo: lambda e: e.tensor_tensor_scan(
                SGb[:, ec * 2048:(ec + 1) * 2048],
                vc("lam", 8)[:, ec:ec + 1].broadcast_to([128, 2048]),
                PA[:, vlo:vlo + 2048], 0.0, OP.mult, OP.add))(ec, vlo),
              vk_ + ("vcon",), SGk(ec * 4096, (ec + 1) * 4096), f"L1sc{ec}")

        for ec in range(NDC):
            wload(dI["cw"][ec * 128:(ec + 1) * 128, :],
                  PCb[:, 8192 + ec * 1024:8192 + (ec + 1) * 1024],
                  PCk(16384 + ec * 2048, 16384 + (ec + 1) * 2048),
                  f"cw{ec}", 1024)

        def l1_out(tb):
            for oc in range(NDC):
                bank = oc % 4

                def mm_o(e, oc=oc, bank=bank):
                    inst = None
                    for ec in range(NDC):
                        inst = e.matmul(
                            PH(bank)[:, :],
                            PCb[:, 8192 + ec * 1024 + oc * 128:
                                8192 + ec * 1024 + (oc + 1) * 128],
                            SGb[:, ec * 2048 + tb * 512:ec * 2048 + (tb + 1) * 512],
                            start=(ec == 0), stop=(ec == NDC - 1))
                    return inst
                PE(mm_o, SGk(0, 32768) + PCk(16384, 32768), pk(bank),
                   f"L1o{oc}_{tb}")
                xi = oc % 2
                dma("sync", xat[xi][:, :],
                    xa0[oc * 128:(oc + 1) * 128, tb * TB:(tb + 1) * TB],
                    (f"xa0_{oc}_{tb}",), (f"xat{xi}",), f"qxa{xi}",
                    f"L1xa{oc}_{tb}")
                V((lambda oc, bank, xi: lambda e: e.scalar_tensor_tensor(
                    osb[xi][:, :], PH(bank)[:, :],
                    vc("cbcol", 8)[:, oc:oc + 1], xat[xi][:, :],
                    OP.add, OP.add))(oc, bank, xi),
                  pk(bank) + (f"xat{xi}", "vcon"), (f"osb{xi}",),
                  f"L1oe{oc}_{tb}")
                dma("gpsimd",
                    outT[oc * 128:(oc + 1) * 128, tb * TB:(tb + 1) * TB],
                    osb[xi][:, :], (f"osb{xi}",), (f"out_{oc}_{tb}",),
                    f"qst{xi}", f"L1os{oc}_{tb}")

        for tb in (() if "l1" in KSKIP else range(NTB)):
            l1_out(tb)

        for dc in range(NDC):
            wload(dI["ws1"][dc * 128:(dc + 1) * 128, :],
                  PCb[:, dc * 1024:(dc + 1) * 1024],
                  PCk(dc * 2048, (dc + 1) * 2048), f"ws1_{dc}", 1024)
            wload(dI["ws3"][dc * 128:(dc + 1) * 128, :],
                  PCb[:, 8192 + dc * 1024:8192 + (dc + 1) * 1024],
                  PCk(16384 + dc * 2048, 16384 + (dc + 1) * 2048),
                  f"ws3_{dc}", 1024)

        def l1_absh(tb):
            load_xnb(tb, "ab")
            for ic in range(NSH):
                b1 = ic % 2
                b3 = 2 + (ic % 2)

                def mm_h1(e, ic=ic, b1=b1):
                    inst = None
                    for dc in range(NDC):
                        inst = e.matmul(
                            PH(b1)[:, :],
                            PCb[:, dc * 1024 + ic * 128:dc * 1024 + (ic + 1) * 128],
                            XN[:, dc * 512:(dc + 1) * 512],
                            start=(dc == 0), stop=(dc == NDC - 1))
                    return inst
                PE(mm_h1, XNk(0, 8192) + PCk(0, 16384), pk(b1),
                   f"sh1_{ic}_{tb}")

                def mm_h3(e, ic=ic, b3=b3):
                    inst = None
                    for dc in range(NDC):
                        inst = e.matmul(
                            PH(b3)[:, :],
                            PCb[:, 8192 + dc * 1024 + ic * 128:
                                8192 + dc * 1024 + (ic + 1) * 128],
                            XN[:, dc * 512:(dc + 1) * 512],
                            start=(dc == 0), stop=(dc == NDC - 1))
                    return inst
                PE(mm_h3, XNk(0, 8192) + PCk(16384, 32768), pk(b3),
                   f"sh3_{ic}_{tb}")
                S((lambda b1: lambda e: e.activation(
                    scr1[:, 0:512], PH(b1)[:, :], AF.Silu))(b1),
                  pk(b1), ("scr1",), f"shs_{ic}_{tb}")
                V((lambda ic, b3: lambda e: e.scalar_tensor_tensor(
                    SGb[:, ic * 2048 + tb * 512:ic * 2048 + (tb + 1) * 512],
                    PH(b3)[:, :], 1.0, scr1[:, 0:512], OP.mult, OP.mult))(ic, b3),
                  pk(b3) + ("scr1",),
                  SGk(ic * 4096 + tb * 1024, ic * 4096 + (tb + 1) * 1024),
                  f"sha_{ic}_{tb}")

        if "l1f" not in KSKIP and "l1" not in KSKIP:
            for tb in range(NTB):
                l1_absh(tb)

        for ic in range(NSH):
            wload(dI["ws2"][ic * 128:(ic + 1) * 128, :],
                  PCb[:, ic * 1024:(ic + 1) * 1024],
                  PCk(ic * 2048, (ic + 1) * 2048), f"ws2_{ic}", 1024)

        def emit_sweep(wv, tb):
            for oc in range(NDC):
                bank = 4 + (oc % 4)

                def mm_w2(e, wv=wv, oc=oc, bank=bank, tb=tb):
                    inst = None
                    n_mm = 8 + (NSH if wv == 0 else 0)
                    i = 0
                    for e4 in range(2):
                        for ic in range(NIC):
                            aw = 8192 + (tb % 2) * 4096 + (e4 * 4 + ic) * 512
                            i += 1
                            inst = e.matmul(
                                PH(bank)[:, :],
                                PBb[:, e4 * 4096 + ic * 1024 + oc * 128:
                                    e4 * 4096 + ic * 1024 + (oc + 1) * 128],
                                PCb[:, aw:aw + 512],
                                start=(i == 1), stop=(i == n_mm))
                    if wv == 0:
                        for ic in range(NSH):
                            i += 1
                            inst = e.matmul(
                                PH(bank)[:, :],
                                PCb[:, ic * 1024 + oc * 128:
                                    ic * 1024 + (oc + 1) * 128],
                                SGb[:, ic * 2048 + tb * 512:
                                    ic * 2048 + (tb + 1) * 512],
                                start=False, stop=(i == n_mm))
                    return inst
                base = 16384 + (tb % 2) * 8192
                rd = PBk(0, 16384) + PCk(base, base + 8192)
                if wv == 0:
                    rd = rd + PCk(0, 16384) + SGk(0, 32768)
                PE(mm_w2, rd, pk(bank), f"ew2_{wv}_{oc}_{tb}")
                xi = oc % 2
                V((lambda oc, bank, xi: lambda e: e.tensor_copy(
                    osb[xi][:, :], PH(bank)[:, :]))(oc, bank, xi),
                  pk(bank), (f"osb{xi}",), f"ewe_{wv}_{oc}_{tb}")
                dma("gpsimd",
                    outT[oc * 128:(oc + 1) * 128, tb * TB:(tb + 1) * TB],
                    osb[xi][:, :], (f"osb{xi}",), (f"out_{oc}_{tb}",),
                    f"qac{xi}", f"ews_{wv}_{oc}_{tb}", accum=True)

        for wv in (() if ("l1f" in KSKIP or "l1" in KSKIP) else range(8)):
            for e4 in range(2):
                ex = 2 * wv + e4
                si = e4
                for dc in range(NDC):
                    wload(dI["w1"][ex][dc * 128:(dc + 1) * 128, :],
                          PAb[:, si * 4096 + dc * 512:si * 4096 + (dc + 1) * 512],
                          PAk(si * 8192 + dc * 1024,
                              si * 8192 + (dc + 1) * 1024),
                          f"w1_{ex}_{dc}", 512)
                    wload(dI["w3"][ex][dc * 128:(dc + 1) * 128, :],
                          PAb[:, 8192 + si * 4096 + dc * 512:
                              8192 + si * 4096 + (dc + 1) * 512],
                          PAk(16384 + si * 8192 + dc * 1024,
                              16384 + si * 8192 + (dc + 1) * 1024),
                          f"w3_{ex}_{dc}", 512)
                for ic in range(NIC):
                    wload(dI["w2"][ex][ic * 128:(ic + 1) * 128, :],
                          PBb[:, e4 * 4096 + ic * 1024:
                              e4 * 4096 + (ic + 1) * 1024],
                          PBk(e4 * 8192 + ic * 2048,
                              e4 * 8192 + (ic + 1) * 2048),
                          f"w2_{ex}_{ic}", 1024)
            swq = []
            for tb in range(NTB):
                load_xnb(tb, f"w{wv}")
                for e4 in range(2):
                    ex = 2 * wv + e4
                    si = e4
                    ci = e4 % 2
                    dma("sync", ceb[ci][:, :],
                        c1d[ex:ex + 1, tb * TB:(tb + 1) * TB]
                        .broadcast_to([128, TB]),
                        tuple(f"cdL1g_{tt}" for tt in range(tb * 4, tb * 4 + 4)),
                        (f"ceb{ci}",), f"qce{ci}", f"ce1_{ex}_{tb}")
                    for ic in range(NIC):
                        b1 = ic % 2
                        b3 = 2 + (ic % 2)

                        def mm_e1(e, si=si, ic=ic, b1=b1):
                            inst = None
                            for dc in range(NDC):
                                inst = e.matmul(
                                    PH(b1)[:, :],
                                    PAb[:, si * 4096 + dc * 512 + ic * 128:
                                        si * 4096 + dc * 512 + (ic + 1) * 128],
                                    XN[:, dc * 512:(dc + 1) * 512],
                                    start=(dc == 0), stop=(dc == NDC - 1))
                            return inst
                        PE(mm_e1, XNk(0, 8192) + PAk(si * 8192,
                                                     si * 8192 + 8192),
                           pk(b1), f"eh1_{ex}_{ic}_{tb}")

                        def mm_e3(e, si=si, ic=ic, b3=b3):
                            inst = None
                            for dc in range(NDC):
                                inst = e.matmul(
                                    PH(b3)[:, :],
                                    PAb[:, 8192 + si * 4096 + dc * 512 + ic * 128:
                                        8192 + si * 4096 + dc * 512 + (ic + 1) * 128],
                                    XN[:, dc * 512:(dc + 1) * 512],
                                    start=(dc == 0), stop=(dc == NDC - 1))
                            return inst
                        PE(mm_e3, XNk(0, 8192) + PAk(16384 + si * 8192,
                                                     16384 + si * 8192 + 8192),
                           pk(b3), f"eh3_{ex}_{ic}_{tb}")
                        S((lambda b1: lambda e: e.activation(
                            scr1[:, 0:512], PH(b1)[:, :], AF.Silu))(b1),
                          pk(b1), ("scr1",), f"ehs_{ex}_{ic}_{tb}")
                        V((lambda b3, ci: lambda e: e.scalar_tensor_tensor(
                            scr2[:, 0:512], PH(b3)[:, :], 1.0, ceb[ci][:, :],
                            OP.mult, OP.mult))(b3, ci),
                          pk(b3) + (f"ceb{ci}",), ("scr2",),
                          f"ehq_{ex}_{ic}_{tb}")
                        aw = 8192 + (tb % 2) * 4096 + (e4 * 4 + ic) * 512
                        V((lambda aw: lambda e: e.tensor_mul(
                            PCb[:, aw:aw + 512], scr1[:, 0:512],
                            scr2[:, 0:512]))(aw),
                          ("scr1", "scr2"), PCk(aw * 2, aw * 2 + 1024),
                          f"eha_{ex}_{ic}_{tb}")
                swq.append(tb)
                if len(swq) > 1:
                    emit_sweep(wv, swq.pop(0))
            while swq:
                emit_sweep(wv, swq.pop(0))

        fin = ("out_" if "l1" not in KSKIP else "xa0_")
        p.add("sync", lambda e: None,
              reads=tuple(f"{fin}{oc}_{tb}" for oc in range(NDC)
                          for tb in range(NTB)),
              name="final")

        with nc.Block() as block:
            p.emit_into(nc, block, ctx)
    return nc


def kernel(**inputs):
    w = _prep_weights(inputs)
    x = np.asarray(inputs["L0_attn_x"], dtype=np.float32)
    nc = build()
    in_maps = []
    for b in range(8):
        m = {"xT": np.ascontiguousarray(x[b].T)}
        m.update(w)
        in_maps.append(m)
    res = run_bass_kernel_spmd(nc, in_maps, core_ids=list(range(8)))
    out = np.stack([np.ascontiguousarray(res.results[b]["outT"].T)
                    for b in range(8)])
    return out.astype(np.float32)


# version 12
# speedup vs baseline: 1.0298x; 1.0173x over previous
import os
import numpy as np
import ml_dtypes
from contextlib import ExitStack

import concourse.bass as bass
import concourse.mybir as mybir
from concourse.bass_utils import run_bass_kernel_spmd

F32 = mybir.dt.float32
BF16 = mybir.dt.bfloat16
AF = mybir.ActivationFunctionType
OP = mybir.AluOpType
AX = mybir.AxisListType

T = 2048
TB = 512
NTB = T // TB
D = 1024
NDC = D // 128
NFC = 32
E0, E1, R = 8, 16, 16
NIC = 4
NSH = 8
EPS = 1e-5

ENGINES = ("sync", "scalar", "vector", "tensor", "gpsimd")


class _Op:
    __slots__ = ("idx", "eng", "emit", "reads", "writes", "dma", "stream",
                 "name", "token", "needed")

    def __init__(self, idx, eng, emit, reads, writes, dma, stream, name):
        self.idx, self.eng, self.emit = idx, eng, emit
        self.reads, self.writes = tuple(reads), tuple(writes)
        self.dma, self.stream, self.name = dma, stream, name
        self.token, self.needed = None, False


class Prog:
    def __init__(self):
        self.ops = []
        self._lw = {}
        self._rd = {}
        self.deps = {}

    def add(self, eng, emit, reads=(), writes=(), dma=False, stream=None,
            name=""):
        assert eng in ENGINES, eng
        if dma and stream is None:
            stream = "q_" + eng
        op = _Op(len(self.ops), eng, emit, reads, writes, dma, stream, name)
        deps = set()
        for r in op.reads:
            w = self._lw.get(r)
            if w is not None:
                deps.add(w)
        for w in op.writes:
            lw = self._lw.get(w)
            if lw is not None:
                deps.add(lw)
            for rd in self._rd.get(w, ()):
                deps.add(rd)
        deps.discard(op)
        self.deps[op.idx] = deps
        for r in op.reads:
            self._rd.setdefault(r, []).append(op)
        for w in op.writes:
            self._lw[w] = op
            self._rd[w] = []
        self.ops.append(op)
        return op

    def emit_into(self, nc, block, ctx):
        if os.environ.get("KSCHED", "1") != "0":
            est = _schedule(self)
        for op in self.ops:
            for d in self.deps[op.idx]:
                d.needed = True
        counts, sem_names = {}, []
        for op in self.ops:
            sem = ("s_" + op.stream) if op.dma else ("e_" + op.eng)
            if op.dma or op.needed:
                counts[sem] = counts.get(sem, 0) + (16 if op.dma else 1)
                op.token = (sem, counts[sem])
                if sem not in sem_names:
                    sem_names.append(sem)
        sems = {nm: ctx.enter_context(nc.semaphore(nm)) for nm in sem_names}
        prog = self

        def make_body(eng_name):
            def body(eng):
                waited = {}
                for op in prog.ops:
                    if op.eng != eng_name:
                        continue
                    byname = {}
                    for d in prog.deps[op.idx]:
                        if d.token is None:
                            continue
                        sem, val = d.token
                        if val > byname.get(sem, 0):
                            byname[sem] = val
                    for sem in sorted(byname):
                        val = byname[sem]
                        if waited.get(sem, 0) >= val:
                            continue
                        eng.wait_ge(sems[sem], val)
                        waited[sem] = val
                    inst = op.emit(eng)
                    if op.token is not None:
                        assert inst is not None, f"op {op.name} returned None"
                        inst.then_inc(sems[op.token[0]], 16 if op.dma else 1)
                return None
            return body

        for eng_name in ENGINES:
            if any(o.eng == eng_name for o in self.ops):
                getattr(block, eng_name)(make_body(eng_name))
        return sems



_COST_TABLE = (
    ("d1_", 520), ("d3_", 520), ("z_", 520), ("t1_", 1400), ("t3_", 1400),
    ("si_", 1200), ("sA_", 2500), ("ap_", 750), ("mx_", 2600),
    ("c1m", 1800), ("c3m", 1800), ("c1e", 800), ("c3e", 800),
    ("sw", 7200), ("b2_", 1800), ("xa1m", 1800), ("xa3m", 1800),
    ("xa1e", 600), ("xa3e", 600), ("ze_", 600), ("zr_", 2000),
    ("xa1r", 1800), ("xa3r", 1800),
    ("L0ug", 1800), ("L0ue", 800), ("L0sg", 1200), ("L0bt", 1800),
    ("L0dt", 1800), ("L0y", 300), ("L0z", 750), ("L0o", 3600),
    ("L0scan", 2500), ("L0dec", 750), ("L0vv", 750), ("L0od", 750),
    ("L1xb", 1800), ("L1ve", 750), ("L1sc", 2500), ("L1o", 1800),
    ("L1oe", 750), ("sh1", 1800), ("sh3", 1800), ("shs", 750), ("sha", 750),
    ("eh1", 1800), ("eh3", 1800), ("ehs", 750), ("ehq", 750), ("eha", 600),
    ("ew2", 4200), ("ewe", 500),
    ("ld_wg", 3000), ("ld_wu", 3000), ("ld_wd", 3000), ("ld_w1", 2200),
    ("ld_w3", 2200), ("ld_w2", 3000), ("ld_", 3000),
    ("cast_", 1150),
    ("_ldx", 2600), ("_sp", 2600), ("_xnb", 2600), ("ce", 2600),
    ("f0xa", 2600), ("f0oe", 750), ("f0os", 2600), ("L0xa", 2600),
    ("L0os", 2600), ("L1xa", 2600), ("L1vs", 2600), ("L1vl", 4500),
    ("ews", 2600), ("_sumx", 1100), ("_ssq", 300), ("_sq", 550),
    ("_bcr", 900), ("_bcm", 900), ("_xnm", 750), ("_xns", 750),
    ("_gl_", 900), ("_gt_", 350), ("_gu_", 250), ("_gv_", 1500),
)


def _op_cost(op):
    n = op.name
    for pat, c in _COST_TABLE:
        if pat in n:
            return c
    return 1600 if op.dma else 400


class _Sched:
    pass


def _schedule(prog):
    ops = prog.ops
    n = len(ops)
    deps = prog.deps
    last_stream = {}
    for op in ops:
        if op.dma:
            prev = last_stream.get(op.stream)
            if prev is not None:
                deps[op.idx].add(prev)
            last_stream[op.stream] = op
    cost = [_op_cost(o) for o in ops]
    succ = [[] for _ in range(n)]
    ndep = [0] * n
    for i in range(n):
        dd = deps[i]
        ndep[i] = len(dd)
        for d in dd:
            succ[d.idx].append(i)
    prio = [0.0] * n
    for i in range(n - 1, -1, -1):
        m = 0.0
        for j in succ[i]:
            if prio[j] > m:
                m = prio[j]
        prio[i] = m + cost[i]
    import heapq
    done_t = [0.0] * n
    eng_free = {e: 0.0 for e in ENGINES}
    ready = []
    for i in range(n):
        if ndep[i] == 0:
            heapq.heappush(ready, (-prio[i], i))
    order = []
    while ready:
        _, i = heapq.heappop(ready)
        op = ops[i]
        est = eng_free[op.eng]
        for d in deps[i]:
            if done_t[d.idx] > est:
                est = done_t[d.idx]
        if op.dma:
            eng_free[op.eng] = est + 250
            done_t[i] = est + cost[i]
        else:
            eng_free[op.eng] = est + cost[i]
            done_t[i] = est + cost[i]
        order.append(i)
        for j in succ[i]:
            ndep[j] -= 1
            if ndep[j] == 0:
                heapq.heappush(ready, (-prio[j], j))
    assert len(order) == n, (len(order), n)
    perm = [ops[i] for i in order]
    fin = [o for o in perm if o.name == "final"]
    perm = [o for o in perm if o.name != "final"] + fin
    prog.ops = perm
    return max(eng_free.values())


def K(pool, lo_b, hi_b):
    return tuple(f"{pool}:{i}" for i in range(lo_b // 512, (hi_b + 511) // 512))


def _prep_weights(ins):
    f = np.float32
    c = np.ascontiguousarray
    w = {}
    w["winT"] = c(ins["L0_attn_W_in"].T)
    w["wstT"] = c(ins["L0_attn_W_state"].T.reshape(8, 128, 128)
                  .transpose(1, 0, 2).reshape(128, 1024))
    w["wdtT"] = c(ins["L0_attn_W_dt"].T.reshape(8, 128, 128)
                  .transpose(1, 0, 2).reshape(128, 1024))
    w["wcT"] = c(ins["L0_attn_W_C"])
    w["wcT"] = c(ins["L0_attn_W_C"].T)
    w["woT"] = c(ins["L0_attn_W_out"].T)
    wg = ins["L0_ffn_w_gate"]
    w["wg_s"] = c(wg.reshape(32, 128, 8, 128).transpose(0, 3, 2, 1)
                  .reshape(32, 128, 1024))
    wu = ins["L0_ffn_w_up"]
    w["wu_s"] = c(wu.reshape(32, 128, 8, 128).transpose(0, 3, 2, 1)
                  .reshape(32, 128, 1024))
    wd = ins["L0_ffn_w_down"]
    w["wd_s"] = c(wd.reshape(8, 128, 32, 128).transpose(2, 3, 0, 1)
                  .reshape(32, 128, 1024))
    w["a1T"] = c(ins["L0_ffn_A1"].transpose(2, 0, 1).reshape(1024, 128))
    w["a3T"] = c(ins["L0_ffn_A3"].transpose(2, 0, 1).reshape(1024, 128))
    for src, dstn in (("L0_ffn_B1", "b1s"), ("L0_ffn_B3", "b3s")):
        Br = ins[src].reshape(8, 8, 4, 128, 16)
        z5 = np.zeros((8, 4, 32, 8, 128), f)
        z5[:, :, :16] = Br.transpose(0, 2, 4, 1, 3)
        w[dstn] = c(z5.reshape(8, 128, 1024))
    w["a2s"] = c(ins["L0_ffn_A2"].reshape(8, 16, 32, 128)
                 .transpose(0, 3, 2, 1).reshape(8, 128, 512))
    B2r = ins["L0_ffn_B2"].reshape(2, 4, 8, 128, 16)
    z5 = np.zeros((2, 4, 32, 8, 128), f)
    z5[:, :, :16] = B2r.transpose(0, 1, 4, 2, 3)
    w["b2s"] = c(z5.reshape(2, 128, 1024))
    w["gwT0"] = c(ins["L0_ffn_gate_w"].T.reshape(8, 128, 8)
                  .transpose(1, 0, 2).reshape(128, 64))
    w["bw"] = c(ins["L1_attn_b_w"].T)
    w["cw"] = c(ins["L1_attn_c_w"].T)
    w["ws1"] = c(ins["L1_ffn_ws1"])
    w["ws3"] = c(ins["L1_ffn_ws3"])
    w["ws2"] = c(ins["L1_ffn_ws2"])
    w["w1"] = c(ins["L1_ffn_w1"])
    w["w2"] = c(ins["L1_ffn_w2"])
    w["w3"] = c(ins["L1_ffn_w3"])
    w["gwT1"] = c(ins["L1_ffn_gate_w"].T.reshape(8, 128, 16)
                  .transpose(1, 0, 2).reshape(128, 128))
    w["bdt"] = c(ins["L0_attn_b_dt"].reshape(128, 1))
    w["alog"] = c(ins["L0_attn_A_log"].reshape(128, 1))
    w["dcol"] = c(ins["L0_attn_D"].reshape(8, 128).T)
    w["llcol"] = c(ins["L1_attn_log_lambda"].reshape(8, 128).T)
    w["bbcol"] = c(ins["L1_attn_b_b"].reshape(8, 128).T)
    w["cbcol"] = c(ins["L1_attn_c_b"].reshape(8, 128).T)
    w["ident"] = np.eye(128, dtype=f)
    cv = np.zeros((128, 4), f)
    cv[:, 0] = 1e-5
    cv[:, 1] = -0.5
    cv[:, 2] = 1.0 / 1024.0
    cv[:, 3] = 2.0
    w["cvec"] = cv
    w["onescol"] = np.ones((128, 1), f)
    w["onesrow"] = np.ones((1, 128), f)
    return {k: np.ascontiguousarray(v, dtype=f) for k, v in w.items()}


_W_SHAPES = {
    "winT": [1024, 2048], "wstT": [128, 1024], "wdtT": [128, 1024],
    "wcT": [128, 1024], "woT": [1024, 1024],
    "wg_s": [32, 128, 1024], "wu_s": [32, 128, 1024], "wd_s": [32, 128, 1024],
    "a1T": [1024, 128], "a3T": [1024, 128],
    "b1s": [8, 128, 1024], "b3s": [8, 128, 1024],
    "a2s": [8, 128, 512], "b2s": [2, 128, 1024],
    "gwT0": [128, 64], "gwT1": [128, 128],
    "bw": [1024, 1024], "cw": [1024, 1024],
    "ws1": [1024, 1024], "ws3": [1024, 1024], "ws2": [1024, 1024],
    "w1": [16, 1024, 512], "w2": [16, 512, 1024], "w3": [16, 1024, 512],
    "bdt": [128, 1], "alog": [128, 1], "dcol": [128, 8], "llcol": [128, 8],
    "bbcol": [128, 8], "cbcol": [128, 8],
    "ident": [128, 128], "onescol": [128, 1], "onesrow": [1, 128],
    "cvec": [128, 4],
}


def build(dbg_internal=False):
    KSKIP = os.environ.get("KSKIP", "")
    nc = bass.Bass("TRN2", target_bir_lowering=False)
    dI = {"xT": nc.dram_tensor("xT", [1024, 2048], F32, kind="ExternalInput").ap()}
    for nm, sh in _W_SHAPES.items():
        dI[nm] = nc.dram_tensor(nm, sh, F32, kind="ExternalInput").ap()
    outT = nc.dram_tensor("outT", [1024, 2048], F32, kind="ExternalOutput").ap()
    scratch_kind = "ExternalOutput" if dbg_internal else "Internal"
    xa0 = nc.dram_tensor("xa0", [1024, 2048], F32, kind=scratch_kind).ap()
    vdram = nc.dram_tensor("vdram", [1024, 2048], F32, kind=scratch_kind).ap()
    c0d = nc.dram_tensor("c0d", [8, 2048], F32, kind=scratch_kind).ap()
    c1d = nc.dram_tensor("c1d", [16, 2048], F32, kind=scratch_kind).ap()
    xnbd = nc.dram_tensor("xnbd", [1024, 2048], BF16, kind="Internal").ap()

    with ExitStack() as ctx:
        def sb(nm, sh, dt=F32):
            return ctx.enter_context(nc.sbuf_tensor(nm, sh, dt))

        def pt(nm, sh, dt=F32):
            return ctx.enter_context(nc.psum_tensor(nm, sh, dt))

        PA = sb("PA", [128, 8192])
        PB = sb("PB", [128, 8192])
        PC = sb("PC", [128, 8192])
        SG = sb("SG", [128, 8192])
        XN = sb("XN", [128, 4096], BF16)
        stg = [sb(f"stg{i}", [128, 1024]) for i in range(2)]
        scr1 = sb("scr1", [128, 1024], BF16)
        scr2 = sb("scr2", [128, 1024])
        t3s = [sb(f"t3s{i}", [128, 1024], BF16) for i in range(2)]
        sss = [sb(f"sss{i}", [128, 1024], BF16) for i in range(2)]
        a1Tb = sb("a1Tb", [128, 1024], BF16)
        a3Tb = sb("a3Tb", [128, 1024], BF16)
        a2sb = [sb(f"a2sb{i}", [128, 512], BF16) for i in range(2)]
        b1sb = [sb(f"b1sb{i}", [128, 1024], BF16) for i in range(2)]
        b3sb = [sb(f"b3sb{i}", [128, 1024], BF16) for i in range(2)]
        b2pb = [sb(f"b2pb{i}", [128, 1024], BF16) for i in range(2)]
        wstb = sb("wstb", [128, 1024], BF16)
        wdtb = sb("wdtb", [128, 1024], BF16)
        wcb = sb("wcb", [128, 1024], BF16)
        gw0b = sb("gw0b", [128, 64], BF16)
        gw1b = sb("gw1b", [128, 128], BF16)
        xa1s = [sb(f"xa1s{i}", [128, 512], BF16) for i in range(2)]
        xa3s = [sb(f"xa3s{i}", [128, 512], BF16) for i in range(2)]
        actp = [sb(f"actp{i}", [128, 1024], BF16) for i in range(2)]
        zpk = sb("zpk", [128, 1024], BF16)
        ceb = [sb(f"ceb{i}", [128, 512]) for i in range(2)]
        xat = [sb(f"xat{i}", [128, 512]) for i in range(2)]
        osb = [sb(f"osb{i}", [128, 512]) for i in range(2)]
        gsc = sb("gsc", [128, 320])
        vcon = sb("vcon", [128, 64])
        identb = sb("identb_s", [128, 128])
        onescol = sb("onescol_s", [128, 1])
        onesrow = sb("onesrow_s", [1, 128])
        PP = [pt(f"PP{i}", [128, 1024]) for i in range(4)]

        def PH(i):
            return PP[i // 2][:, (i % 2) * 512:(i % 2) * 512 + 512]

        def pk(i):
            return (f"ps:{i}",)

        PAb = PA[:, :].bitcast(BF16)
        PBb = PB[:, :].bitcast(BF16)
        PCb = PC[:, :].bitcast(BF16)
        SGb = SG[:, :].bitcast(BF16)

        PAk = lambda lo, hi: K("PA", lo, hi)
        PBk = lambda lo, hi: K("PB", lo, hi)
        PCk = lambda lo, hi: K("PC", lo, hi)
        SGk = lambda lo, hi: K("SG", lo, hi)
        XNk = lambda lo, hi: K("XN", lo, hi)

        p = Prog()

        def dma(eng, dst_ap, src_ap, reads, writes, stream, name, accum=False):
            if accum:
                p.add(eng, (lambda d, s: lambda e: e.dma_start(
                    out=d, in_=s, accum_op=OP.add))(dst_ap, src_ap),
                    reads=reads, writes=writes, dma=True, stream=stream,
                    name=name)
            else:
                p.add(eng, (lambda d, s: lambda e: e.dma_start(
                    out=d, in_=s))(dst_ap, src_ap),
                    reads=reads, writes=writes, dma=True, stream=stream,
                    name=name)

        def V(emit, reads, writes, name):
            p.add("vector", emit, reads=reads, writes=writes, name=name)

        def S(emit, reads, writes, name):
            p.add("scalar", emit, reads=reads, writes=writes, name=name)

        def G(emit, reads, writes, name):
            p.add("gpsimd", emit, reads=reads, writes=writes, name=name)

        def PE(emit, reads, writes, name):
            p.add("tensor", emit, reads=reads, writes=writes, name=name)

        stg_ctr = [0]

        def wload1(src_ap, dst_ap, dst_keys, name, ncols, scale_col=None,
                   scale_keys=(), ceng="S"):
            si4 = stg_ctr[0] % 4
            stg_ctr[0] += 1
            buf, half = si4 // 2, si4 % 2
            lo = half * 512
            assert ncols <= 512
            stk = K(f"stg{buf}", lo * 4, (lo + ncols) * 4)
            dma("sync", stg[buf][:, lo:lo + ncols], src_ap, (), stk,
                f"wst{si4}", "ld_" + name)
            if scale_col is None:
                if ceng == "V":
                    V((lambda d, s, l, n: lambda e: e.tensor_copy(
                        d, s[:, l:l + n]))(dst_ap, stg[buf], lo, ncols),
                      stk, dst_keys, "cast_" + name)
                else:
                    S((lambda d, s, l, n: lambda e: e.copy(
                        d, s[:, l:l + n]))(dst_ap, stg[buf], lo, ncols),
                      stk, dst_keys, "cast_" + name)
            else:
                V((lambda d, s, l, n, sc: lambda e: e.tensor_scalar_mul(
                    d, s[:, l:l + n], sc))(dst_ap, stg[buf], lo, ncols,
                                           scale_col),
                  stk + scale_keys, dst_keys, "cast_" + name)

        def wload(src_ap, dst_ap, dst_keys, name, ncols, scale_col=None,
                  scale_keys=(), ceng="S"):
            if ncols <= 512:
                wload1(src_ap, dst_ap, dst_keys, name, ncols, scale_col,
                       scale_keys, ceng)
                return
            for h in range(0, ncols, 512):
                w_ = min(512, ncols - h)
                wload1(src_ap[:, h:h + w_], dst_ap[:, h:h + w_], dst_keys,
                       f"{name}_h{h}", w_, scale_col, scale_keys, ceng)

        for nm, dst, keyn in (("ident", identb[:, :], "identb"),
                              ("onescol", onescol[:, :], "onescol"),
                              ("onesrow", onesrow[0:1, :], "onesrow")):
            dma("sync", dst, dI[nm], (), (keyn,), "cst_" + nm, "ld_" + nm)
        VC = {"bdt": 0, "alog": 1, "negrate": 2, "dcol": 3, "llcol": 11,
              "lam": 19, "oml": 27, "omlbb": 35, "bbcol": 43, "cbcol": 51,
              "cvec": 59}
        for nm, w_ in (("bdt", 1), ("alog", 1), ("dcol", 8), ("llcol", 8),
                       ("bbcol", 8), ("cbcol", 8), ("cvec", 4)):
            c0 = VC[nm]
            dma("sync", vcon[:, c0:c0 + w_], dI[nm], (), ("vcon",), "cst_" + nm,
                "ld_" + nm)

        def vc(nm, w_=1):
            c0 = VC[nm]
            return vcon[:, c0:c0 + w_]

        S(lambda e: e.activation(vc("negrate"), vc("alog"), AF.Exp),
          ("vcon",), ("vcon",), "nr1")
        S(lambda e: e.activation(vc("negrate"), vc("negrate"), AF.Ln, bias=1.0),
          ("vcon",), ("vcon",), "nr2")
        V(lambda e: e.tensor_scalar(vc("negrate"), vc("negrate"), 1e-4, -1.0,
                                    OP.add, OP.mult),
          ("vcon",), ("vcon",), "nr3")
        S(lambda e: e.activation(vc("lam", 8), vc("llcol", 8), AF.Exp),
          ("vcon",), ("vcon",), "lam1")
        S(lambda e: e.activation(vc("lam", 8), vc("lam", 8), AF.Sigmoid),
          ("vcon",), ("vcon",), "lam2")
        V(lambda e: e.tensor_scalar(vc("oml", 8), vc("lam", 8), -1.0, 1.0,
                                    OP.mult, OP.add),
          ("vcon",), ("vcon",), "lam3")
        V(lambda e: e.tensor_mul(vc("omlbb", 8), vc("oml", 8), vc("bbcol", 8)),
          ("vcon",), ("vcon",), "lam4")
        G(lambda e: e.memset(zpk[:, :], 0.0), (),
          tuple(f"zpk{i}" for i in range(8)) + ("zpk_pad",), "zpk0")
        wload(dI["gwT0"], gw0b[:, :], ("gw0b",), "gw0", 64)
        wload(dI["gwT1"], gw1b[:, :], ("gw1b",), "gw1", 128)

        def layernorm(tb, src_dram, lname):
            xtk = []
            for dc in range(NDC):
                kk = PBk(dc * 2048, (dc + 1) * 2048)
                xtk.append(kk)
                dma("sync", PB[:, dc * 512:(dc + 1) * 512],
                    src_dram[dc * 128:(dc + 1) * 128, tb * TB:(tb + 1) * TB],
                    (), kk, f"qx{dc}", f"{lname}_ldx{dc}_{tb}")

            def mm_sumx(e):
                inst = None
                for dc in range(NDC):
                    inst = e.matmul(PH(0)[0:1, :], onescol[:, 0:1],
                                    PB[:, dc * 512:(dc + 1) * 512],
                                    start=(dc == 0), stop=(dc == NDC - 1))
                return inst
            PE(mm_sumx, tuple(x for k in xtk for x in k) + ("onescol",),
               pk(0), f"{lname}_sumx_{tb}")
            sqk = [PBk(16384 + i * 2048, 16384 + (i + 1) * 2048) for i in range(2)]
            for dc in range(NDC):
                si = dc % 2
                S((lambda dc, si: lambda e: e.activation(
                    PB[:, 4096 + si * 512:4096 + (si + 1) * 512],
                    PB[:, dc * 512:(dc + 1) * 512], AF.Square))(dc, si),
                  xtk[dc], sqk[si], f"{lname}_sq{dc}_{tb}")
                PE((lambda dc, si: lambda e: e.matmul(
                    PH(1)[0:1, :], onescol[:, 0:1],
                    PB[:, 4096 + si * 512:4096 + (si + 1) * 512],
                    start=(dc == 0), stop=(dc == NDC - 1)))(dc, si),
                   sqk[si] + ("onescol",), pk(1), f"{lname}_ssq{dc}_{tb}")
            mrow = PB[0:1, 5120:5632]
            m2row = PB[0:1, 5632:6144]
            vrow = PB[0:1, 6144:6656]
            mk = PBk(20480, 22528)
            m2k = PBk(22528, 24576)
            vk = PBk(24576, 26624)
            S(lambda e: e.activation(mrow, PH(0)[0:1, :], AF.Copy,
                                     scale=vc("cvec", 4)[0:1, 2:3]),
              pk(0) + ("vcon",), mk, f"{lname}_m_{tb}")
            V(lambda e: e.tensor_mul(m2row, mrow, mrow), mk, m2k,
              f"{lname}_m2_{tb}")
            V(lambda e: e.scalar_tensor_tensor(vrow, PH(1)[0:1, :], 1.0 / D,
                                               m2row, OP.mult, OP.subtract),
              pk(1) + m2k, vk, f"{lname}_var_{tb}")
            S(lambda e: e.activation(vrow, vrow, AF.Ln,
                                     bias=vc("cvec", 4)[0:1, 0:1]),
              vk + ("vcon",), vk, f"{lname}_lnv_{tb}")
            S(lambda e: e.activation(vrow, vrow, AF.Exp,
                                     scale=vc("cvec", 4)[0:1, 1:2]),
              vk + ("vcon",), vk, f"{lname}_rs_{tb}")
            V(lambda e: e.tensor_mul(m2row, mrow, vrow), mk + vk, m2k,
              f"{lname}_mrs_{tb}")
            PE(lambda e: e.matmul(PH(2)[:, :], onesrow[0:1, :], vrow,
                                  start=True, stop=True),
               vk + ("onesrow",), pk(2), f"{lname}_bcr_{tb}")
            PE(lambda e: e.matmul(PH(3)[:, :], onesrow[0:1, :], m2row,
                                  start=True, stop=True),
               m2k + ("onesrow",), pk(3), f"{lname}_bcm_{tb}")
            for dc in range(NDC):
                si = dc % 2
                V((lambda dc, si: lambda e: e.tensor_mul(
                    PB[:, 4096 + si * 512:4096 + (si + 1) * 512],
                    PB[:, dc * 512:(dc + 1) * 512], PH(2)[:, :]))(dc, si),
                  xtk[dc] + pk(2), sqk[si], f"{lname}_xnm{dc}_{tb}")
                V((lambda dc, si: lambda e: e.tensor_sub(
                    XN[:, dc * 512:(dc + 1) * 512],
                    PB[:, 4096 + si * 512:4096 + (si + 1) * 512],
                    PH(3)[:, :]))(dc, si),
                  sqk[si] + pk(3), XNk(dc * 1024, (dc + 1) * 1024),
                  f"{lname}_xns{dc}_{tb}")
                dma("gpsimd",
                    xnbd[dc * 128:(dc + 1) * 128, tb * TB:(tb + 1) * TB],
                    XN[:, dc * 512:(dc + 1) * 512],
                    XNk(dc * 1024, (dc + 1) * 1024), (f"xnbd{dc}_{tb}",),
                    f"qsp{dc}", f"{lname}_sp{dc}_{tb}")

        def load_xnb(tb, who):
            for dc in range(NDC):
                dma("sync", XN[:, dc * 512:(dc + 1) * 512],
                    xnbd[dc * 128:(dc + 1) * 128, tb * TB:(tb + 1) * TB],
                    (f"xnbd{dc}_{tb}",), XNk(dc * 1024, (dc + 1) * 1024),
                    f"qx{dc}", f"{who}_xnb{dc}_{tb}")

        def gating(tb, nexp, gw_sb, gw_key, cd_dram, lname):
            W = 4 * nexp
            ga = gsc[:, 0:W]
            gb = gsc[:, 64:64 + W]
            nm = gsc[:, 256:260]
            m1 = gsc[:, 260:264]
            m2 = gsc[:, 264:268]
            wv = gsc[:, 268:272]
            gs = gsc[:, 272:288]
            gk = ("gsc",)

            def v3(ap):
                return ap.rearrange("p (t n) -> p t n", t=4, n=nexp)

            def b3(ap):
                return ap.unsqueeze(2).broadcast_to([128, 4, nexp])

            for tt4 in range(4):
                def mm_log(e, tt4=tt4):
                    inst = None
                    for dc in range(NDC):
                        inst = e.matmul(
                            PH(4)[:, tt4 * nexp:(tt4 + 1) * nexp],
                            XN[:, dc * 512 + tt4 * 128:dc * 512 + (tt4 + 1) * 128],
                            gw_sb[:, dc * nexp:(dc + 1) * nexp],
                            start=(dc == 0), stop=(dc == NDC - 1))
                    return inst
                PE(mm_log, XNk(0, 8192) + (gw_key,), pk(4),
                   f"{lname}_gl_{tb}_{tt4}")
            V(lambda e: e.reduce_max(nm, v3(PH(4)[:, 0:W]), axis=AX.X),
              pk(4), gk, f"{lname}_ga_{tb}")
            V(lambda e: e.tensor_tensor(v3(ga), v3(PH(4)[:, 0:W]), b3(nm),
                                        OP.subtract),
              pk(4) + gk, gk, f"{lname}_gb_{tb}")
            S(lambda e: e.activation(ga, ga, AF.Exp), gk, gk, f"{lname}_gc_{tb}")
            V(lambda e: e.reduce_sum(nm, v3(ga), axis=AX.X), gk, gk,
              f"{lname}_gd_{tb}")
            V(lambda e: e.reciprocal(nm, nm), gk, gk, f"{lname}_ge_{tb}")
            V(lambda e: e.tensor_tensor(v3(ga), v3(ga), b3(nm), OP.mult),
              gk, gk, f"{lname}_gf_{tb}")
            if nexp == 8:
                V(lambda e: e.reduce_max(m1, v3(ga), axis=AX.X), gk, gk,
                  f"{lname}_g1_{tb}")
                V(lambda e: e.tensor_tensor(v3(gb), v3(ga), b3(m1), OP.is_ge),
                  gk, gk, f"{lname}_g2_{tb}")
                V(lambda e: e.scalar_tensor_tensor(gb, gb, -2.0, ga,
                                                   OP.mult, OP.add),
                  gk, gk, f"{lname}_g3_{tb}")
                V(lambda e: e.reduce_max(m2, v3(gb), axis=AX.X), gk, gk,
                  f"{lname}_g4_{tb}")
                V(lambda e: e.tensor_add(wv, m1, m2), gk, gk, f"{lname}_g5_{tb}")
                V(lambda e: e.reciprocal(wv, wv), gk, gk, f"{lname}_g6_{tb}")
                V(lambda e: e.tensor_tensor(v3(gb), v3(ga), b3(m2), OP.is_ge),
                  gk, gk, f"{lname}_g7_{tb}")
                V(lambda e: e.tensor_tensor(v3(ga), v3(ga), b3(wv), OP.mult),
                  gk, gk, f"{lname}_g8_{tb}")
                V(lambda e: e.tensor_mul(ga, ga, gb), gk, gk, f"{lname}_g9_{tb}")
            else:
                gs2 = gsc[:, 288:304]
                gc = gsc[:, 128:192]
                gd = gsc[:, 192:256]
                gtg = lambda ap: ap.rearrange("p (t g) -> p t g", t=4, g=4)
                b4 = lambda ap: ap.unsqueeze(2).broadcast_to([128, 4, 4])
                vg = lambda ap: ap.rearrange("p (a k) -> p a k", a=16, k=4)
                bgk = lambda ap: ap.unsqueeze(2).broadcast_to([128, 16, 4])
                v16 = lambda ap: ap.rearrange("p (t e) -> p t e", t=4, e=16)
                b16 = lambda ap: ap.unsqueeze(2).broadcast_to([128, 4, 16])
                V(lambda e: e.reduce_max(gs, vg(ga), axis=AX.X), gk, gk,
                  f"{lname}_h1_{tb}")
                V(lambda e: e.reduce_max(m1, gtg(gs), axis=AX.X), gk, gk,
                  f"{lname}_h2_{tb}")
                V(lambda e: e.tensor_tensor(gtg(gs2), gtg(gs), b4(m1),
                                            OP.is_ge), gk, gk,
                  f"{lname}_h3_{tb}")
                V(lambda e: e.scalar_tensor_tensor(gs2, gs2, -2.0, gs,
                                                   OP.mult, OP.add), gk, gk,
                  f"{lname}_h4_{tb}")
                V(lambda e: e.reduce_max(m2, gtg(gs2), axis=AX.X), gk, gk,
                  f"{lname}_h5_{tb}")
                V(lambda e: e.tensor_tensor(gtg(gs2), gtg(gs), b4(m2),
                                            OP.is_ge), gk, gk,
                  f"{lname}_h6_{tb}")
                V(lambda e: e.tensor_tensor(vg(gb), vg(ga), bgk(gs2),
                                            OP.mult), gk, ("gtm",) + gk,
                  f"{lname}_h7_{tb}")
                V(lambda e: e.tensor_copy(gc, gb), gk + ("gtm",), ("gtm",),
                  f"{lname}_h8_{tb}")
                for it in range(3):
                    V((lambda it: lambda e: e.reduce_max(
                        m1, v16(gc), axis=AX.X))(it), ("gtm",) + gk, gk,
                      f"{lname}_i{it}a_{tb}")
                    V((lambda it: lambda e: e.tensor_tensor(
                        v16(gd), v16(gc), b16(m1), OP.is_ge))(it),
                      ("gtm",) + gk, ("gtm",), f"{lname}_i{it}b_{tb}")
                    V((lambda it: lambda e: e.scalar_tensor_tensor(
                        gc, gd, -4.0, gc, OP.mult, OP.add))(it),
                      ("gtm",), ("gtm",), f"{lname}_i{it}c_{tb}")
                V(lambda e: e.reduce_max(m1, v16(gc), axis=AX.X),
                  ("gtm",) + gk, gk, f"{lname}_i3_{tb}")
                V(lambda e: e.tensor_tensor(v16(gd), v16(gb), b16(m1),
                                            OP.is_ge), gk + ("gtm",), ("gtm",),
                  f"{lname}_i4_{tb}")
                V(lambda e: e.tensor_mul(ga, ga, gd), gk + ("gtm",), gk,
                  f"{lname}_i5_{tb}")
            PE(lambda e: e.transpose(PH(5)[0:W, 0:128], ga, identb[:, :]),
               gk + ("identb",), pk(5), f"{lname}_gt_{tb}")
            V(lambda e: e.tensor_copy(gsc[0:W, 128:256], PH(5)[0:W, 0:128]),
              pk(5), ("gtm",), f"{lname}_gu_{tb}")
            for tt4 in range(4):
                dma("gpsimd",
                    cd_dram[0:nexp, (tb * 4 + tt4) * 128:(tb * 4 + tt4 + 1) * 128],
                    gsc[tt4 * nexp:(tt4 + 1) * nexp, 128:256],
                    ("gtm",), (f"cd{lname}_{tb * 4 + tt4}",),
                    "qcd", f"{lname}_gv_{tb}_{tt4}")

        for dc in range(NDC):
            for h in range(2):
                wload(dI["winT"][dc * 128:(dc + 1) * 128, h * 1024:(h + 1) * 1024],
                      PCb[:, dc * 2048 + h * 1024:dc * 2048 + (h + 1) * 1024],
                      PCk(dc * 4096 + h * 2048, dc * 4096 + (h + 1) * 2048),
                      f"winT{dc}_{h}", 1024)
        wload(dI["wstT"], wstb[:, :], ("wstb",), "wstT", 1024)
        wload(dI["wdtT"], wdtb[:, :], ("wdtb",), "wdtT", 1024)

        def l0_phase1(tb):
            layernorm(tb, dI["xT"], "L0ln")
            gating(tb, 8, gw0b, "gw0b", c0d, "L0g")
            for oc in range(16):
                bank = 5 + (oc % 2)

                def mm_ug(e, oc=oc, bank=bank):
                    inst = None
                    for dc in range(NDC):
                        inst = e.matmul(
                            PH(bank)[:, :],
                            PCb[:, dc * 2048 + oc * 128:dc * 2048 + (oc + 1) * 128],
                            XN[:, dc * 512:(dc + 1) * 512],
                            start=(dc == 0), stop=(dc == NDC - 1))
                    return inst
                PE(mm_ug, XNk(0, 8192) + PCk(0, 32768), pk(bank),
                   f"L0ug{oc}_{tb}")
                if oc < 8:
                    utk = PAk(22528 + oc * 1024, 22528 + (oc + 1) * 1024)
                    S((lambda oc, bank, utk: lambda e: e.copy(
                        PAb[:, 11264 + oc * 512:11264 + (oc + 1) * 512],
                        PH(bank)[:, :]))(oc, bank, utk),
                      pk(bank), utk, f"L0ue{oc}_{tb}")
                else:
                    og = oc - 8
                    S((lambda og, bank: lambda e: e.activation(
                        SGb[:, og * 2048 + tb * 512:og * 2048 + (tb + 1) * 512],
                        PH(bank)[:, :], AF.Silu))(og, bank),
                      pk(bank), SGk(og * 4096 + tb * 1024,
                                    og * 4096 + (tb + 1) * 1024),
                      f"L0sg{og}_{tb}")
            utk_all = PAk(22528, 30720)

            def mm_bt(e):
                inst = None
                for dc in range(NDC):
                    inst = e.matmul(PH(2)[:, :],
                                    wstb[:, dc * 128:(dc + 1) * 128],
                                    PAb[:, 11264 + dc * 512:11264 + (dc + 1) * 512],
                                    start=(dc == 0), stop=(dc == NDC - 1))
                return inst
            PE(mm_bt, utk_all + ("wstb",), pk(2), f"L0bt_{tb}")
            btk = PAk(16384, 18432)
            V(lambda e: e.tensor_copy(PA[:, 4096:4608], PH(2)[:, :]),
              pk(2), btk, f"L0bte_{tb}")

            def mm_dt(e):
                inst = None
                for dc in range(NDC):
                    inst = e.matmul(PH(3)[:, :],
                                    wdtb[:, dc * 128:(dc + 1) * 128],
                                    PAb[:, 11264 + dc * 512:11264 + (dc + 1) * 512],
                                    start=(dc == 0), stop=(dc == NDC - 1))
                return inst
            PE(mm_dt, utk_all + ("wdtb",), pk(3), f"L0dt_{tb}")
            sak = PAk(18432, 20480)
            S(lambda e: e.activation(PA[:, 4608:5120], PH(3)[:, :], AF.Exp,
                                     bias=vc("bdt")[:, 0:1]),
              pk(3) + ("vcon",), sak, f"L0dta_{tb}")
            S(lambda e: e.activation(PA[:, 4608:5120], PA[:, 4608:5120],
                                     AF.Ln, bias=1.0), sak, sak, f"L0dtb_{tb}")
            V(lambda e: e.tensor_scalar(PA[:, 4608:5120], PA[:, 4608:5120],
                                        10.0, vc("negrate")[:, 0:1],
                                        OP.min, OP.mult),
              sak + ("vcon",), sak, f"L0dtc_{tb}")
            deck = PAk(tb * 2048, (tb + 1) * 2048)
            S(lambda e: e.activation(PA[:, tb * 512:(tb + 1) * 512],
                                     PA[:, 4608:5120], AF.Exp),
              sak, deck, f"L0dec_{tb}")
            sbk = PAk(20480, 22528)
            V(lambda e: e.tensor_scalar(PA[:, 5120:5632],
                                        PA[:, tb * 512:(tb + 1) * 512],
                                        -1.0, 1.0, OP.mult, OP.add),
              deck, sbk, f"L0od_{tb}")
            vvk = PAk(8192 + tb * 2048, 8192 + (tb + 1) * 2048)
            V(lambda e: e.tensor_mul(PA[:, 2048 + tb * 512:2048 + (tb + 1) * 512],
                                     PA[:, 5120:5632], PA[:, 4096:4608]),
              sbk + btk, vvk, f"L0vv_{tb}")

        for tb in (() if "l0" in KSKIP else range(NTB)):
            l0_phase1(tb)

        if "l0" not in KSKIP:
            V(lambda e: e.tensor_tensor_scan(PBb[:, 0:2048], PA[:, 0:2048],
                                             PA[:, 2048:4096], 0.0,
                                             OP.mult, OP.add),
              PAk(0, 16384), PBk(0, 4096), "L0scan")

        wload(dI["wcT"], wcb[:, :], ("wcb",), "wcT", 1024)
        for dc in range(NDC):
            wload(dI["woT"][dc * 128:(dc + 1) * 128, :],
                  PCb[:, dc * 1024:(dc + 1) * 1024],
                  PCk(dc * 2048, (dc + 1) * 2048), f"woT{dc}", 1024)
            wload(dI["woT"][dc * 128:(dc + 1) * 128, :],
                  PCb[:, 8192 + dc * 1024:8192 + (dc + 1) * 1024],
                  PCk(16384 + dc * 2048, 16384 + (dc + 1) * 2048),
                  f"woTD{dc}", 1024,
                  scale_col=vc("dcol", 8)[:, dc:dc + 1], scale_keys=("vcon",))

        def l0_phase2(tb):
            load_xnb(tb, "p2")
            for oc in range(NDC):
                bank = oc % 2
                PE((lambda oc, bank: lambda e: e.matmul(
                    PH(bank)[:, :], wcb[:, oc * 128:(oc + 1) * 128],
                    PBb[:, tb * 512:(tb + 1) * 512],
                    start=True, stop=True))(oc, bank),
                   ("wcb",) + PBk(0, 4096), pk(bank), f"L0y{oc}_{tb}")
                ztk = PAk(22528 + oc * 1024, 22528 + (oc + 1) * 1024)
                V((lambda oc, bank: lambda e: e.scalar_tensor_tensor(
                    PAb[:, 11264 + oc * 512:11264 + (oc + 1) * 512],
                    PH(bank)[:, :], 1.0,
                    SGb[:, oc * 2048 + tb * 512:oc * 2048 + (tb + 1) * 512],
                    OP.mult, OP.mult))(oc, bank),
                  pk(bank) + SGk(oc * 4096 + tb * 1024,
                                 oc * 4096 + (tb + 1) * 1024),
                  ztk, f"L0z{oc}_{tb}")
            ztk_all = PAk(22528, 30720)
            for oc in range(NDC):
                bank = 2 + (oc % 4)

                def mm_out(e, oc=oc, bank=bank):
                    inst = None
                    for dc in range(NDC):
                        inst = e.matmul(
                            PH(bank)[:, :],
                            PCb[:, 8192 + dc * 1024 + oc * 128:
                                8192 + dc * 1024 + (oc + 1) * 128],
                            XN[:, dc * 512:(dc + 1) * 512],
                            start=(dc == 0), stop=False)
                    for dc in range(NDC):
                        inst = e.matmul(
                            PH(bank)[:, :],
                            PCb[:, dc * 1024 + oc * 128:
                                dc * 1024 + (oc + 1) * 128],
                            PAb[:, 11264 + dc * 512:11264 + (dc + 1) * 512],
                            start=False, stop=(dc == NDC - 1))
                    return inst
                PE(mm_out, XNk(0, 8192) + ztk_all + PCk(0, 32768), pk(bank),
                   f"L0o{oc}_{tb}")
                xi = oc % 2
                dma("sync", xat[xi][:, :],
                    dI["xT"][oc * 128:(oc + 1) * 128, tb * TB:(tb + 1) * TB],
                    (), (f"xat{xi}",), f"qxa{xi}", f"L0xa{oc}_{tb}")
                V((lambda oc, bank, xi: lambda e: e.scalar_tensor_tensor(
                    osb[xi][:, :], PH(bank)[:, :], 1.0, xat[xi][:, :],
                    OP.mult, OP.add))(oc, bank, xi),
                  pk(bank) + (f"xat{xi}",), (f"osb{xi}",), f"L0oe{oc}_{tb}")
                dma("gpsimd", xa0[oc * 128:(oc + 1) * 128, tb * TB:(tb + 1) * TB],
                    osb[xi][:, :], (f"osb{xi}",), (f"xa0_{oc}_{tb}",),
                    f"qst{xi}", f"L0os{oc}_{tb}")

        for tb in (() if "l0" in KSKIP else range(NTB)):
            l0_phase2(tb)

        for tb in range(NTB):
            l0_phase2(tb)

        for dc in range(NDC):
            wload(dI["a1T"][dc * 128:(dc + 1) * 128, :],
                  a1Tb[:, dc * 128:(dc + 1) * 128], ("a1Tb",), f"a1T{dc}", 128)
            wload(dI["a3T"][dc * 128:(dc + 1) * 128, :],
                  a3Tb[:, dc * 128:(dc + 1) * 128], ("a3Tb",), f"a3T{dc}", 128)
        for g in range(2):
            wload(dI["b2s"][g], b2pb[g][:, :], (f"b2pb{g}",), f"b2s{g}", 1024)

        def l0_ffn(tb):
            load_xnb(tb, "ffn")
            for fc in (() if "ffnc" in KSKIP else range(NFC)):
                si = fc % 2
                wgk = SGk(16384 + si * 2048, 16384 + (si + 1) * 2048)
                wuk = SGk(20480 + si * 2048, 20480 + (si + 1) * 2048)
                wload(dI["wg_s"][fc],
                      SGb[:, 8192 + si * 1024:8192 + (si + 1) * 1024],
                      wgk, f"wg{fc}_{tb}", 1024, ceng="V")
                wload(dI["wu_s"][fc],
                      SGb[:, 10240 + si * 1024:10240 + (si + 1) * 1024],
                      wuk, f"wu{fc}_{tb}", 1024, ceng="V")
                bank = fc % 2

                def mm_c1(e, fc=fc, si=si, bank=bank):
                    inst = None
                    for dc in range(NDC):
                        inst = e.matmul(
                            PH(bank)[:, :],
                            SGb[:, 8192 + si * 1024 + dc * 128:
                                8192 + si * 1024 + (dc + 1) * 128],
                            XN[:, dc * 512:(dc + 1) * 512],
                            start=(dc == 0), stop=(dc == NDC - 1))
                    return inst
                PE(mm_c1, XNk(0, 8192) + wgk, pk(bank), f"c1m{fc}_{tb}")
                S((lambda fc, bank: lambda e: e.copy(
                    PAb[:, fc * 512:(fc + 1) * 512], PH(bank)[:, :]))(fc, bank),
                  pk(bank), PAk(fc * 1024, (fc + 1) * 1024), f"c1e{fc}_{tb}")

                def mm_c3(e, fc=fc, si=si):
                    inst = None
                    for dc in range(NDC):
                        inst = e.matmul(
                            PH(6)[:, :],
                            SGb[:, 10240 + si * 1024 + dc * 128:
                                10240 + si * 1024 + (dc + 1) * 128],
                            XN[:, dc * 512:(dc + 1) * 512],
                            start=(dc == 0), stop=(dc == NDC - 1))
                    return inst
                PE(mm_c3, XNk(0, 8192) + wuk, pk(6), f"c3m{fc}_{tb}")
                S((lambda fc: lambda e: e.copy(
                    PBb[:, fc * 512:(fc + 1) * 512], PH(6)[:, :]))(fc),
                  pk(6), PBk(fc * 1024, (fc + 1) * 1024), f"c3e{fc}_{tb}")

            for ex in (() if "ffne" in KSKIP else range(E0)):
                si = ex % 2
                wload(dI["b1s"][ex], b1sb[si][:, :], (f"b1sb{si}",),
                      f"b1_{ex}_{tb}", 1024)
                wload(dI["b3s"][ex], b3sb[si][:, :], (f"b3sb{si}",),
                      f"b3_{ex}_{tb}", 1024)
                wload(dI["a2s"][ex], a2sb[si][:, :], (f"a2sb{si}",),
                      f"a2_{ex}_{tb}", 512)
                dma("sync", ceb[si][:, :],
                    c0d[ex:ex + 1, tb * TB:(tb + 1) * TB].broadcast_to([128, TB]),
                    tuple(f"cdL0g_{tt}" for tt in range(tb * 4, tb * 4 + 4)),
                    (f"ceb{si}",), f"qce{si}", f"ce{ex}_{tb}")

                def mm_xa1(e, ex=ex):
                    inst = None
                    for dc in range(NDC):
                        inst = e.matmul(
                            PH(7)[0:16, :],
                            a1Tb[:, dc * 128 + ex * 16:dc * 128 + ex * 16 + 16],
                            XN[:, dc * 512:(dc + 1) * 512],
                            start=(dc == 0), stop=(dc == NDC - 1))
                    return inst
                PE(mm_xa1, XNk(0, 8192) + ("a1Tb",), pk(7), f"xa1m{ex}_{tb}")
                S((lambda si: lambda e: e.activation(
                    xa1s[si][0:16, :], PH(7)[0:16, :],
                    AF.Copy, scale=vc("cvec", 4)[0:16, 3:4]))(si),
                  pk(7) + ("vcon",), (f"xa1s{si}",), f"xa1e{ex}_{tb}")
                for q in (1, 2, 3):
                    dma("sync", xa1s[si][32 * q:32 * q + 16, :],
                        xa1s[si][0:16, :], (f"xa1s{si}",), (f"xa1s{si}r{q}",),
                        f"qxr1{si}_{q}", f"xa1r{ex}_{q}_{tb}")

                def mm_xa3(e, ex=ex):
                    inst = None
                    for dc in range(NDC):
                        inst = e.matmul(
                            PH(7)[0:16, :],
                            a3Tb[:, dc * 128 + ex * 16:dc * 128 + ex * 16 + 16],
                            XN[:, dc * 512:(dc + 1) * 512],
                            start=(dc == 0), stop=(dc == NDC - 1))
                    return inst
                PE(mm_xa3, XNk(0, 8192) + ("a3Tb",), pk(7), f"xa3m{ex}_{tb}")
                S((lambda si: lambda e: e.activation(
                    xa3s[si][0:16, :], PH(7)[0:16, :],
                    AF.Copy, scale=vc("cvec", 4)[0:16, 3:4]))(si),
                  pk(7) + ("vcon",), (f"xa3s{si}",), f"xa3e{ex}_{tb}")
                for q in (1, 2, 3):
                    dma("sync", xa3s[si][32 * q:32 * q + 16, :],
                        xa3s[si][0:16, :], (f"xa3s{si}",), (f"xa3s{si}r{q}",),
                        f"qxr3{si}_{q}", f"xa3r{ex}_{q}_{tb}")

                zq = []

                def flush_z():
                    mmz, rds, nm = zq.pop(0)
                    PE(mmz, rds, pk(6), nm)

                aq = []

                def flush_a():
                    fp, par, adst, adk = aq.pop(0)
                    V((lambda adst, par: lambda e: e.tensor_mul(
                        adst, sss[par][:, :], t3s[par][:, :]))(adst, par),
                      (f"sss{par}", f"t3s{par}"), adk, f"ap_{ex}_{fp}_{tb}")
                    fc0, fc1 = 2 * fp, 2 * fp + 1

                    def mm_z(e, fc0=fc0, fc1=fc1, si=si, adst=adst):
                        inst = None
                        for h, fc in ((0, fc0), (1, fc1)):
                            inst = e.matmul(
                                PH(6)[0:16, :],
                                a2sb[si][:, fc * 16:fc * 16 + 16],
                                adst[:, h * 512:(h + 1) * 512],
                                start=(fc == 0), stop=(fc == NFC - 1))
                        return inst
                    zq.append((mm_z, (f"a2sb{si}",) + adk, f"z_{ex}_{fp}_{tb}"))
                    if len(zq) > 1:
                        flush_z()
                    if ex > 0:
                        dma("gpsimd", PCb[:, fp * 1024:(fp + 1) * 1024],
                            adst,
                            adk + PCk(fp * 2048, (fp + 1) * 2048),
                            PCk(fp * 2048, (fp + 1) * 2048),
                            f"qmx{fp % 2}", f"mx_{ex}_{fp}_{tb}", accum=True)

                for fp in range(16):
                    fc0, fc1 = 2 * fp, 2 * fp + 1
                    par = fp % 2
                    d1p = [PP[1], PP[0]][par]
                    d1k = (pk(2) + pk(3), pk(0) + pk(1))[par]

                    def mm_d1(e, fc0=fc0, fc1=fc1, si=si, d1p=d1p):
                        for h, fc in ((0, fc0), (1, fc1)):
                            q = fc % 4
                            blk = fc // 4
                            inst = e.matmul(
                                d1p[:, h * 512:(h + 1) * 512],
                                b1sb[si][32 * q:32 * q + 16,
                                         blk * 128:(blk + 1) * 128],
                                xa1s[si][32 * q:32 * q + 16, :],
                                start=True, stop=True,
                                tile_position=(32 * q, 0))
                        return inst
                    PE(mm_d1, (f"b1sb{si}", f"xa1s{si}") +
                       tuple(f"xa1s{si}r{q}" for q in (1, 2, 3)),
                       d1k, f"d1_{ex}_{fp}_{tb}")

                    def mm_d3(e, fc0=fc0, fc1=fc1, si=si):
                        for h, fc in ((0, fc0), (1, fc1)):
                            q = fc % 4
                            blk = fc // 4
                            inst = e.matmul(
                                PP[2][:, h * 512:(h + 1) * 512],
                                b3sb[si][32 * q:32 * q + 16,
                                         blk * 128:(blk + 1) * 128],
                                xa3s[si][32 * q:32 * q + 16, :],
                                start=True, stop=True,
                                tile_position=(32 * q, 0))
                        return inst
                    PE(mm_d3, (f"b3sb{si}", f"xa3s{si}") +
                       tuple(f"xa3s{si}r{q}" for q in (1, 2, 3)),
                       pk(4) + pk(5), f"d3_{ex}_{fp}_{tb}")
                    c1k = PAk(fc0 * 1024, (fc1 + 1) * 1024)
                    c3k = PBk(fc0 * 1024, (fc1 + 1) * 1024)
                    V((lambda fc0, par, d1p: lambda e: e.scalar_tensor_tensor(
                        scr1[:, :] if par == 0 else scr2[:, :], d1p[:, :], 1.0,
                        PAb[:, fc0 * 512:(fc0 + 2) * 512], OP.mult, OP.add))(fc0, par, d1p),
                      d1k + c1k, (("scr1",), ("scr2",))[par], f"t1_{ex}_{fp}_{tb}")
                    V((lambda fc0, par: lambda e: e.scalar_tensor_tensor(
                        t3s[par][:, :], PP[2][:, :], 1.0,
                        PBb[:, fc0 * 512:(fc0 + 2) * 512], OP.mult, OP.add))(fc0, par),
                      pk(4) + pk(5) + c3k, (f"t3s{par}",), f"t3_{ex}_{fp}_{tb}")
                    S((lambda par: lambda e: e.activation(
                        sss[par][:, :], scr1[:, :] if par == 0 else scr2[:, :],
                        AF.Silu))(par),
                      (("scr1",), ("scr2",))[par], (f"sss{par}",),
                      f"si_{ex}_{fp}_{tb}")
                    G((lambda par, si: lambda e: e.tensor_tensor(
                        sss[par][:, :].rearrange("p (a b) -> p a b", a=2, b=512),
                        sss[par][:, :].rearrange("p (a b) -> p a b", a=2, b=512),
                        ceb[si][:, :].unsqueeze(1).broadcast_to([128, 2, 512]),
                        OP.mult))(par, si),
                      (f"sss{par}", f"ceb{si}"), (f"sss{par}",),
                      f"sA_{ex}_{fp}_{tb}")
                    if ex == 0:
                        adst = PCb[:, fp * 1024:(fp + 1) * 1024]
                        adk = PCk(fp * 2048, (fp + 1) * 2048)
                    else:
                        adst = actp[fp % 2][:, :]
                        adk = (f"actp{fp % 2}",)
                    aq.append((fp, par, adst, adk))
                    if len(aq) > 1:
                        flush_a()
                while aq:
                    flush_a()
                while zq:
                    flush_z()
                S((lambda ex: lambda e: e.activation(
                    SGb[0:16, 12288 + ex * 512:12288 + (ex + 1) * 512],
                    PH(6)[0:16, :], AF.Copy,
                    scale=vc("cvec", 4)[0:16, 3:4]))(ex),
                  pk(6) + ("vcon",), SGk(24576 + ex * 1024, 24576 + (ex + 1) * 1024),
                  f"ze_{ex}_{tb}")
                qz = ex % 4
                dma("sync", (lambda ex, qz: zpk[
                    32 * qz:32 * qz + 16,
                    (ex // 4) * 512:(ex // 4 + 1) * 512])(ex, qz),
                    SGb[0:16, 12288 + ex * 512:12288 + (ex + 1) * 512],
                    SGk(24576 + ex * 1024, 24576 + (ex + 1) * 1024),
                    (f"zpk{ex}",), f"qzr{ex}", f"zr_{ex}_{tb}")

            allb = tuple(x for i in range(8) for x in pk(i))
            if "ffns" in KSKIP:
                return
            for g in range(8):
                wdk = []
                for j in range(4):
                    fc = 4 * g + j
                    sl = fc % 8
                    kk = SGk(sl * 2048, (sl + 1) * 2048)
                    wdk.append(kk)
                    wload(dI["wd_s"][fc],
                          SGb[:, sl * 1024:(sl + 1) * 1024], kk,
                          f"wd{fc}_{tb}", 1024)

                def mm_sw(e, g=g):
                    inst = None
                    laststop = ("ffnb" in KSKIP)
                    for j in range(4):
                        fc = 4 * g + j
                        sl = fc % 8
                        for oc in range(NDC):
                            inst = e.matmul(
                                PH(oc)[:, :],
                                SGb[:, sl * 1024 + oc * 128:
                                    sl * 1024 + (oc + 1) * 128],
                                PCb[:, fc * 512:(fc + 1) * 512],
                                start=(fc == 0), stop=(laststop and fc == 31))
                    return inst
                PE(mm_sw, tuple(x for k in wdk for x in k) + PCk(0, 32768),
                   allb, f"sw{g}_{tb}")

            def mm_b2(e):
                inst = None
                for oc in range(NDC):
                    for g in range(2):
                        inst = e.matmul(
                            PH(oc)[:, :],
                            b2pb[g][:, oc * 128:(oc + 1) * 128],
                            zpk[:, g * 512:(g + 1) * 512],
                            start=False, stop=(g == 1))
                return inst
            if "ffnb" not in KSKIP:
                PE(mm_b2, ("b2pb0", "b2pb1", "zpk_pad") +
                   tuple(f"zpk{ex}" for ex in range(E0)),
                   allb, f"b2_{tb}")
            for oc in range(NDC):
                xi = oc % 2
                dma("sync", xat[xi][:, :],
                    xa0[oc * 128:(oc + 1) * 128, tb * TB:(tb + 1) * TB],
                    (f"xa0_{oc}_{tb}",), (f"xat{xi}",), f"qxa{xi}",
                    f"f0xa{oc}_{tb}")
                V((lambda oc, xi: lambda e: e.scalar_tensor_tensor(
                    osb[xi][:, :], PH(oc)[:, :], 1.0, xat[xi][:, :],
                    OP.mult, OP.add))(oc, xi),
                  pk(oc) + (f"xat{xi}",), (f"osb{xi}",), f"f0oe{oc}_{tb}")
                dma("gpsimd",
                    xa0[oc * 128:(oc + 1) * 128, tb * TB:(tb + 1) * TB],
                    osb[xi][:, :], (f"osb{xi}",), (f"xa0_{oc}_{tb}",),
                    f"qst{xi}", f"f0os{oc}_{tb}")

        for tb in (() if ("l0" in KSKIP or "l0f" in KSKIP) else range(NTB)):
            l0_ffn(tb)

        for dc in range(NDC):
            wload(dI["bw"][dc * 128:(dc + 1) * 128, :],
                  PCb[:, dc * 1024:(dc + 1) * 1024],
                  PCk(dc * 2048, (dc + 1) * 2048), f"bw{dc}", 1024)

        def l1_phase1(tb):
            layernorm(tb, xa0, "L1ln")
            gating(tb, 16, gw1b, "gw1b", c1d, "L1g")
            for ec in range(NDC):
                bank = 5 + (ec % 2)

                def mm_xb(e, ec=ec, bank=bank):
                    inst = None
                    for dc in range(NDC):
                        inst = e.matmul(
                            PH(bank)[:, :],
                            PCb[:, dc * 1024 + ec * 128:dc * 1024 + (ec + 1) * 128],
                            XN[:, dc * 512:(dc + 1) * 512],
                            start=(dc == 0), stop=(dc == NDC - 1))
                    return inst
                PE(mm_xb, XNk(0, 8192) + PCk(0, 16384), pk(bank),
                   f"L1xb{ec}_{tb}")
                xi = ec % 2
                V((lambda ec, bank, xi: lambda e: e.tensor_scalar(
                    osb[xi][:, :], PH(bank)[:, :],
                    vc("oml", 8)[:, ec:ec + 1],
                    vc("omlbb", 8)[:, ec:ec + 1], OP.mult, OP.add))(ec, bank, xi),
                  pk(bank) + ("vcon",), (f"osb{xi}",), f"L1ve{ec}_{tb}")
                dma("gpsimd",
                    vdram[ec * 128:(ec + 1) * 128, tb * TB:(tb + 1) * TB],
                    osb[xi][:, :], (f"osb{xi}",), (f"vd{ec}_{tb}",),
                    f"qst{xi}", f"L1vs{ec}_{tb}")

        for tb in (() if "l1" in KSKIP else range(NTB)):
            l1_phase1(tb)

        for ec in (() if "l1" in KSKIP else range(NDC)):
            si = ec % 2
            vlo = si * 2048
            vk_ = PAk(vlo * 4, (vlo + 2048) * 4)
            llo = 4096 + si * 2048
            lk_ = PAk(llo * 4, (llo + 2048) * 4)
            dma("sync", PA[:, vlo:vlo + 2048],
                vdram[ec * 128:(ec + 1) * 128, :],
                tuple(f"vd{ec}_{tb}" for tb in range(NTB)), vk_,
                f"qv{si}", f"L1vl{ec}")
            V((lambda ec, vlo: lambda e: e.tensor_tensor_scan(
                SGb[:, ec * 2048:(ec + 1) * 2048],
                vc("lam", 8)[:, ec:ec + 1].broadcast_to([128, 2048]),
                PA[:, vlo:vlo + 2048], 0.0, OP.mult, OP.add))(ec, vlo),
              vk_ + ("vcon",), SGk(ec * 4096, (ec + 1) * 4096), f"L1sc{ec}")

        for ec in range(NDC):
            wload(dI["cw"][ec * 128:(ec + 1) * 128, :],
                  PCb[:, 8192 + ec * 1024:8192 + (ec + 1) * 1024],
                  PCk(16384 + ec * 2048, 16384 + (ec + 1) * 2048),
                  f"cw{ec}", 1024)

        def l1_out(tb):
            for oc in range(NDC):
                bank = oc % 4

                def mm_o(e, oc=oc, bank=bank):
                    inst = None
                    for ec in range(NDC):
                        inst = e.matmul(
                            PH(bank)[:, :],
                            PCb[:, 8192 + ec * 1024 + oc * 128:
                                8192 + ec * 1024 + (oc + 1) * 128],
                            SGb[:, ec * 2048 + tb * 512:ec * 2048 + (tb + 1) * 512],
                            start=(ec == 0), stop=(ec == NDC - 1))
                    return inst
                PE(mm_o, SGk(0, 32768) + PCk(16384, 32768), pk(bank),
                   f"L1o{oc}_{tb}")
                xi = oc % 2
                dma("sync", xat[xi][:, :],
                    xa0[oc * 128:(oc + 1) * 128, tb * TB:(tb + 1) * TB],
                    (f"xa0_{oc}_{tb}",), (f"xat{xi}",), f"qxa{xi}",
                    f"L1xa{oc}_{tb}")
                V((lambda oc, bank, xi: lambda e: e.scalar_tensor_tensor(
                    osb[xi][:, :], PH(bank)[:, :],
                    vc("cbcol", 8)[:, oc:oc + 1], xat[xi][:, :],
                    OP.add, OP.add))(oc, bank, xi),
                  pk(bank) + (f"xat{xi}", "vcon"), (f"osb{xi}",),
                  f"L1oe{oc}_{tb}")
                dma("gpsimd",
                    outT[oc * 128:(oc + 1) * 128, tb * TB:(tb + 1) * TB],
                    osb[xi][:, :], (f"osb{xi}",), (f"out_{oc}_{tb}",),
                    f"qst{xi}", f"L1os{oc}_{tb}")

        for tb in (() if "l1" in KSKIP else range(NTB)):
            l1_out(tb)

        for dc in range(NDC):
            wload(dI["ws1"][dc * 128:(dc + 1) * 128, :],
                  PCb[:, dc * 1024:(dc + 1) * 1024],
                  PCk(dc * 2048, (dc + 1) * 2048), f"ws1_{dc}", 1024)
            wload(dI["ws3"][dc * 128:(dc + 1) * 128, :],
                  PCb[:, 8192 + dc * 1024:8192 + (dc + 1) * 1024],
                  PCk(16384 + dc * 2048, 16384 + (dc + 1) * 2048),
                  f"ws3_{dc}", 1024)

        def l1_absh(tb):
            load_xnb(tb, "ab")
            for ic in range(NSH):
                b1 = ic % 2
                b3 = 2 + (ic % 2)

                def mm_h1(e, ic=ic, b1=b1):
                    inst = None
                    for dc in range(NDC):
                        inst = e.matmul(
                            PH(b1)[:, :],
                            PCb[:, dc * 1024 + ic * 128:dc * 1024 + (ic + 1) * 128],
                            XN[:, dc * 512:(dc + 1) * 512],
                            start=(dc == 0), stop=(dc == NDC - 1))
                    return inst
                PE(mm_h1, XNk(0, 8192) + PCk(0, 16384), pk(b1),
                   f"sh1_{ic}_{tb}")

                def mm_h3(e, ic=ic, b3=b3):
                    inst = None
                    for dc in range(NDC):
                        inst = e.matmul(
                            PH(b3)[:, :],
                            PCb[:, 8192 + dc * 1024 + ic * 128:
                                8192 + dc * 1024 + (ic + 1) * 128],
                            XN[:, dc * 512:(dc + 1) * 512],
                            start=(dc == 0), stop=(dc == NDC - 1))
                    return inst
                PE(mm_h3, XNk(0, 8192) + PCk(16384, 32768), pk(b3),
                   f"sh3_{ic}_{tb}")
                S((lambda b1: lambda e: e.activation(
                    scr1[:, 0:512], PH(b1)[:, :], AF.Silu))(b1),
                  pk(b1), ("scr1",), f"shs_{ic}_{tb}")
                V((lambda ic, b3: lambda e: e.scalar_tensor_tensor(
                    SGb[:, ic * 2048 + tb * 512:ic * 2048 + (tb + 1) * 512],
                    PH(b3)[:, :], 1.0, scr1[:, 0:512], OP.mult, OP.mult))(ic, b3),
                  pk(b3) + ("scr1",),
                  SGk(ic * 4096 + tb * 1024, ic * 4096 + (tb + 1) * 1024),
                  f"sha_{ic}_{tb}")

        if "l1f" not in KSKIP and "l1" not in KSKIP:
            for tb in range(NTB):
                l1_absh(tb)

        for ic in range(NSH):
            wload(dI["ws2"][ic * 128:(ic + 1) * 128, :],
                  PCb[:, ic * 1024:(ic + 1) * 1024],
                  PCk(ic * 2048, (ic + 1) * 2048), f"ws2_{ic}", 1024)

        def emit_sweep(wv, tb):
            for oc in range(NDC):
                bank = 4 + (oc % 4)

                def mm_w2(e, wv=wv, oc=oc, bank=bank, tb=tb):
                    inst = None
                    n_mm = 8 + (NSH if wv == 0 else 0)
                    i = 0
                    for e4 in range(2):
                        for ic in range(NIC):
                            aw = 8192 + (tb % 2) * 4096 + (e4 * 4 + ic) * 512
                            i += 1
                            inst = e.matmul(
                                PH(bank)[:, :],
                                PBb[:, e4 * 4096 + ic * 1024 + oc * 128:
                                    e4 * 4096 + ic * 1024 + (oc + 1) * 128],
                                PCb[:, aw:aw + 512],
                                start=(i == 1), stop=(i == n_mm))
                    if wv == 0:
                        for ic in range(NSH):
                            i += 1
                            inst = e.matmul(
                                PH(bank)[:, :],
                                PCb[:, ic * 1024 + oc * 128:
                                    ic * 1024 + (oc + 1) * 128],
                                SGb[:, ic * 2048 + tb * 512:
                                    ic * 2048 + (tb + 1) * 512],
                                start=False, stop=(i == n_mm))
                    return inst
                base = 16384 + (tb % 2) * 8192
                rd = PBk(0, 16384) + PCk(base, base + 8192)
                if wv == 0:
                    rd = rd + PCk(0, 16384) + SGk(0, 32768)
                PE(mm_w2, rd, pk(bank), f"ew2_{wv}_{oc}_{tb}")
                xi = oc % 2
                V((lambda oc, bank, xi: lambda e: e.tensor_copy(
                    osb[xi][:, :], PH(bank)[:, :]))(oc, bank, xi),
                  pk(bank), (f"osb{xi}",), f"ewe_{wv}_{oc}_{tb}")
                dma("gpsimd",
                    outT[oc * 128:(oc + 1) * 128, tb * TB:(tb + 1) * TB],
                    osb[xi][:, :], (f"osb{xi}",), (f"out_{oc}_{tb}",),
                    f"qac{xi}", f"ews_{wv}_{oc}_{tb}", accum=True)

        for wv in (() if ("l1f" in KSKIP or "l1" in KSKIP) else range(8)):
            for e4 in range(2):
                ex = 2 * wv + e4
                si = e4
                for dc in range(NDC):
                    wload(dI["w1"][ex][dc * 128:(dc + 1) * 128, :],
                          PAb[:, si * 4096 + dc * 512:si * 4096 + (dc + 1) * 512],
                          PAk(si * 8192 + dc * 1024,
                              si * 8192 + (dc + 1) * 1024),
                          f"w1_{ex}_{dc}", 512)
                    wload(dI["w3"][ex][dc * 128:(dc + 1) * 128, :],
                          PAb[:, 8192 + si * 4096 + dc * 512:
                              8192 + si * 4096 + (dc + 1) * 512],
                          PAk(16384 + si * 8192 + dc * 1024,
                              16384 + si * 8192 + (dc + 1) * 1024),
                          f"w3_{ex}_{dc}", 512)
                for ic in range(NIC):
                    wload(dI["w2"][ex][ic * 128:(ic + 1) * 128, :],
                          PBb[:, e4 * 4096 + ic * 1024:
                              e4 * 4096 + (ic + 1) * 1024],
                          PBk(e4 * 8192 + ic * 2048,
                              e4 * 8192 + (ic + 1) * 2048),
                          f"w2_{ex}_{ic}", 1024)
            swq = []
            for tb in range(NTB):
                load_xnb(tb, f"w{wv}")
                for e4 in range(2):
                    ex = 2 * wv + e4
                    si = e4
                    ci = e4 % 2
                    dma("sync", ceb[ci][:, :],
                        c1d[ex:ex + 1, tb * TB:(tb + 1) * TB]
                        .broadcast_to([128, TB]),
                        tuple(f"cdL1g_{tt}" for tt in range(tb * 4, tb * 4 + 4)),
                        (f"ceb{ci}",), f"qce{ci}", f"ce1_{ex}_{tb}")
                    for ic in range(NIC):
                        b1 = ic % 2
                        b3 = 2 + (ic % 2)

                        def mm_e1(e, si=si, ic=ic, b1=b1):
                            inst = None
                            for dc in range(NDC):
                                inst = e.matmul(
                                    PH(b1)[:, :],
                                    PAb[:, si * 4096 + dc * 512 + ic * 128:
                                        si * 4096 + dc * 512 + (ic + 1) * 128],
                                    XN[:, dc * 512:(dc + 1) * 512],
                                    start=(dc == 0), stop=(dc == NDC - 1))
                            return inst
                        PE(mm_e1, XNk(0, 8192) + PAk(si * 8192,
                                                     si * 8192 + 8192),
                           pk(b1), f"eh1_{ex}_{ic}_{tb}")

                        def mm_e3(e, si=si, ic=ic, b3=b3):
                            inst = None
                            for dc in range(NDC):
                                inst = e.matmul(
                                    PH(b3)[:, :],
                                    PAb[:, 8192 + si * 4096 + dc * 512 + ic * 128:
                                        8192 + si * 4096 + dc * 512 + (ic + 1) * 128],
                                    XN[:, dc * 512:(dc + 1) * 512],
                                    start=(dc == 0), stop=(dc == NDC - 1))
                            return inst
                        PE(mm_e3, XNk(0, 8192) + PAk(16384 + si * 8192,
                                                     16384 + si * 8192 + 8192),
                           pk(b3), f"eh3_{ex}_{ic}_{tb}")
                        S((lambda b1: lambda e: e.activation(
                            scr1[:, 0:512], PH(b1)[:, :], AF.Silu))(b1),
                          pk(b1), ("scr1",), f"ehs_{ex}_{ic}_{tb}")
                        V((lambda b3, ci: lambda e: e.scalar_tensor_tensor(
                            scr2[:, 0:512], PH(b3)[:, :], 1.0, ceb[ci][:, :],
                            OP.mult, OP.mult))(b3, ci),
                          pk(b3) + (f"ceb{ci}",), ("scr2",),
                          f"ehq_{ex}_{ic}_{tb}")
                        aw = 8192 + (tb % 2) * 4096 + (e4 * 4 + ic) * 512
                        V((lambda aw: lambda e: e.tensor_mul(
                            PCb[:, aw:aw + 512], scr1[:, 0:512],
                            scr2[:, 0:512]))(aw),
                          ("scr1", "scr2"), PCk(aw * 2, aw * 2 + 1024),
                          f"eha_{ex}_{ic}_{tb}")
                swq.append(tb)
                if len(swq) > 1:
                    emit_sweep(wv, swq.pop(0))
            while swq:
                emit_sweep(wv, swq.pop(0))

        fin = ("out_" if "l1" not in KSKIP else "xa0_")
        p.add("sync", lambda e: None,
              reads=tuple(f"{fin}{oc}_{tb}" for oc in range(NDC)
                          for tb in range(NTB)),
              name="final")

        with nc.Block() as block:
            p.emit_into(nc, block, ctx)
    return nc


def kernel(**inputs):
    w = _prep_weights(inputs)
    x = np.asarray(inputs["L0_attn_x"], dtype=np.float32)
    nc = build()
    in_maps = []
    for b in range(8):
        m = {"xT": np.ascontiguousarray(x[b].T)}
        m.update(w)
        in_maps.append(m)
    res = run_bass_kernel_spmd(nc, in_maps, core_ids=list(range(8)))
    out = np.stack([np.ascontiguousarray(res.results[b]["outT"].T)
                    for b in range(8)])
    return out.astype(np.float32)
